# version 16
# baseline (speedup 1.0000x reference)
import contextlib

import numpy as np
import ml_dtypes

import concourse.bass as bass
import concourse.mybir as mybir
from concourse.bass_utils import run_bass_kernel_spmd

F32 = mybir.dt.float32
BF16 = mybir.dt.bfloat16
I32 = mybir.dt.int32
AF = mybir.ActivationFunctionType
ALU = mybir.AluOpType
AX = mybir.AxisListType

T = 4096
TB = 2048
NTB = 16
NT = 32
D = 1024
ND = 8
FF = 4096
NF = 32
SCALE = float(np.float32(1.0) / np.float32(np.sqrt(np.float32(2048.0))))
EPS = 1e-5

BASE = 16448
CAP = 229344
OFF_IDENTF = BASE + 0
OFF_IDENT16 = BASE + 512
OFF_PTH0 = BASE + 768
OFF_CONS = BASE + 2816
OFF_STAT = BASE + 3456
OFF_NW8 = BASE + 5248
OFF_NB16 = BASE + 5280
OFF_NW16 = BASE + 5312
OFF_ONES16R = BASE + 5344
OFF_FL16 = BASE + 5632
OFF_FR16 = BASE + 9728
OFF_PROJB16 = BASE + 15872
PWL_SET0 = [BASE + 5632 + 2048 * i for i in range(6)]
OFF_Q = BASE + 17920
OFF_K = BASE + 50688
OFF_W1B = BASE + 17920
OFF_B16STG = BASE + 17920
OFF_ROWA = BASE + 26112
OFF_ROWB = BASE + 42496
OFF_W1Q = BASE + 83456
OFF_XCM = BASE + 116224
OFF_W2B = BASE + 83456
OFF_PWT = BASE + 148992
OFF_VCM = BASE + 165376
OFF_XST0 = OFF_VCM + 0
OFF_XST1 = OFF_VCM + 4096
OFF_X1T0 = OFF_VCM + 8192
OFF_X1T1 = OFF_VCM + 12288
OFF_SQSCR = OFF_VCM + 16384
OFF_Z2T0 = OFF_VCM + 20480
OFF_Z2T1 = OFF_VCM + 22528
OFF_BSCR0 = OFF_VCM + 24576
OFF_BSCR1 = OFF_VCM + 28672
OFF_WT = BASE + 198144
OFF_WN = BASE + 200192
OFF_Z2CM = BASE + 148992
OFF_YT = BASE + 165376
OFF_X1B = BASE + 198144
PWL_SET1 = [BASE + 157184 + 2048 * i for i in range(4)] + [BASE + 202240, BASE + 204288]
OFF_FT4 = BASE + 206336
OFF_FY1I = BASE + 208384
OFF_B2ROW16 = BASE + 210432
assert BASE + 212480 + 256 <= CAP

DTB = {F32: 4, BF16: 2, I32: 4}


def build_program():
    nc = bass.Bass("TRN2", target_bir_lowering=False)

    xs = nc.dram_tensor("xs", [T, D], F32, kind="ExternalInput").ap()
    qkv_w = nc.dram_tensor("qkv_w", [3 * D, D], F32, kind="ExternalInput").ap()
    qkv_b = nc.dram_tensor("qkv_b", [3 * D], F32, kind="ExternalInput").ap()
    norm_w = nc.dram_tensor("norm_w", [D], F32, kind="ExternalInput").ap()
    norm_b = nc.dram_tensor("norm_b", [D], F32, kind="ExternalInput").ap()
    proj_w = nc.dram_tensor("proj_w", [D, D], F32, kind="ExternalInput").ap()
    proj_b = nc.dram_tensor("proj_b", [D], F32, kind="ExternalInput").ap()
    w1 = nc.dram_tensor("w1", [D, FF], F32, kind="ExternalInput").ap()
    b1 = nc.dram_tensor("b1", [FF], F32, kind="ExternalInput").ap()
    w2 = nc.dram_tensor("w2", [FF, D], F32, kind="ExternalInput").ap()
    b2 = nc.dram_tensor("b2", [D], F32, kind="ExternalInput").ap()
    identf_in = nc.dram_tensor("identf", [128, 128], F32, kind="ExternalInput").ap()
    ident16_in = nc.dram_tensor("ident16", [128, 128], BF16, kind="ExternalInput").ap()
    ys = nc.dram_tensor("ys", [T, D], F32, kind="ExternalOutput").ap()

    x16scr = nc.dram_tensor("x16scr", [T, D], BF16, kind="Internal").ap()
    qw16scr = nc.dram_tensor("qw16scr", [3 * D, D], BF16, kind="Internal").ap()
    pw16scr = nc.dram_tensor("pw16scr", [D, D], BF16, kind="Internal").ap()
    x1scr = nc.dram_tensor("x1scr", [T, D], F32, kind="Internal").ap()
    z2scr = nc.dram_tensor("z2scr", [T, D], BF16, kind="Internal").ap()
    rowscr = nc.dram_tensor("rowscr", [2, T], F32, kind="Internal").ap()
    r16scr = nc.dram_tensor("r16scr", [4, T], BF16, kind="Internal").ap()

    with contextlib.ExitStack() as stack:
        E = stack.enter_context

        def sb(name, shape, dtype, off):
            need = 1
            for s_ in shape[1:]:
                need *= s_
            need *= DTB[dtype]
            assert off + need <= CAP, (name, off, need)
            return nc.alloc_sbuf_tensor_at(name, list(shape), dtype, offset=off)

        identf = sb("identf", [128, 128], F32, OFF_IDENTF)
        ident16 = sb("ident16", [128, 128], BF16, OFF_IDENT16)
        pth16 = sb("pth16", [128, 1024], BF16, OFF_PTH0)
        strm = sb("strm", [128, 64], F32, BASE + 212480)
        nwb = sb("nwb", [128, 16], BF16, OFF_PTH0 + 512)
        ST = {}
        for i, n in enumerate(["mu", "mun", "s1", "s", "rs", "a1", "a2", "m2n", "s2",
                               "tmpA", "tmpB", "ssq", "b1adj", "stx"]):
            ST[n] = sb("st_" + n, [128, 32], F32, OFF_STAT + 128 * i)
        b1adj = ST["b1adj"]
        nw8 = sb("nw8", [128, 8], F32, OFF_NW8)
        nb16c = sb("nb16c", [128, 8], BF16, OFF_NB16)
        nw16c = sb("nw16c", [128, 8], BF16, OFF_NW16)
        ones16r = sb("ones16r", [1, 128], BF16, OFF_ONES16R)
        fl16 = sb("fl16", [2, 2048], BF16, OFF_FL16)
        fr16 = sb("fr16", [2, 3072], BF16, OFF_FR16)
        projb16 = sb("projb16", [1, 1024], BF16, OFF_PROJB16)
        b2row16 = sb("b2row16", [1, 1024], BF16, OFF_B2ROW16)
        Q = sb("Q", [128, 16, 1024], BF16, OFF_Q)
        Kt = sb("K", [128, 16, 1024], BF16, OFF_K)
        w1b = sb("w1b", [128, 8, 4096], BF16, OFF_W1B)
        b16stg = sb("b16stg", [1, 4096], BF16, OFF_B16STG)
        rowA = sb("rowA", [1, 4096], F32, OFF_ROWA)
        rowB = sb("rowB", [1, 4096], F32, OFF_ROWB)
        W1q = sb("W1q", [128, 8, 2048], BF16, OFF_W1Q)
        W1v = sb("W1v", [128, 8, 1024], BF16, OFF_W1Q)
        xcm = sb("xcm", [128, 8, 2048], BF16, OFF_XCM)
        acm = sb("acm", [128, 8, 2048], BF16, OFF_XCM)
        w2b = sb("w2b", [128, 32, 1024], BF16, OFF_W2B)
        pwT = sb("pwT", [128, 8, 1024], BF16, OFF_PWT)
        vcm = sb("vcm", [128, 8, 2048], BF16, OFF_VCM)
        xst = [sb("xst0", [128, 1024], F32, OFF_XST0), sb("xst1", [128, 1024], F32, OFF_XST1)]
        x1t = [sb("x1t0", [128, 1024], F32, OFF_X1T0), sb("x1t1", [128, 1024], F32, OFF_X1T1)]
        sqscr = sb("sqscr", [128, 1024], F32, OFF_SQSCR)
        z2t = [sb("z2t0", [128, 1024], BF16, OFF_Z2T0), sb("z2t1", [128, 1024], BF16, OFF_Z2T1)]
        bscr = [sb("bscr0", [128, 2048], BF16, OFF_BSCR0), sb("bscr1", [128, 2048], BF16, OFF_BSCR1)]
        Wt = sb("Wt", [128, 1024], BF16, OFF_WT)
        Wn = sb("Wn", [128, 1024], BF16, OFF_WN)
        vtm = sb("vtm", [128, 1024], BF16, OFF_WN)
        z2cm = sb("z2cm", [128, 8, 512], BF16, OFF_Z2CM)
        yt = sb("yt", [128, 32, 512], BF16, OFF_YT)
        x1b = sb("x1b", [128, 1024], F32, OFF_X1B)

        def con(name, dtype, idx):
            return nc.alloc_sbuf_tensor_at(name, [128, 1], dtype, offset=OFF_CONS + 32 * idx)
        mA = con("mA", I32, 0)
        mR = con("mR", I32, 1)
        c16k = con("c16k", F32, 2)
        ceps = con("ceps", F32, 3)
        c025 = con("c025", F32, 4)
        onesf1 = con("onesf1", F32, 6)
        e0row = nc.alloc_sbuf_tensor_at("e0row", [1, 2], BF16, offset=OFF_CONS + 32 * 7)
        mxc = [con("mxc0", F32, 8), con("mxc1", F32, 9)]
        mbc = [con("mbc0", F32, 10), con("mbc1", F32, 11)]
        su1 = [con("su10", F32, 12), con("su11", F32, 13)]
        su2 = [con("su20", F32, 14), con("su21", F32, 15)]
        suc = [con("suc0", F32, 16), con("suc1", F32, 17)]
        rcc = [con("rcc0", F32, 18), con("rcc1", F32, 19)]

        SETO = [PWL_SET0, PWL_SET1]
        HS = []
        for s_i in range(2):
            fv = [nc.alloc_sbuf_tensor_at(f"hs{s_i}f{k}", [128, 512], F32, offset=SETO[s_i][k]) for k in range(6)]
            iv = {k: nc.alloc_sbuf_tensor_at(f"hs{s_i}i{k}", [128, 512], I32, offset=SETO[s_i][k]) for k in (1, 2, 5)}
            HS.append({"f": fv, "i": iv})
        fT4 = nc.alloc_sbuf_tensor_at("fT4", [128, 512], F32, offset=OFF_FT4)
        fY1I = nc.alloc_sbuf_tensor_at("fY1I", [128, 512], I32, offset=OFF_FY1I)
        fC2 = nc.alloc_sbuf_tensor_at("fC2", [128, 512], F32, offset=OFF_FY1I)
        fYF1 = nc.alloc_sbuf_tensor_at("fYF1", [128, 512], F32, offset=OFF_PTH0)
        fX2 = nc.alloc_sbuf_tensor_at("fX2", [128, 512], F32, offset=OFF_PTH0)

        pf = E(nc.psum_tensor("pf", [128, 3584], F32))
        pb = E(nc.psum_tensor("pb", [128, 1024], BF16))

        sem_names = ["dXF0", "dXF1", "dW", "dWq", "dTRa", "dTRb", "dTRp",
                     "dTB0", "dTB1", "dTZ", "dCX0", "dCX1", "dCQ", "dCP",
                     "dCW", "dCW2", "dZ20", "dZ21", "dX1S0", "dX1S1",
                     "dX1L", "dOUT", "cPE", "cDV", "cAC", "cPL"]
        sems = {n: E(nc.semaphore(n)) for n in sem_names}
        cnt = {n: 0 for n in sem_names}
        ops = {e: [] for e in ("sync", "tensor", "vector", "scalar", "gpsimd")}

        def emit(e, fn):
            ops[e].append(fn)

        def inc(e, sem, n, build):
            cnt[sem] += n
            emit(e, lambda eng, b_=build, s_=sems[sem], k_=n: b_(eng).then_inc(s_, k_))
            return cnt[sem]

        def wait(e, sem, v):
            if v > 0:
                emit(e, lambda eng, s_=sems[sem], v_=v: eng.wait_ge(s_, v_))

        fill = {}
        drain = {}

        def add_drain(key, sem, v):
            drain[key] = [(sem, v)]

        def app_drain(key, sem, v):
            drain.setdefault(key, []).append((sem, v))

        def wait_drain(e, key):
            for s_, v_ in drain.get(key, ()):
                wait(e, s_, v_)

        def barrier(e):
            for s_ in ("cDV", "cAC", "cPL"):
                wait(e, s_, cnt[s_])

        def vsync_op(build):
            v = inc("vector", "cDV", 1, build)
            wait("vector", "cDV", v)
            return v

        for f in (lambda v: v.memset(mA[:], -64),
                  lambda v: v.memset(mR[:], 63),
                  lambda v: v.memset(c16k[:], 16384.0),
                  lambda v: v.memset(ceps[:], float(EPS)),
                  lambda v: v.memset(c025[:], 0.25),
                  lambda v: v.memset(onesf1[:], 1.0)):
            emit("vector", f)
        v_memset = inc("vector", "cDV", 1, lambda v: v.memset(ones16r[:], 1.0))

        vx16 = [0, 0]
        for b in range(2):
            vx16[b] = inc("gpsimd", f"dCX{b}", 16,
                          lambda g, b=b: g.dma_start(out=x16scr[b * TB:(b + 1) * TB, :],
                                                     in_=xs[b * TB:(b + 1) * TB, :]))
        vqw16 = inc("gpsimd", "dCQ", 16, lambda g: g.dma_start(out=qw16scr, in_=qkv_w))
        vpw16 = inc("gpsimd", "dCP", 16, lambda g: g.dma_start(out=pw16scr, in_=proj_w))

        videt = inc("sync", "dW", 16, lambda s: s.dma_start(out=identf[:], in_=identf_in))
        videt16 = inc("sync", "dW", 16, lambda s: s.dma_start(out=ident16[:], in_=ident16_in))
        vnw = inc("sync", "dW", 16, lambda s: s.dma_start(out=rowA[:1, 0:1024], in_=norm_w[None, :]))
        vnb = inc("sync", "dW", 16, lambda s: s.dma_start(out=rowA[:1, 1024:2048], in_=norm_b[None, :]))
        vpbld = inc("sync", "dW", 16, lambda s: s.dma_start(out=rowA[:1, 2048:3072], in_=proj_b[None, :]))
        vb2ld = inc("sync", "dW", 16, lambda s: s.dma_start(out=rowA[:1, 3072:4096], in_=b2[None, :]))
        vb1ld = inc("sync", "dW", 16, lambda s: s.dma_start(out=rowB[:1, :], in_=b1[None, :]))
        v_dW_all = cnt["dW"]

        wait("sync", "dCQ", vqw16)

        def load_W1q():
            vv = 0
            for j in range(ND):
                vv = inc("sync", "dTRa", 16,
                         lambda s, j=j: s.dma_start(out=W1q[:, j, :],
                                                    in_=qw16scr[0:2048, 128 * j:128 * (j + 1)],
                                                    transpose=True))
            return vv

        def load_W1v():
            vv = 0
            for j in range(ND):
                vv = inc("sync", "dTRb", 16,
                         lambda s, j=j: s.dma_start(out=W1v[:, j, :],
                                                    in_=qw16scr[2048:3072, 128 * j:128 * (j + 1)],
                                                    transpose=True))
            return vv

        vW1 = load_W1q()
        wait("sync", "dCP", vpw16)
        for j in range(ND):
            inc("sync", "dTRp", 16,
                lambda s, j=j: s.dma_start(out=pwT[:, j, :], in_=pw16scr[:, 128 * j:128 * (j + 1)],
                                           transpose=True))
        v_pwT = cnt["dTRp"]

        wait("tensor", "dW", v_dW_all)
        wait("tensor", "cDV", v_memset)

        def columnize(src_row_fn, evac_fn, n):
            for c in range(n):
                slot = c % 4
                key = ("bank6",)
                wait_drain("tensor", key)
                v = inc("tensor", "cPE", 1,
                        lambda t, c=c, slot=slot: t.matmul(pf[:, 3072 + 128 * slot:3072 + 128 * slot + 1],
                                                           src_row_fn(c), onesf1[0:1, 0:1],
                                                           start=True, stop=True))
                wait("vector", "cPE", v)
                dv = evac_fn(c, slot)
                add_drain(key, "cDV", dv)

        columnize(lambda c: rowB[0:1, 128 * c:128 * (c + 1)],
                  lambda c, slot: inc("vector", "cDV", 1,
                                      lambda v_, c=c, slot=slot: v_.tensor_scalar(
                                          out=b1adj[:, c:c + 1],
                                          in0=pf[:, 3072 + 128 * slot:3072 + 128 * slot + 1],
                                          scalar1=256.0, scalar2=None, op0=ALU.mult)), 32)
        columnize(lambda c: rowA[0:1, 128 * c:128 * (c + 1)],
                  lambda c, slot: inc("vector", "cDV", 1,
                                      lambda v_, c=c, slot=slot: v_.tensor_copy(
                                          out=nw8[:, c:c + 1],
                                          in_=pf[:, 3072 + 128 * slot:3072 + 128 * slot + 1])), 8)
        columnize(lambda c: rowA[0:1, 1024 + 128 * c:1024 + 128 * (c + 1)],
                  lambda c, slot: inc("vector", "cDV", 1,
                                      lambda v_, c=c, slot=slot: v_.tensor_copy(
                                          out=nb16c[:, c:c + 1],
                                          in_=pf[:, 3072 + 128 * slot:3072 + 128 * slot + 1])), 8)
        vnw16 = inc("vector", "cDV", 1, lambda v_: v_.tensor_copy(out=nw16c[:], in_=nw8[:]))
        wait("vector", "dW", v_dW_all)
        inc("vector", "cDV", 1, lambda v_: v_.tensor_copy(out=projb16[:1, :], in_=rowA[:1, 2048:3072]))
        inc("vector", "cDV", 1, lambda v_: v_.tensor_scalar(out=b2row16[:1, :], in0=rowA[:1, 3072:4096],
                                                            scalar1=256.0, scalar2=None, op0=ALU.mult))

        wait("tensor", "dTRa", vW1)
        wait("tensor", "cDV", vnw16)
        barrier("tensor")
        wait("sync", "cPE", cnt["cPE"])
        vqbld = inc("sync", "dWq", 16, lambda s: s.dma_start(out=rowB[:1, 0:3072], in_=qkv_b[None, :]))
        v_e0 = inc("vector", "cDV", 1, lambda v_: v_.memset(e0row[:], 0.0))
        wait("vector", "cDV", v_e0)
        emit("vector", lambda v_: v_.tensor_scalar(out=e0row[0:1, 0:1], in0=onesf1[0:1, 0:1],
                                                   scalar1=1.0, scalar2=None, op0=ALU.mult))
        emit("vector", lambda v_: v_.tensor_copy(out=nwb[:, 0:16:2], in_=nb16c[:]))
        emit("vector", lambda v_: v_.tensor_copy(out=nwb[:, 1:16:2], in_=nw16c[:]))
        wait("vector", "dWq", vqbld)
        dvqb = inc("vector", "cDV", 1, lambda v_: v_.tensor_copy(out=b16stg[:1, 0:3072], in_=rowB[:1, 0:3072]))
        wait("tensor", "cDV", dvqb)

        def wsum_bias_slices(wt_tensor, col0, sls):
            for sl in sls:
                rel = 512 * sl - col0
                key = ("bank6",)
                wait_drain("tensor", key)
                v = 0
                for j in range(ND):
                    v = inc("tensor", "cPE", 1,
                            lambda t, j=j, rel=rel, wt=wt_tensor: t.matmul(
                                pf[0:2, 3072:3584], nwb[:, 2 * j:2 * j + 2],
                                wt[:, j, rel:rel + 512], start=(j == 0), stop=False))
                v = inc("tensor", "cPE", 1,
                        lambda t, sl=sl: t.matmul(pf[0:2, 3072:3584], e0row[0:1, :],
                                                  b16stg[0:1, 512 * sl:512 * (sl + 1)],
                                                  start=False, stop=True))
                wait("vector", "cPE", v)
                dv = inc("vector", "cDV", 1,
                         lambda v_, sl=sl: v_.tensor_copy(out=fr16[0:2, 512 * sl:512 * (sl + 1)],
                                                          in_=pf[0:2, 3072:3584]))
                add_drain(key, "cDV", dv)

        wsum_bias_slices(W1q, 0, [0, 1, 2, 3])
        v_wb1_pe = cnt["cPE"]
        wait("sync", "cPE", v_wb1_pe)
        vW1v = load_W1v()
        wait("tensor", "dTRb", vW1v)
        wsum_bias_slices(W1v, 2048, [4, 5])
        v_wb2_pe = cnt["cPE"]

        ev = lambda f: emit("vector", f)
        tile_sq = [0] * NT
        tile_rd = [0] * NT
        tile_mu = [0] * NT
        for g in range(NT):
            buf = g % 2
            if g >= 2:
                wait("sync", "cAC", tile_sq[g - 2])
                wait("sync", "cDV", tile_rd[g - 2])
            vld = inc("sync", f"dXF{buf}", 16,
                      lambda s, g=g, buf=buf: s.dma_start(out=xst[buf][:], in_=xs[128 * g:128 * (g + 1), :]))
            wait("vector", f"dXF{buf}", vld)
            tile_mu[g] = inc("vector", "cDV", 1,
                             lambda v_, g=g, buf=buf: v_.reduce_sum(out=ST["mu"][:, g:g + 1], in_=xst[buf][:], axis=AX.X))
            wait("scalar", f"dXF{buf}", vld)
            if g >= 1:
                wait("scalar", "cDV", tile_rd[g - 1])
            tile_sq[g] = inc("scalar", "cAC", 1,
                             lambda sc, buf=buf: sc.activation(out=sqscr[:], in_=xst[buf][:], func=AF.Square))
            wait("vector", "cAC", tile_sq[g])
            tile_rd[g] = inc("vector", "cDV", 1,
                             lambda v_, g=g: v_.reduce_sum(out=ST["ssq"][:, g:g + 1], in_=sqscr[:], axis=AX.X))

        wait("vector", "cDV", tile_mu[NT - 1])
        wait("vector", "cDV", tile_rd[NT - 1])
        vsync_op(lambda v_: v_.tensor_scalar(out=ST["tmpA"][:], in0=ST["mu"][:], scalar1=1.0 / 1024.0, scalar2=None, op0=ALU.mult))
        vsync_op(lambda v_: v_.tensor_scalar(out=ST["tmpB"][:], in0=ST["ssq"][:], scalar1=1.0 / 1024.0, scalar2=None, op0=ALU.mult))
        vsync_op(lambda v_: v_.tensor_tensor(out=ST["stx"][:], in0=ST["tmpA"][:], in1=ST["tmpA"][:], op=ALU.mult))
        vv1 = vsync_op(lambda v_: v_.tensor_tensor(out=ST["ssq"][:], in0=ST["tmpB"][:], in1=ST["stx"][:], op=ALU.subtract))
        wait("scalar", "cDV", vv1)
        va1 = inc("scalar", "cAC", 1,
                  lambda sc: sc.activation(out=ST["tmpB"][:], in_=ST["ssq"][:], func=AF.Sqrt, bias=ceps[:], scale=1.0))
        va1b = inc("scalar", "cAC", 1,
                   lambda sc: sc.activation(out=ST["stx"][:], in_=ST["ssq"][:], func=AF.Relu, bias=ceps[:], scale=1.0))
        wait("vector", "cAC", va1b)
        vsync_op(lambda v_: v_.reciprocal(out=ST["s1"][:], in_=ST["tmpB"][:]))
        vsync_op(lambda v_: v_.tensor_scalar(out=ST["a1"][:], in0=ST["s1"][:], scalar1=1.0, scalar2=None, op0=ALU.add))
        vsync_op(lambda v_: v_.scalar_tensor_tensor(out=ST["a2"][:], in0=ST["tmpA"][:], scalar=-1.0, in1=ST["s1"][:],
                                                    op0=ALU.mult, op1=ALU.mult))
        vsync_op(lambda v_: v_.tensor_scalar(out=ST["mun"][:], in0=ST["tmpA"][:], scalar1=-1.0, scalar2=None, op0=ALU.mult))
        vsync_op(lambda v_: v_.reciprocal(out=ST["tmpA"][:], in_=ST["stx"][:]))
        vv2 = vsync_op(lambda v_: v_.tensor_tensor(out=ST["stx"][:], in0=ST["ssq"][:], in1=ST["tmpA"][:], op=ALU.mult))
        wait("scalar", "cDV", vv2)
        va2 = inc("scalar", "cAC", 1,
                  lambda sc: sc.activation(out=ST["tmpA"][:], in_=ST["stx"][:], func=AF.Sqrt, bias=ceps[:], scale=1.0))
        wait("vector", "cAC", va2)
        vsync_op(lambda v_: v_.tensor_tensor(out=ST["rs"][:], in0=ST["tmpB"][:], in1=ST["tmpA"][:], op=ALU.mult))
        vsync_op(lambda v_: v_.reciprocal(out=ST["stx"][:], in_=ST["tmpA"][:]))
        vsync_op(lambda v_: v_.tensor_tensor(out=ST["s"][:], in0=ST["s1"][:], in1=ST["stx"][:], op=ALU.mult))
        vstat = inc("vector", "cDV", 1, lambda v_: v_.engine_nop())

        wait("tensor", "cDV", vstat)
        wait("tensor", "dW", v_dW_all)
        ev(lambda v_: v_.tensor_copy(out=strm[:, 0:64:2], in_=ST["rs"][:]))
        v_strm = inc("vector", "cDV", 1, lambda v_: v_.tensor_copy(out=strm[:, 1:64:2], in_=ST["mun"][:]))

        v_SlastPE = [0, 0]
        v_projpe = [0, 0]
        v_alast = [0, 0]
        v_z2st = [0, 0]
        v_x1st = [0, 0]
        vz2end = [None, None]
        vx1end = [None, None]
        qk_regions = [0, 1024, 2048]
        v_lastW1pe = v_wb2_pe
        for b in range(2):
            T0 = b * TB
            wait("tensor", "cDV", v_strm)
            if b == 1:
                barrier("tensor")
            for c in range(NTB):
                g = NTB * b + c
                key = ("bank6",)
                wait_drain("tensor", key)
                v = inc("tensor", "cPE", 1,
                        lambda t, g=g: t.matmul(pf[0:2, 3072:3200], strm[:, 2 * g:2 * g + 2],
                                                identf[:], start=True, stop=True))
                wait("vector", "cPE", v)
                if b == 1 and c == 0:
                    wait("vector", "cPE", v_projpe[0])
                dv = inc("vector", "cDV", 1,
                         lambda v_, c=c: v_.tensor_copy(out=fl16[0:2, 128 * c:128 * (c + 1)],
                                                        in_=pf[0:2, 3072:3200]))
                add_drain(key, "cDV", dv)
            v_fl_dv = cnt["cDV"]
            wait("sync", "cPE", v_lastW1pe)
            vW1 = load_W1q()
            wait("sync", f"dCX{b}", vx16[b])
            if b == 1:
                wait("sync", "cPE", v_alast[0])
            v_xcm = 0
            for j in range(ND):
                sbf = j % 2
                wait_drain("sync", ("bscr", sbf))
                vld = inc("sync", f"dTB{sbf}", 16,
                          lambda s, j=j, sbf=sbf, T0=T0: s.dma_start(out=bscr[sbf][:],
                                                                     in_=x16scr[T0:T0 + TB, 128 * j:128 * (j + 1)],
                                                                     transpose=True))
                wait("vector", f"dTB{sbf}", vld)
                v_xcm = inc("vector", "cDV", 1,
                            lambda v_, j=j, sbf=sbf: v_.tensor_scalar(out=xcm[:, j, :], in0=bscr[sbf][:],
                                                                      scalar1=nw8[:, j:j + 1], scalar2=None,
                                                                      op0=ALU.mult))
                add_drain(("bscr", sbf), "cDV", v_xcm)

            wait("tensor", "cDV", v_xcm)
            wait("tensor", "cDV", v_fl_dv)
            wait("tensor", "dTRa", vW1)
            barrier("tensor")
            for m in range(NTB):
                for half in range(2):
                    off = qk_regions[(2 * m + half) % 3]
                    key = ("qk", off)
                    wait_drain("tensor", key)
                    v = 0
                    for j in range(ND):
                        for s2_ in range(2):
                            v = inc("tensor", "cPE", 1,
                                    lambda t, j=j, s2_=s2_, m=m, half=half, off=off:
                                    t.matmul(pf[:, off + 512 * s2_:off + 512 * (s2_ + 1)],
                                             xcm[:, j, 128 * m:128 * (m + 1)],
                                             W1q[:, j, 1024 * half + 512 * s2_:1024 * half + 512 * (s2_ + 1)],
                                             start=(j == 0), stop=False))
                    for s2_ in range(2):
                        v = inc("tensor", "cPE", 1,
                                lambda t, s2_=s2_, half=half, off=off, m=m:
                                t.matmul(pf[:, off + 512 * s2_:off + 512 * (s2_ + 1)],
                                         fl16[0:2, 128 * m:128 * (m + 1)],
                                         fr16[0:2, 1024 * half + 512 * s2_:1024 * half + 512 * (s2_ + 1)],
                                         start=False, stop=True))
                    wait("vector", "cPE", v)
                    dst_t = Q if half == 0 else Kt
                    g = NTB * b + m
                    dv = inc("vector", "cDV", 1,
                             lambda v_, dst_t=dst_t, m=m, off=off, g=g:
                             v_.tensor_scalar(out=dst_t[:, m, :], in0=pf[:, off:off + 1024],
                                              scalar1=ST["s"][:, g:g + 1], scalar2=None, op0=ALU.mult))
                    add_drain(key, "cDV", dv)
            v_qk_pe = cnt["cPE"]

            for _k in (("x1t", 0), ("x1t", 1), ("z2t", 0), ("z2t", 1), ("xst", 0), ("xst", 1)):
                wait_drain("vector", _k)
            wait("sync", "cPE", v_qk_pe)
            vW1v = load_W1v()
            wait("tensor", "dTRb", vW1v)
            for m in range(NTB):
                off = qk_regions[m % 3]
                key = ("qk", off)
                wait_drain("tensor", key)
                v = 0
                for j in range(ND):
                    for s2_ in range(2):
                        v = inc("tensor", "cPE", 1,
                                lambda t, j=j, s2_=s2_, m=m, off=off:
                                t.matmul(pf[:, off + 512 * s2_:off + 512 * (s2_ + 1)],
                                         xcm[:, j, 128 * m:128 * (m + 1)],
                                         W1v[:, j, 512 * s2_:512 * (s2_ + 1)],
                                         start=(j == 0), stop=False))
                for s2_ in range(2):
                    v = inc("tensor", "cPE", 1,
                            lambda t, s2_=s2_, off=off, m=m:
                            t.matmul(pf[:, off + 512 * s2_:off + 512 * (s2_ + 1)],
                                     fl16[0:2, 128 * m:128 * (m + 1)],
                                     fr16[0:2, 2048 + 512 * s2_:2048 + 512 * (s2_ + 1)],
                                     start=False, stop=True))
                wait("vector", "cPE", v)
                wait_drain("vector", ("vtm",))
                dv = inc("vector", "cDV", 1,
                         lambda v_, off=off: v_.tensor_copy(out=vtm[:], in_=pf[:, off:off + 1024]))
                add_drain(key, "cDV", dv)
                wait("tensor", "cDV", dv)
                wait_drain("tensor", ("pbV",))
                vtr = 0
                for kc in range(8):
                    vtr = inc("tensor", "cPE", 1,
                              lambda t, kc=kc: t.transpose(pb[:, 128 * kc:128 * (kc + 1)],
                                                           vtm[:, 128 * kc:128 * (kc + 1)], ident16[:]))
                add_drain(("vtm",), "cPE", vtr)
                wait("vector", "cPE", vtr)
                dvv = 0
                for kc in range(8):
                    dvv = inc("vector", "cDV", 1,
                              lambda v_, kc=kc, m=m: v_.tensor_copy(out=vcm[:, kc, 128 * m:128 * (m + 1)],
                                                                    in_=pb[:, 128 * kc:128 * (kc + 1)]))
                add_drain(("pbV",), "cDV", dvv)
            v_lastW1pe = cnt["cPE"]
            v_vtm_pe = cnt["cPE"]
            v_vcm_dv = cnt["cDV"]

            wait("tensor", "cDV", v_vcm_dv)
            wait("tensor", "dW", v_dW_all)
            barrier("tensor")
            wait("vector", "cPE", v_vtm_pe)
            for qt in range(8):
                p = qt % 2
                s_off = 1024 * p
                key = ("S", p)
                wait_drain("tensor", key)
                v = 0
                for c in range(NTB):
                    for s2_ in range(2):
                        v = inc("tensor", "cPE", 1,
                                lambda t, c=c, s2_=s2_, qt=qt, s_off=s_off:
                                t.matmul(pf[:, s_off + 512 * s2_:s_off + 512 * (s2_ + 1)],
                                         Q[:, c, 128 * qt:128 * (qt + 1)],
                                         Kt[:, c, 512 * s2_:512 * (s2_ + 1)],
                                         start=(c == 0), stop=(c == NTB - 1)))
                if qt == 7:
                    v_SlastPE[b] = v
                wait("vector", "cPE", v)
                vsync_op(lambda v_, p=p, s_off=s_off: v_.reduce_max(out=mxc[p][:], in_=pf[:, s_off:s_off + 1024], axis=AX.X))
                dvm = inc("vector", "cDV", 1,
                          lambda v_, p=p: v_.tensor_scalar(out=mbc[p][:], in0=mxc[p][:], scalar1=-SCALE,
                                                           scalar2=None, op0=ALU.mult))
                wait("scalar", "cDV", dvm)
                wait_drain("scalar", ("pth",))
                va = 0
                for hh in range(2):
                    va = inc("scalar", "cAC", 1,
                             lambda sc, p=p, s_off=s_off, hh=hh:
                             sc.activation(out=pth16[:, 512 * hh:512 * (hh + 1)],
                                           in_=pf[:, s_off + 512 * hh:s_off + 512 * (hh + 1)],
                                           func=AF.Exp, bias=mbc[p][:], scale=SCALE,
                                           accum_out=(su1[p][:] if hh == 0 else su2[p][:])))
                add_drain(key, "cAC", va)
                wait("vector", "cAC", va)
                wait("gpsimd", "cAC", va)
                vsuc = inc("gpsimd", "cPL", 1,
                           lambda g_, p=p: g_.tensor_tensor(out=suc[p][:], in0=su1[p][:], in1=su2[p][:], op=ALU.add))
                wait("vector", "cPL", vsuc)
                v_rcc = vsync_op(lambda v_, p=p: v_.reciprocal(out=rcc[p][:], in_=suc[p][:]))
                wait_drain("gpsimd", ("Wn",))
                wait("gpsimd", "cDV", v_rcc)
                dvn = 0
                for hh in range(2):
                    dvn = inc("gpsimd", "cPL", 1,
                              lambda g_, p=p, hh=hh:
                              g_.tensor_scalar(out=Wn[:, 512 * hh:512 * (hh + 1)],
                                               in0=pth16[:, 512 * hh:512 * (hh + 1)],
                                               scalar1=rcc[p][:], scalar2=None, op0=ALU.mult))
                add_drain(("pth",), "cPL", dvn)
                wait("tensor", "cPL", dvn)
                wait_drain("tensor", ("Wt",))
                vtr = 0
                for kc in range(8):
                    vtr = inc("tensor", "cPE", 1,
                              lambda t, kc=kc: t.transpose(pb[:, 128 * kc:128 * (kc + 1)],
                                                           Wn[:, 128 * kc:128 * (kc + 1)], ident16[:]))
                add_drain(("Wn",), "cPE", vtr)
                wait("vector", "cPE", vtr)
                dvt = inc("vector", "cDV", 1, lambda v_: v_.tensor_copy(out=Wt[:], in_=pb[:]))
                wait("tensor", "cDV", dvt)
                va_last = 0
                for w in range(4):
                    slot = 4 + (4 * qt + w) % 3
                    off = 512 * slot
                    key2 = ("a", slot)
                    wait_drain("tensor", key2)
                    v = 0
                    for kc in range(8):
                        v = inc("tensor", "cPE", 1,
                                lambda t, kc=kc, w=w, off=off:
                                t.matmul(pf[:, off:off + 512], Wt[:, 128 * kc:128 * (kc + 1)],
                                         vcm[:, kc, 512 * w:512 * (w + 1)],
                                         start=(kc == 0), stop=(kc == 7)))
                    va_last = v
                    wait("scalar", "cPE", v)
                    vaa = inc("scalar", "cAC", 1,
                              lambda sc, qt=qt, w=w, off=off:
                              sc.activation(out=acm[:, qt, 512 * w:512 * (w + 1)], in_=pf[:, off:off + 512],
                                            func=AF.Copy, scale=1.0))
                    add_drain(key2, "cAC", vaa)
                add_drain(("Wt",), "cPE", va_last)
            v_alast[b] = cnt["cPE"]
            v_acm_done = cnt["cAC"]

            wait("sync", "cPE", v_alast[b])
            wait("vector", "cPE", v_alast[b])
            wait("tensor", "cAC", v_acm_done)
            barrier("tensor")
            barrier("scalar")
            for _sm in ("dX1S0", "dX1S1", "dZ20", "dZ21"):
                wait("scalar", _sm, cnt[_sm])
                wait("vector", _sm, cnt[_sm])
            wait("tensor", "dTRp", v_pwT)
            for m in range(NTB):
                g = NTB * b + m
                xbuf = m % 2
                wait_drain("sync", ("xst", xbuf))
                vxl = inc("sync", f"dXF{xbuf}", 16,
                          lambda s, g=g, xbuf=xbuf: s.dma_start(out=xst[xbuf][:], in_=xs[128 * g:128 * (g + 1), :]))
                pe_fill = [0, 0]
                for s2_ in range(2):
                    slot = (2 * m + s2_) % 4
                    off = 512 * slot
                    key = ("pj", slot)
                    wait_drain("tensor", key)
                    v = 0
                    for qc in range(8):
                        v = inc("tensor", "cPE", 1,
                                lambda t, qc=qc, m=m, s2_=s2_, off=off:
                                t.matmul(pf[:, off:off + 512], acm[:, qc, 128 * m:128 * (m + 1)],
                                         pwT[:, qc, 512 * s2_:512 * (s2_ + 1)],
                                         start=(qc == 0), stop=False))
                    v = inc("tensor", "cPE", 1,
                            lambda t, s2_=s2_, off=off, m=m:
                            t.matmul(pf[:, off:off + 512], fl16[0:1, 128 * m:128 * (m + 1)],
                                     projb16[0:1, 512 * s2_:512 * (s2_ + 1)],
                                     start=False, stop=True))
                    pe_fill[s2_] = v
                v_projpe[b] = pe_fill[1]
                wait("vector", f"dXF{xbuf}", vxl)
                x1buf = m % 2
                wait_drain("vector", ("x1t", x1buf))
                dvxl = inc("vector", "cDV", 1,
                           lambda v_, xbuf=xbuf, g=g: v_.tensor_scalar(out=sqscr[:], in0=xst[xbuf][:],
                                                                      scalar1=ST["a1"][:, g:g + 1],
                                                                      scalar2=ST["a2"][:, g:g + 1],
                                                                      op0=ALU.mult, op1=ALU.add))
                add_drain(("xst", xbuf), "cDV", dvxl)
                wait("vector", "cDV", dvxl)
                dvx = 0
                for s2_ in range(2):
                    slot = (2 * m + s2_) % 4
                    wait("vector", "cPE", pe_fill[s2_])
                    dvx = inc("vector", "cDV", 1,
                              lambda v_, s2_=s2_, slot=slot, g=g, x1buf=x1buf:
                              v_.scalar_tensor_tensor(out=x1t[x1buf][:, 512 * s2_:512 * (s2_ + 1)],
                                                      in0=pf[:, 512 * slot:512 * slot + 512],
                                                      scalar=ST["s"][:, g:g + 1],
                                                      in1=sqscr[:, 512 * s2_:512 * (s2_ + 1)],
                                                      op0=ALU.mult, op1=ALU.add))
                    add_drain(("pj", slot), "cDV", dvx)
                wait("scalar", "cDV", dvx)
                v_x1st[b] = inc("scalar", f"dX1S{x1buf}", 16,
                                lambda sc, g=g, x1buf=x1buf: sc.dma_start(out=x1scr[128 * g:128 * (g + 1), :],
                                                                          in_=x1t[x1buf][:]))
                vid = inc("scalar", "cAC", 1,
                          lambda sc, g=g, x1buf=x1buf: sc.activation(out=sqscr[:], in_=x1t[x1buf][:],
                                                                     func=AF.Identity,
                                                                     accum_out=ST["tmpA"][:, g:g + 1]))
                wait("scalar", "cAC", vid)
                vsq = inc("scalar", "cAC", 1,
                          lambda sc, g=g, x1buf=x1buf: sc.activation(out=sqscr[:], in_=x1t[x1buf][:],
                                                                     func=AF.Square,
                                                                     accum_out=ST["tmpB"][:, g:g + 1]))
                wait("gpsimd", "cAC", vsq)
                inc("gpsimd", "cPL", 1,
                    lambda g_, g=g: g_.tensor_scalar(out=ST["m2n"][:, g:g + 1], in0=ST["tmpA"][:, g:g + 1],
                                                     scalar1=-1.0 / 1024.0, scalar2=None, op0=ALU.mult))
                vpl2 = inc("gpsimd", "cPL", 1,
                           lambda g_, g=g: g_.tensor_scalar(out=ST["tmpB"][:, g:g + 1], in0=ST["tmpB"][:, g:g + 1],
                                                            scalar1=1.0 / 1024.0, scalar2=None, op0=ALU.mult))
                wait("vector", "cPL", vpl2)
                dvv = inc("vector", "cDV", 1,
                          lambda v_, g=g: v_.scalar_tensor_tensor(out=ST["tmpA"][:, g:g + 1],
                                                                  in0=ST["m2n"][:, g:g + 1],
                                                                  scalar=ST["m2n"][:, g:g + 1],
                                                                  in1=ST["tmpB"][:, g:g + 1],
                                                                  op0=ALU.mult, op1=ALU.subtract))
                wait("scalar", "cDV", dvv)
                vsq2 = inc("scalar", "cAC", 1,
                           lambda sc, g=g: sc.activation(out=ST["stx"][:, g:g + 1], in_=ST["tmpA"][:, g:g + 1],
                                                         func=AF.Sqrt, bias=ceps[:], scale=-1.0))
                wait("vector", "cAC", vsq2)
                v_s2 = vsync_op(lambda v_, g=g: v_.reciprocal(out=ST["s2"][:, g:g + 1], in_=ST["stx"][:, g:g + 1]))
                zbuf = m % 2
                wait_drain("gpsimd", ("z2t", zbuf))
                wait("gpsimd", "cDV", v_s2)
                dvz = inc("gpsimd", "cPL", 1,
                          lambda g_, g=g, x1buf=x1buf, zbuf=zbuf:
                          g_.tensor_scalar(out=z2t[zbuf][:], in0=x1t[x1buf][:],
                                           scalar1=ST["m2n"][:, g:g + 1], scalar2=ST["s2"][:, g:g + 1],
                                           op0=ALU.add, op1=ALU.mult))
                drain[("x1t", x1buf)] = [("cPL", dvz), (f"dX1S{x1buf}", v_x1st[b]), ("cAC", vsq)]
                wait("scalar", "cPL", dvz)
                v_z2st[b] = inc("scalar", f"dZ2{zbuf}", 16,
                                lambda sc, g=g, zbuf=zbuf: sc.dma_start(out=z2scr[128 * g:128 * (g + 1), :],
                                                                        in_=z2t[zbuf][:]))
                add_drain(("z2t", zbuf), f"dZ2{zbuf}", v_z2st[b])
            vz2end[b] = (cnt["dZ20"], cnt["dZ21"])
            vx1end[b] = (cnt["dX1S0"], cnt["dX1S1"])

        wait("gpsimd", "cPE", v_SlastPE[1])
        v_w1b = 0
        for j in range(ND):
            v_w1b = inc("gpsimd", "dCW", 16,
                        lambda g, j=j: g.dma_start(out=w1b[:, j, :], in_=w1[128 * j:128 * (j + 1), :]))
        wait("gpsimd", "cPE", v_projpe[1])
        v_w2b = 0
        for jj in range(8):
            v_w2b = inc("gpsimd", "dCW2", 16,
                        lambda g, jj=jj: g.dma_start(
                            out=w2b[:, 4 * jj:4 * (jj + 1), :],
                            in_=w2[512 * jj:512 * (jj + 1), :].rearrange("(c p) n -> p c n", p=128)))

        wait("sync", "cPE", v_projpe[1])
        wait("tensor", "dCW", v_w1b)
        barrier("tensor")
        for _sm in ("dZ20", "dZ21", "dX1S0", "dX1S1", "dXF0", "dXF1"):
            wait("vector", _sm, cnt[_sm])
        first_w2_wait = True

        def load_z2cm(b_, tc4_):
            base_tok = b_ * TB + 512 * tc4_
            wait("sync", "dZ20", vz2end[b_][0])
            wait("sync", "dZ21", vz2end[b_][1])
            wait_drain("sync", ("z2cm",))
            vz = 0
            for j in range(ND):
                vz = inc("sync", "dTZ", 16,
                         lambda s, j=j, base_tok=base_tok:
                         s.dma_start(out=z2cm[:, j, :],
                                     in_=z2scr[base_tok:base_tok + 512, 128 * j:128 * (j + 1)],
                                     transpose=True))
            return vz

        chunks = [(b_, t_) for b_ in range(2) for t_ in range(4)]
        v_z2cm = load_z2cm(*chunks[0])
        for ci, (b, tc4) in enumerate(chunks):
            if True:
                base_tok = b * TB + 512 * tc4
                gbase = base_tok // 128
                wait("tensor", "dTZ", v_z2cm)
                if first_w2_wait:
                    wait("tensor", "dCW2", v_w2b)
                    first_w2_wait = False
                hv, t0v, rfv, gbv = {}, {}, {}, {}

                def emit_h(ft):
                    slot = ft % 3
                    off = 512 * slot
                    wait_drain("tensor", ("h", slot))
                    v = 0
                    for j in range(ND):
                        v = inc("tensor", "cPE", 1,
                                lambda t, j=j, ft=ft, off=off:
                                t.matmul(pf[:, off:off + 512], w1b[:, j, 128 * ft:128 * (ft + 1)],
                                         z2cm[:, j, :],
                                         start=(j == 0), stop=(j == ND - 1)))
                    hv[ft] = v

                def emit_t0(ft):
                    slot = ft % 3
                    off = 512 * slot
                    Fv = HS[ft % 2]["f"]
                    wait("scalar", "cPE", hv[ft])
                    wait_drain("scalar", ("chain", ft % 2))
                    vt0 = inc("scalar", "cAC", 1,
                              lambda sc, off=off, ft=ft, Fv=Fv: sc.activation(
                                  out=Fv[0][:], in_=pf[:, off:off + 512], func=AF.Identity,
                                  bias=b1adj[:, ft:ft + 1], scale=256.0))
                    add_drain(("h", slot), "cAC", vt0)
                    t0v[ft] = vt0

                def emit_stage1(ft):
                    S_ = HS[ft % 2]
                    Fv, Iv = S_["f"], S_["i"]
                    wait("vector", "cAC", t0v[ft])
                    wait_drain("vector", ("chain", ft % 2))
                    dvv1 = inc("vector", "cDV", 1,
                               lambda v_, Iv=Iv, Fv=Fv: v_.tensor_copy(out=Iv[1][:], in_=Fv[0][:]))
                    wait("vector", "cDV", dvv1)
                    inc("vector", "cDV", 1,
                        lambda v_, Iv=Iv: v_.tensor_scalar(out=Iv[2][:], in0=Iv[1][:], scalar1=mA[:],
                                                           scalar2=None, op0=ALU.bitwise_and))
                    dvr = inc("vector", "cDV", 1,
                              lambda v_, Iv=Iv: v_.tensor_scalar(out=Iv[5][:], in0=Iv[1][:], scalar1=mR[:],
                                                                 scalar2=None, op0=ALU.bitwise_and))
                    wait("vector", "cDV", dvr)
                    rfv[ft] = inc("vector", "cDV", 1,
                                  lambda v_, Iv=Iv, Fv=Fv: v_.tensor_scalar(out=Fv[3][:], in0=Iv[5][:],
                                                                            scalar1=4.0, scalar2=None,
                                                                            op0=ALU.mult))

                def emit_gagb(ft):
                    S_ = HS[ft % 2]
                    Fv, Iv = S_["f"], S_["i"]
                    wait("scalar", "cDV", rfv[ft])
                    inc("scalar", "cAC", 1,
                        lambda sc, Iv=Iv, Fv=Fv: sc.activation(out=Fv[4][:], in_=Iv[2][:], func=AF.Gelu,
                                                               scale=float(2.0 ** -8)))
                    gbv[ft] = inc("scalar", "cAC", 1,
                                  lambda sc, Iv=Iv, Fv=Fv: sc.activation(out=Fv[5][:], in_=Iv[2][:], func=AF.Gelu,
                                                                         bias=c025[:], scale=float(2.0 ** -8)))

                def emit_tail(ft):
                    S_ = HS[ft % 2]
                    Fv, Iv = S_["f"], S_["i"]
                    wait("vector", "cAC", gbv[ft])
                    dve1 = inc("vector", "cDV", 1,
                               lambda v_, Fv=Fv: v_.tensor_tensor(out=Fv[1][:], in0=Fv[5][:], in1=Fv[4][:], op=ALU.subtract))
                    wait("vector", "cDV", dve1)
                    dve2 = inc("vector", "cDV", 1,
                               lambda v_, Fv=Fv: v_.tensor_tensor(out=Fv[2][:], in0=Fv[3][:], in1=Fv[1][:], op=ALU.mult))
                    wait("vector", "cDV", dve2)
                    dve3 = inc("vector", "cDV", 1,
                               lambda v_, Fv=Fv: v_.scalar_tensor_tensor(out=Fv[0][:], in0=Fv[4][:], scalar=256.0,
                                                                         in1=Fv[2][:], op0=ALU.mult, op1=ALU.add))
                    wait("vector", "cDV", dve3)
                    dvy1 = inc("vector", "cDV", 1,
                               lambda v_, Iv=Iv, Fv=Fv: v_.tensor_copy(out=Iv[1][:], in_=Fv[0][:]))
                    wait("vector", "cDV", dvy1)
                    wait_drain("vector", ("ytr", ft))
                    dvq = inc("vector", "cDV", 1,
                              lambda v_, ft=ft, Iv=Iv: v_.tensor_scalar(out=yt[:, ft, :], in0=Iv[1][:],
                                                                        scalar1=1.0, scalar2=None, op0=ALU.mult))
                    fill[("yt", ft)] = dvq
                    drain[("chain", ft % 2)] = [("cDV", dvq)]

                def w2_sl0_partial(ftw):
                    wait("tensor", "cDV", fill[("yt", ftw)])
                    for tt in range(4):
                        off2 = 512 * (3 + tt)
                        if ftw == 0:
                            wait_drain("tensor", ("o", tt))
                        inc("tensor", "cPE", 1,
                            lambda t, ftw=ftw, tt=tt, off2=off2:
                            t.matmul(pf[:, off2:off2 + 512], yt[:, ftw, 128 * tt:128 * (tt + 1)],
                                     w2b[:, ftw, 0:512],
                                     start=(ftw == 0), stop=False))

                for ft in range(NF + 1):
                    if ft < NF:
                        emit_h(ft)
                    if ft >= 1:
                        emit_gagb(ft - 1)
                    if ft < NF:
                        emit_t0(ft)
                    if ft >= 1:
                        emit_tail(ft - 1)
                    if ft < NF:
                        emit_stage1(ft)
                    if ft >= 5:
                        w2_sl0_partial(ft - 5)
                for ftw in range(NF - 4, NF):
                    w2_sl0_partial(ftw)
                v_h_last = hv[NF - 1]
                drain[("z2cm",)] = [("cPE", v_h_last)]
                if ci + 1 < len(chunks):
                    v_z2cm = load_z2cm(*chunks[ci + 1])
                for sl in range(2):
                    if sl == 1:
                        v_by_tt = [0, 0, 0, 0]
                        for ft in range(NF):
                            wait("tensor", "cDV", fill[("yt", ft)])
                            for tt in range(4):
                                off = 512 * (3 + tt)
                                if ft == 0:
                                    wait_drain("tensor", ("o", tt))
                                v = inc("tensor", "cPE", 1,
                                        lambda t, ft=ft, tt=tt, off=off:
                                        t.matmul(pf[:, off:off + 512], yt[:, ft, 128 * tt:128 * (tt + 1)],
                                                 w2b[:, ft, 512:1024],
                                                 start=(ft == 0), stop=False))
                                v_by_tt[tt] = v
                            drain[("ytr", ft)] = [("cPE", v_by_tt[3])]
                    for tt in range(4):
                        off = 512 * (3 + tt)
                        v = inc("tensor", "cPE", 1,
                                lambda t, tt=tt, sl=sl, off=off:
                                t.matmul(pf[:, off:off + 512], ones16r[0:1, :],
                                         b2row16[0:1, 512 * sl:512 * (sl + 1)],
                                         start=False, stop=True))
                        fill[("o", tt)] = v
                    for tt in range(4):
                        g2 = gbase + tt
                        off = 512 * (3 + tt)
                        wait_drain("sync", ("x1b",))
                        wait("sync", "dX1S0", vx1end[b][0])
                        wait("sync", "dX1S1", vx1end[b][1])
                        vx1 = inc("sync", "dX1L", 16,
                                  lambda s, g2=g2: s.dma_start(out=x1b[:],
                                                               in_=x1scr[128 * g2:128 * (g2 + 1), :]))
                        wait("vector", "cPE", fill[("o", tt)])
                        wait_drain("vector", ("fchain",))
                        dvu = inc("vector", "cDV", 1,
                                  lambda v_, off=off: v_.tensor_copy(out=fY1I[:], in_=pf[:, off:off + 512]))
                        add_drain(("o", tt), "cDV", dvu)
                        wait("vector", "cDV", dvu)
                        wait("vector", "dX1L", vx1)
                        dvo = inc("vector", "cDV", 1,
                                  lambda v_, sl=sl: v_.scalar_tensor_tensor(
                                      out=fX2[:], in0=fY1I[:], scalar=float(2.0 ** -8),
                                      in1=x1b[:, 512 * sl:512 * (sl + 1)], op0=ALU.mult, op1=ALU.add))
                        drain[("x1b",)] = [("cDV", dvo)]
                        drain[("fchain",)] = [("cDV", dvo)]
                        wait("scalar", "cDV", dvo)
                        vout = inc("scalar", "dOUT", 16,
                                   lambda sc, g2=g2, sl=sl: sc.dma_start(
                                       out=ys[128 * g2:128 * (g2 + 1), 512 * sl:512 * (sl + 1)],
                                       in_=fX2[:]))
                        app_drain(("fchain",), "dOUT", vout)
                drain[("z2cm",)] = [("cPE", v_h_last)]

        wait("scalar", "dOUT", cnt["dOUT"])
        wait("sync", "dOUT", cnt["dOUT"])

        blk = E(nc.Block())
        blk.sync(lambda eng: [f(eng) for f in ops["sync"]])
        blk.tensor(lambda eng: [f(eng) for f in ops["tensor"]])
        blk.vector(lambda eng: [f(eng) for f in ops["vector"]])
        blk.scalar(lambda eng: [f(eng) for f in ops["scalar"]])
        blk.gpsimd(lambda eng: [f(eng) for f in ops["gpsimd"]])

    return nc


_CACHE = {}


def kernel(L0_attn_x, L0_attn_norm_w, L0_attn_norm_b, L0_attn_qkv_w, L0_attn_qkv_b,
           L0_attn_proj_w, L0_attn_proj_b, L0_ffn_w1, L0_ffn_b1, L0_ffn_w2, L0_ffn_b2):
    x = np.ascontiguousarray(np.asarray(L0_attn_x, dtype=np.float32))
    com = dict(
        qkv_w=np.ascontiguousarray(np.asarray(L0_attn_qkv_w, dtype=np.float32)),
        qkv_b=np.ascontiguousarray(np.asarray(L0_attn_qkv_b, dtype=np.float32)),
        norm_w=np.ascontiguousarray(np.asarray(L0_attn_norm_w, dtype=np.float32)),
        norm_b=np.ascontiguousarray(np.asarray(L0_attn_norm_b, dtype=np.float32)),
        proj_w=np.ascontiguousarray(np.asarray(L0_attn_proj_w, dtype=np.float32)),
        proj_b=np.ascontiguousarray(np.asarray(L0_attn_proj_b, dtype=np.float32)),
        w1=np.ascontiguousarray(np.asarray(L0_ffn_w1, dtype=np.float32)),
        b1=np.ascontiguousarray(np.asarray(L0_ffn_b1, dtype=np.float32)),
        w2=np.ascontiguousarray(np.asarray(L0_ffn_w2, dtype=np.float32)),
        b2=np.ascontiguousarray(np.asarray(L0_ffn_b2, dtype=np.float32)),
        identf=np.eye(128, dtype=np.float32),
        ident16=np.eye(128, dtype=ml_dtypes.bfloat16),
    )
    if "nc" not in _CACHE:
        _CACHE["nc"] = build_program()
    nc = _CACHE["nc"]
    in_maps = []
    for c in range(8):
        m = dict(com)
        m["xs"] = np.ascontiguousarray(x[2 * c:2 * c + 2].reshape(T, D))
        in_maps.append(m)
    res = run_bass_kernel_spmd(nc, in_maps, core_ids=list(range(8)))
    out = np.empty((16, TB, D), dtype=np.float32)
    for c in range(8):
        out[2 * c:2 * c + 2] = np.asarray(res.results[c]["ys"]).reshape(2, TB, D)
    return out


# version 17
# speedup vs baseline: 1.2800x; 1.2800x over previous
import contextlib

import numpy as np
import ml_dtypes

import concourse.bass as bass
import concourse.mybir as mybir
from concourse.bass_utils import run_bass_kernel_spmd

F32 = mybir.dt.float32
BF16 = mybir.dt.bfloat16
I32 = mybir.dt.int32
AF = mybir.ActivationFunctionType
ALU = mybir.AluOpType
AX = mybir.AxisListType

T = 4096
TB = 2048
NTB = 16
NT = 32
D = 1024
ND = 8
FF = 4096
NF = 32
SCALE = float(np.float32(1.0) / np.float32(np.sqrt(np.float32(2048.0))))
EPS = 1e-5

BASE = 16448
CAP = 229344
OFF_IDENTF = BASE + 0
OFF_IDENT16 = BASE + 512
OFF_PTH0 = BASE + 768
OFF_CONS = BASE + 2816
OFF_STAT = BASE + 3456
OFF_NW8 = BASE + 5248
OFF_NB16 = BASE + 5280
OFF_NW16 = BASE + 5312
OFF_ONES16R = BASE + 5344
OFF_FL16 = BASE + 5632
OFF_FR16 = BASE + 9728
OFF_PROJB16 = BASE + 15872
PWL_SET0 = [BASE + 5632 + 2048 * i for i in range(6)]
OFF_Q = BASE + 17920
OFF_K = BASE + 50688
OFF_W1B = BASE + 17920
OFF_B16STG = BASE + 17920
OFF_ROWA = BASE + 26112
OFF_ROWB = BASE + 42496
OFF_W1Q = BASE + 83456
OFF_XCM = BASE + 116224
OFF_W2B = BASE + 83456
OFF_PWT = BASE + 148992
OFF_VCM = BASE + 165376
OFF_XST0 = OFF_VCM + 0
OFF_XST1 = OFF_VCM + 4096
OFF_X1T0 = OFF_VCM + 8192
OFF_X1T1 = OFF_VCM + 12288
OFF_SQSCR = OFF_VCM + 16384
OFF_Z2T0 = OFF_VCM + 20480
OFF_Z2T1 = OFF_VCM + 22528
OFF_BSCR0 = OFF_VCM + 24576
OFF_BSCR1 = OFF_VCM + 28672
OFF_WT = BASE + 198144
OFF_WN = BASE + 200192
OFF_Z2CM = BASE + 148992
OFF_YT = BASE + 165376
OFF_X1B = BASE + 198144
PWL_SET1 = [BASE + 157184 + 2048 * i for i in range(4)] + [BASE + 202240, BASE + 204288]
OFF_FT4 = BASE + 206336
OFF_FY1I = BASE + 208384
OFF_B2ROW16 = BASE + 210432
assert BASE + 212480 + 256 <= CAP

DTB = {F32: 4, BF16: 2, I32: 4}


def build_program():
    nc = bass.Bass("TRN2", target_bir_lowering=False)

    xs = nc.dram_tensor("xs", [T, D], F32, kind="ExternalInput").ap()
    qkv_w = nc.dram_tensor("qkv_w", [3 * D, D], F32, kind="ExternalInput").ap()
    qkv_b = nc.dram_tensor("qkv_b", [3 * D], F32, kind="ExternalInput").ap()
    norm_w = nc.dram_tensor("norm_w", [D], F32, kind="ExternalInput").ap()
    norm_b = nc.dram_tensor("norm_b", [D], F32, kind="ExternalInput").ap()
    proj_w = nc.dram_tensor("proj_w", [D, D], F32, kind="ExternalInput").ap()
    proj_b = nc.dram_tensor("proj_b", [D], F32, kind="ExternalInput").ap()
    w1 = nc.dram_tensor("w1", [D, FF], F32, kind="ExternalInput").ap()
    b1 = nc.dram_tensor("b1", [FF], F32, kind="ExternalInput").ap()
    w2 = nc.dram_tensor("w2", [FF, D], F32, kind="ExternalInput").ap()
    b2 = nc.dram_tensor("b2", [D], F32, kind="ExternalInput").ap()
    identf_in = nc.dram_tensor("identf", [128, 128], F32, kind="ExternalInput").ap()
    ident16_in = nc.dram_tensor("ident16", [128, 128], BF16, kind="ExternalInput").ap()
    ys = nc.dram_tensor("ys", [T, D], F32, kind="ExternalOutput").ap()

    x16scr = nc.dram_tensor("x16scr", [T, D], BF16, kind="Internal").ap()
    qw16scr = nc.dram_tensor("qw16scr", [3 * D, D], BF16, kind="Internal").ap()
    pw16scr = nc.dram_tensor("pw16scr", [D, D], BF16, kind="Internal").ap()
    x1scr = nc.dram_tensor("x1scr", [T, D], F32, kind="Internal").ap()
    z2scr = nc.dram_tensor("z2scr", [T, D], BF16, kind="Internal").ap()
    rowscr = nc.dram_tensor("rowscr", [2, T], F32, kind="Internal").ap()
    r16scr = nc.dram_tensor("r16scr", [4, T], BF16, kind="Internal").ap()

    with contextlib.ExitStack() as stack:
        E = stack.enter_context

        def sb(name, shape, dtype, off):
            need = 1
            for s_ in shape[1:]:
                need *= s_
            need *= DTB[dtype]
            assert off + need <= CAP, (name, off, need)
            return nc.alloc_sbuf_tensor_at(name, list(shape), dtype, offset=off)

        identf = sb("identf", [128, 128], F32, OFF_IDENTF)
        ident16 = sb("ident16", [128, 128], BF16, OFF_IDENT16)
        pth16 = sb("pth16", [128, 1024], BF16, OFF_PTH0)
        strm = sb("strm", [128, 64], F32, BASE + 212480)
        nwb = sb("nwb", [128, 16], BF16, OFF_PTH0 + 512)
        ST = {}
        for i, n in enumerate(["mu", "mun", "s1", "s", "rs", "a1", "a2", "m2n", "s2",
                               "tmpA", "tmpB", "ssq", "b1adj", "stx"]):
            ST[n] = sb("st_" + n, [128, 32], F32, OFF_STAT + 128 * i)
        b1adj = ST["b1adj"]
        nw8 = sb("nw8", [128, 8], F32, OFF_NW8)
        nb16c = sb("nb16c", [128, 8], BF16, OFF_NB16)
        nw16c = sb("nw16c", [128, 8], BF16, OFF_NW16)
        ones16r = sb("ones16r", [1, 128], BF16, OFF_ONES16R)
        fl16 = sb("fl16", [2, 2048], BF16, OFF_FL16)
        fr16 = sb("fr16", [2, 3072], BF16, OFF_FR16)
        projb16 = sb("projb16", [1, 1024], BF16, OFF_PROJB16)
        b2row16 = sb("b2row16", [1, 1024], BF16, OFF_B2ROW16)
        Q = sb("Q", [128, 16, 1024], BF16, OFF_Q)
        Kt = sb("K", [128, 16, 1024], BF16, OFF_K)
        w1b = sb("w1b", [128, 8, 4096], BF16, OFF_W1B)
        b16stg = sb("b16stg", [1, 4096], BF16, OFF_B16STG)
        rowA = sb("rowA", [1, 4096], F32, OFF_ROWA)
        rowB = sb("rowB", [1, 4096], F32, OFF_ROWB)
        W1q = sb("W1q", [128, 8, 2048], BF16, OFF_W1Q)
        W1v = sb("W1v", [128, 8, 1024], BF16, OFF_W1Q)
        xcm = sb("xcm", [128, 8, 2048], BF16, OFF_XCM)
        acm = sb("acm", [128, 8, 2048], BF16, OFF_XCM)
        w2b = sb("w2b", [128, 32, 1024], BF16, OFF_W2B)
        pwT = sb("pwT", [128, 8, 1024], BF16, OFF_PWT)
        vcm = sb("vcm", [128, 8, 2048], BF16, OFF_VCM)
        xst = [sb("xst0", [128, 1024], F32, OFF_XST0), sb("xst1", [128, 1024], F32, OFF_XST1)]
        x1t = [sb("x1t0", [128, 1024], F32, OFF_X1T0), sb("x1t1", [128, 1024], F32, OFF_X1T1)]
        sqscr = sb("sqscr", [128, 1024], F32, OFF_SQSCR)
        z2t = [sb("z2t0", [128, 1024], BF16, OFF_Z2T0), sb("z2t1", [128, 1024], BF16, OFF_Z2T1)]
        bscr = [sb("bscr0", [128, 2048], BF16, OFF_BSCR0), sb("bscr1", [128, 2048], BF16, OFF_BSCR1)]
        Wt = sb("Wt", [128, 1024], BF16, OFF_WT)
        Wn = sb("Wn", [128, 1024], BF16, OFF_WN)
        vtm = sb("vtm", [128, 1024], BF16, OFF_WN)
        z2cm = sb("z2cm", [128, 8, 512], BF16, OFF_Z2CM)
        yt = sb("yt", [128, 32, 512], BF16, OFF_YT)
        x1b = sb("x1b", [128, 1024], F32, OFF_X1B)

        def con(name, dtype, idx):
            return nc.alloc_sbuf_tensor_at(name, [128, 1], dtype, offset=OFF_CONS + 32 * idx)
        mA = con("mA", I32, 0)
        mR = con("mR", I32, 1)
        c16k = con("c16k", F32, 2)
        ceps = con("ceps", F32, 3)
        c025 = con("c025", F32, 4)
        onesf1 = con("onesf1", F32, 6)
        e0row = nc.alloc_sbuf_tensor_at("e0row", [1, 2], BF16, offset=OFF_CONS + 32 * 7)
        mxc = [con("mxc0", F32, 8), con("mxc1", F32, 9)]
        mbc = [con("mbc0", F32, 10), con("mbc1", F32, 11)]
        su1 = [con("su10", F32, 12), con("su11", F32, 13)]
        su2 = [con("su20", F32, 14), con("su21", F32, 15)]
        suc = [con("suc0", F32, 16), con("suc1", F32, 17)]
        rcc = [con("rcc0", F32, 18), con("rcc1", F32, 19)]

        SETO = [PWL_SET0, PWL_SET1]
        HS = []
        for s_i in range(2):
            fv = [nc.alloc_sbuf_tensor_at(f"hs{s_i}f{k}", [128, 512], F32, offset=SETO[s_i][k]) for k in range(6)]
            iv = {k: nc.alloc_sbuf_tensor_at(f"hs{s_i}i{k}", [128, 512], I32, offset=SETO[s_i][k]) for k in (1, 2, 5)}
            HS.append({"f": fv, "i": iv})
        fT4 = nc.alloc_sbuf_tensor_at("fT4", [128, 512], F32, offset=OFF_FT4)
        fY1I = nc.alloc_sbuf_tensor_at("fY1I", [128, 512], I32, offset=OFF_FY1I)
        fC2 = nc.alloc_sbuf_tensor_at("fC2", [128, 512], F32, offset=OFF_FY1I)
        fYF1 = nc.alloc_sbuf_tensor_at("fYF1", [128, 512], F32, offset=OFF_PTH0)
        fX2 = nc.alloc_sbuf_tensor_at("fX2", [128, 512], F32, offset=OFF_PTH0)

        pf = E(nc.psum_tensor("pf", [128, 3584], F32))
        pb = E(nc.psum_tensor("pb", [128, 1024], BF16))

        sem_names = ["dXF0", "dXF1", "dW", "dWq", "dTRa", "dTRb", "dTRp",
                     "dTB0", "dTB1", "dTZ", "dCX0", "dCX1", "dCQ", "dCP",
                     "dCW", "dCW2", "dZ20", "dZ21", "dX1S0", "dX1S1",
                     "dX1L", "dOUT", "cPE", "cDV", "cAC", "cPL"]
        sems = {n: E(nc.semaphore(n)) for n in sem_names}
        cnt = {n: 0 for n in sem_names}
        ops = {e: [] for e in ("sync", "tensor", "vector", "scalar", "gpsimd")}

        def emit(e, fn):
            ops[e].append(fn)

        def inc(e, sem, n, build):
            cnt[sem] += n
            emit(e, lambda eng, b_=build, s_=sems[sem], k_=n: b_(eng).then_inc(s_, k_))
            return cnt[sem]

        def wait(e, sem, v):
            if v > 0:
                emit(e, lambda eng, s_=sems[sem], v_=v: eng.wait_ge(s_, v_))

        fill = {}
        drain = {}

        def add_drain(key, sem, v):
            drain[key] = [(sem, v)]

        def app_drain(key, sem, v):
            drain.setdefault(key, []).append((sem, v))

        def wait_drain(e, key):
            for s_, v_ in drain.get(key, ()):
                wait(e, s_, v_)

        def barrier(e):
            for s_ in ("cDV", "cAC", "cPL"):
                wait(e, s_, cnt[s_])

        def vsync_op(build):
            v = inc("vector", "cDV", 1, build)
            wait("vector", "cDV", v)
            return v

        for f in (lambda v: v.memset(mA[:], -64),
                  lambda v: v.memset(mR[:], 63),
                  lambda v: v.memset(c16k[:], 16384.0),
                  lambda v: v.memset(ceps[:], float(EPS)),
                  lambda v: v.memset(c025[:], 0.25),
                  lambda v: v.memset(onesf1[:], 1.0)):
            emit("vector", f)
        v_memset = inc("vector", "cDV", 1, lambda v: v.memset(ones16r[:], 1.0))

        vx16 = [0, 0]
        for b in range(2):
            vx16[b] = inc("gpsimd", f"dCX{b}", 16,
                          lambda g, b=b: g.dma_start(out=x16scr[b * TB:(b + 1) * TB, :],
                                                     in_=xs[b * TB:(b + 1) * TB, :]))
        vqw16 = inc("gpsimd", "dCQ", 16, lambda g: g.dma_start(out=qw16scr, in_=qkv_w))
        vpw16 = inc("gpsimd", "dCP", 16, lambda g: g.dma_start(out=pw16scr, in_=proj_w))

        videt = inc("sync", "dW", 16, lambda s: s.dma_start(out=identf[:], in_=identf_in))
        videt16 = inc("sync", "dW", 16, lambda s: s.dma_start(out=ident16[:], in_=ident16_in))
        vnw = inc("sync", "dW", 16, lambda s: s.dma_start(out=rowA[:1, 0:1024], in_=norm_w[None, :]))
        vnb = inc("sync", "dW", 16, lambda s: s.dma_start(out=rowA[:1, 1024:2048], in_=norm_b[None, :]))
        vpbld = inc("sync", "dW", 16, lambda s: s.dma_start(out=rowA[:1, 2048:3072], in_=proj_b[None, :]))
        vb2ld = inc("sync", "dW", 16, lambda s: s.dma_start(out=rowA[:1, 3072:4096], in_=b2[None, :]))
        vb1ld = inc("sync", "dW", 16, lambda s: s.dma_start(out=rowB[:1, :], in_=b1[None, :]))
        v_dW_all = cnt["dW"]

        wait("sync", "dCQ", vqw16)

        def load_W1q():
            vv = 0
            for j in range(ND):
                vv = inc("sync", "dTRa", 16,
                         lambda s, j=j: s.dma_start(out=W1q[:, j, :],
                                                    in_=qw16scr[0:2048, 128 * j:128 * (j + 1)],
                                                    transpose=True))
            return vv

        def load_W1v():
            vv = 0
            for j in range(ND):
                vv = inc("sync", "dTRb", 16,
                         lambda s, j=j: s.dma_start(out=W1v[:, j, :],
                                                    in_=qw16scr[2048:3072, 128 * j:128 * (j + 1)],
                                                    transpose=True))
            return vv

        vW1 = load_W1q()
        wait("sync", "dCP", vpw16)
        for j in range(ND):
            inc("sync", "dTRp", 16,
                lambda s, j=j: s.dma_start(out=pwT[:, j, :], in_=pw16scr[:, 128 * j:128 * (j + 1)],
                                           transpose=True))
        v_pwT = cnt["dTRp"]

        wait("tensor", "dW", v_dW_all)
        wait("tensor", "cDV", v_memset)

        def columnize(src_row_fn, evac_fn, n):
            for c in range(n):
                slot = c % 4
                key = ("bank6",)
                wait_drain("tensor", key)
                v = inc("tensor", "cPE", 1,
                        lambda t, c=c, slot=slot: t.matmul(pf[:, 3072 + 128 * slot:3072 + 128 * slot + 1],
                                                           src_row_fn(c), onesf1[0:1, 0:1],
                                                           start=True, stop=True))
                wait("vector", "cPE", v)
                dv = evac_fn(c, slot)
                add_drain(key, "cDV", dv)

        columnize(lambda c: rowB[0:1, 128 * c:128 * (c + 1)],
                  lambda c, slot: inc("vector", "cDV", 1,
                                      lambda v_, c=c, slot=slot: v_.tensor_scalar(
                                          out=b1adj[:, c:c + 1],
                                          in0=pf[:, 3072 + 128 * slot:3072 + 128 * slot + 1],
                                          scalar1=256.0, scalar2=None, op0=ALU.mult)), 32)
        columnize(lambda c: rowA[0:1, 128 * c:128 * (c + 1)],
                  lambda c, slot: inc("vector", "cDV", 1,
                                      lambda v_, c=c, slot=slot: v_.tensor_copy(
                                          out=nw8[:, c:c + 1],
                                          in_=pf[:, 3072 + 128 * slot:3072 + 128 * slot + 1])), 8)
        columnize(lambda c: rowA[0:1, 1024 + 128 * c:1024 + 128 * (c + 1)],
                  lambda c, slot: inc("vector", "cDV", 1,
                                      lambda v_, c=c, slot=slot: v_.tensor_copy(
                                          out=nb16c[:, c:c + 1],
                                          in_=pf[:, 3072 + 128 * slot:3072 + 128 * slot + 1])), 8)
        vnw16 = inc("vector", "cDV", 1, lambda v_: v_.tensor_copy(out=nw16c[:], in_=nw8[:]))
        wait("vector", "dW", v_dW_all)
        inc("vector", "cDV", 1, lambda v_: v_.tensor_copy(out=projb16[:1, :], in_=rowA[:1, 2048:3072]))
        inc("vector", "cDV", 1, lambda v_: v_.tensor_scalar(out=b2row16[:1, :], in0=rowA[:1, 3072:4096],
                                                            scalar1=256.0, scalar2=None, op0=ALU.mult))

        wait("tensor", "dTRa", vW1)
        wait("tensor", "cDV", vnw16)
        barrier("tensor")
        wait("sync", "cPE", cnt["cPE"])
        vqbld = inc("sync", "dWq", 16, lambda s: s.dma_start(out=rowB[:1, 0:3072], in_=qkv_b[None, :]))
        v_e0 = inc("vector", "cDV", 1, lambda v_: v_.memset(e0row[:], 0.0))
        wait("vector", "cDV", v_e0)
        emit("vector", lambda v_: v_.tensor_scalar(out=e0row[0:1, 0:1], in0=onesf1[0:1, 0:1],
                                                   scalar1=1.0, scalar2=None, op0=ALU.mult))
        emit("vector", lambda v_: v_.tensor_copy(out=nwb[:, 0:16:2], in_=nb16c[:]))
        emit("vector", lambda v_: v_.tensor_copy(out=nwb[:, 1:16:2], in_=nw16c[:]))
        wait("vector", "dWq", vqbld)
        dvqb = inc("vector", "cDV", 1, lambda v_: v_.tensor_copy(out=b16stg[:1, 0:3072], in_=rowB[:1, 0:3072]))
        wait("tensor", "cDV", dvqb)

        def wsum_bias_slices(wt_tensor, col0, sls):
            for sl in sls:
                rel = 512 * sl - col0
                key = ("bank6",)
                wait_drain("tensor", key)
                v = 0
                for j in range(ND):
                    v = inc("tensor", "cPE", 1,
                            lambda t, j=j, rel=rel, wt=wt_tensor: t.matmul(
                                pf[0:2, 3072:3584], nwb[:, 2 * j:2 * j + 2],
                                wt[:, j, rel:rel + 512], start=(j == 0), stop=False))
                v = inc("tensor", "cPE", 1,
                        lambda t, sl=sl: t.matmul(pf[0:2, 3072:3584], e0row[0:1, :],
                                                  b16stg[0:1, 512 * sl:512 * (sl + 1)],
                                                  start=False, stop=True))
                wait("vector", "cPE", v)
                dv = inc("vector", "cDV", 1,
                         lambda v_, sl=sl: v_.tensor_copy(out=fr16[0:2, 512 * sl:512 * (sl + 1)],
                                                          in_=pf[0:2, 3072:3584]))
                add_drain(key, "cDV", dv)

        wsum_bias_slices(W1q, 0, [0, 1, 2, 3])
        v_wb1_pe = cnt["cPE"]
        wait("sync", "cPE", v_wb1_pe)
        vW1v = load_W1v()
        wait("tensor", "dTRb", vW1v)
        wsum_bias_slices(W1v, 2048, [4, 5])
        v_wb2_pe = cnt["cPE"]

        ev = lambda f: emit("vector", f)
        tile_sq = [0] * NT
        tile_rd = [0] * NT
        tile_mu = [0] * NT
        for g in range(NT):
            buf = g % 2
            if g >= 2:
                wait("sync", "cAC", tile_sq[g - 2])
                wait("sync", "cDV", tile_rd[g - 2])
            vld = inc("sync", f"dXF{buf}", 16,
                      lambda s, g=g, buf=buf: s.dma_start(out=xst[buf][:], in_=xs[128 * g:128 * (g + 1), :]))
            wait("vector", f"dXF{buf}", vld)
            tile_mu[g] = inc("vector", "cDV", 1,
                             lambda v_, g=g, buf=buf: v_.reduce_sum(out=ST["mu"][:, g:g + 1], in_=xst[buf][:], axis=AX.X))
            wait("scalar", f"dXF{buf}", vld)
            if g >= 1:
                wait("scalar", "cDV", tile_rd[g - 1])
            tile_sq[g] = inc("scalar", "cAC", 1,
                             lambda sc, buf=buf: sc.activation(out=sqscr[:], in_=xst[buf][:], func=AF.Square))
            wait("vector", "cAC", tile_sq[g])
            tile_rd[g] = inc("vector", "cDV", 1,
                             lambda v_, g=g: v_.reduce_sum(out=ST["ssq"][:, g:g + 1], in_=sqscr[:], axis=AX.X))

        wait("vector", "cDV", tile_mu[NT - 1])
        wait("vector", "cDV", tile_rd[NT - 1])
        vsync_op(lambda v_: v_.tensor_scalar(out=ST["tmpA"][:], in0=ST["mu"][:], scalar1=1.0 / 1024.0, scalar2=None, op0=ALU.mult))
        vsync_op(lambda v_: v_.tensor_scalar(out=ST["tmpB"][:], in0=ST["ssq"][:], scalar1=1.0 / 1024.0, scalar2=None, op0=ALU.mult))
        vsync_op(lambda v_: v_.tensor_tensor(out=ST["stx"][:], in0=ST["tmpA"][:], in1=ST["tmpA"][:], op=ALU.mult))
        vv1 = vsync_op(lambda v_: v_.tensor_tensor(out=ST["ssq"][:], in0=ST["tmpB"][:], in1=ST["stx"][:], op=ALU.subtract))
        wait("scalar", "cDV", vv1)
        va1 = inc("scalar", "cAC", 1,
                  lambda sc: sc.activation(out=ST["tmpB"][:], in_=ST["ssq"][:], func=AF.Sqrt, bias=ceps[:], scale=1.0))
        va1b = inc("scalar", "cAC", 1,
                   lambda sc: sc.activation(out=ST["stx"][:], in_=ST["ssq"][:], func=AF.Relu, bias=ceps[:], scale=1.0))
        wait("vector", "cAC", va1b)
        vsync_op(lambda v_: v_.reciprocal(out=ST["s1"][:], in_=ST["tmpB"][:]))
        vsync_op(lambda v_: v_.tensor_scalar(out=ST["a1"][:], in0=ST["s1"][:], scalar1=1.0, scalar2=None, op0=ALU.add))
        vsync_op(lambda v_: v_.scalar_tensor_tensor(out=ST["a2"][:], in0=ST["tmpA"][:], scalar=-1.0, in1=ST["s1"][:],
                                                    op0=ALU.mult, op1=ALU.mult))
        vsync_op(lambda v_: v_.tensor_scalar(out=ST["mun"][:], in0=ST["tmpA"][:], scalar1=-1.0, scalar2=None, op0=ALU.mult))
        vsync_op(lambda v_: v_.reciprocal(out=ST["tmpA"][:], in_=ST["stx"][:]))
        vv2 = vsync_op(lambda v_: v_.tensor_tensor(out=ST["stx"][:], in0=ST["ssq"][:], in1=ST["tmpA"][:], op=ALU.mult))
        wait("scalar", "cDV", vv2)
        va2 = inc("scalar", "cAC", 1,
                  lambda sc: sc.activation(out=ST["tmpA"][:], in_=ST["stx"][:], func=AF.Sqrt, bias=ceps[:], scale=1.0))
        wait("vector", "cAC", va2)
        vsync_op(lambda v_: v_.tensor_tensor(out=ST["rs"][:], in0=ST["tmpB"][:], in1=ST["tmpA"][:], op=ALU.mult))
        vsync_op(lambda v_: v_.reciprocal(out=ST["stx"][:], in_=ST["tmpA"][:]))
        vsync_op(lambda v_: v_.tensor_tensor(out=ST["s"][:], in0=ST["s1"][:], in1=ST["stx"][:], op=ALU.mult))
        vstat = inc("vector", "cDV", 1, lambda v_: v_.engine_nop())

        wait("tensor", "cDV", vstat)
        wait("tensor", "dW", v_dW_all)
        ev(lambda v_: v_.tensor_copy(out=strm[:, 0:64:2], in_=ST["rs"][:]))
        v_strm = inc("vector", "cDV", 1, lambda v_: v_.tensor_copy(out=strm[:, 1:64:2], in_=ST["mun"][:]))

        v_SlastPE = [0, 0]
        v_projpe = [0, 0]
        v_alast = [0, 0]
        v_z2st = [0, 0]
        v_x1st = [0, 0]
        vz2end = [None, None]
        vx1end = [None, None]
        qk_regions = [0, 1024, 2048]
        v_lastW1pe = v_wb2_pe
        for b in range(2):
            T0 = b * TB
            wait("tensor", "cDV", v_strm)
            if b == 1:
                barrier("tensor")
            for c in range(NTB):
                g = NTB * b + c
                key = ("bank6",)
                wait_drain("tensor", key)
                v = inc("tensor", "cPE", 1,
                        lambda t, g=g: t.matmul(pf[0:2, 3072:3200], strm[:, 2 * g:2 * g + 2],
                                                identf[:], start=True, stop=True))
                wait("vector", "cPE", v)
                if b == 1 and c == 0:
                    wait("vector", "cPE", v_projpe[0])
                dv = inc("vector", "cDV", 1,
                         lambda v_, c=c: v_.tensor_copy(out=fl16[0:2, 128 * c:128 * (c + 1)],
                                                        in_=pf[0:2, 3072:3200]))
                add_drain(key, "cDV", dv)
            v_fl_dv = cnt["cDV"]
            wait("sync", "cPE", v_lastW1pe)
            vW1 = load_W1q()
            wait("sync", f"dCX{b}", vx16[b])
            if b == 1:
                wait("sync", "cPE", v_alast[0])
            v_xcm = 0
            for j in range(ND):
                sbf = j % 2
                wait_drain("sync", ("bscr", sbf))
                vld = inc("sync", f"dTB{sbf}", 16,
                          lambda s, j=j, sbf=sbf, T0=T0: s.dma_start(out=bscr[sbf][:],
                                                                     in_=x16scr[T0:T0 + TB, 128 * j:128 * (j + 1)],
                                                                     transpose=True))
                wait("vector", f"dTB{sbf}", vld)
                v_xcm = inc("vector", "cDV", 1,
                            lambda v_, j=j, sbf=sbf: v_.tensor_scalar(out=xcm[:, j, :], in0=bscr[sbf][:],
                                                                      scalar1=nw8[:, j:j + 1], scalar2=None,
                                                                      op0=ALU.mult))
                add_drain(("bscr", sbf), "cDV", v_xcm)

            wait("tensor", "cDV", v_xcm)
            wait("tensor", "cDV", v_fl_dv)
            wait("tensor", "dTRa", vW1)
            barrier("tensor")
            for m in range(NTB):
                for half in range(2):
                    off = qk_regions[(2 * m + half) % 3]
                    key = ("qk", off)
                    wait_drain("tensor", key)
                    v = 0
                    for j in range(ND):
                        for s2_ in range(2):
                            v = inc("tensor", "cPE", 1,
                                    lambda t, j=j, s2_=s2_, m=m, half=half, off=off:
                                    t.matmul(pf[:, off + 512 * s2_:off + 512 * (s2_ + 1)],
                                             xcm[:, j, 128 * m:128 * (m + 1)],
                                             W1q[:, j, 1024 * half + 512 * s2_:1024 * half + 512 * (s2_ + 1)],
                                             start=(j == 0), stop=False))
                    for s2_ in range(2):
                        v = inc("tensor", "cPE", 1,
                                lambda t, s2_=s2_, half=half, off=off, m=m:
                                t.matmul(pf[:, off + 512 * s2_:off + 512 * (s2_ + 1)],
                                         fl16[0:2, 128 * m:128 * (m + 1)],
                                         fr16[0:2, 1024 * half + 512 * s2_:1024 * half + 512 * (s2_ + 1)],
                                         start=False, stop=True))
                    wait("vector", "cPE", v)
                    dst_t = Q if half == 0 else Kt
                    g = NTB * b + m
                    dv = inc("vector", "cDV", 1,
                             lambda v_, dst_t=dst_t, m=m, off=off, g=g:
                             v_.tensor_scalar(out=dst_t[:, m, :], in0=pf[:, off:off + 1024],
                                              scalar1=ST["s"][:, g:g + 1], scalar2=None, op0=ALU.mult))
                    add_drain(key, "cDV", dv)
            v_qk_pe = cnt["cPE"]

            for _k in (("x1t", 0), ("x1t", 1), ("z2t", 0), ("z2t", 1), ("xst", 0), ("xst", 1)):
                wait_drain("vector", _k)
            wait("sync", "cPE", v_qk_pe)
            vW1v = load_W1v()
            wait("tensor", "dTRb", vW1v)
            for m in range(NTB):
                off = qk_regions[m % 3]
                key = ("qk", off)
                wait_drain("tensor", key)
                v = 0
                for j in range(ND):
                    for s2_ in range(2):
                        v = inc("tensor", "cPE", 1,
                                lambda t, j=j, s2_=s2_, m=m, off=off:
                                t.matmul(pf[:, off + 512 * s2_:off + 512 * (s2_ + 1)],
                                         xcm[:, j, 128 * m:128 * (m + 1)],
                                         W1v[:, j, 512 * s2_:512 * (s2_ + 1)],
                                         start=(j == 0), stop=False))
                for s2_ in range(2):
                    v = inc("tensor", "cPE", 1,
                            lambda t, s2_=s2_, off=off, m=m:
                            t.matmul(pf[:, off + 512 * s2_:off + 512 * (s2_ + 1)],
                                     fl16[0:2, 128 * m:128 * (m + 1)],
                                     fr16[0:2, 2048 + 512 * s2_:2048 + 512 * (s2_ + 1)],
                                     start=False, stop=True))
                wait("vector", "cPE", v)
                wait_drain("vector", ("vtm",))
                dv = inc("vector", "cDV", 1,
                         lambda v_, off=off: v_.tensor_copy(out=vtm[:], in_=pf[:, off:off + 1024]))
                add_drain(key, "cDV", dv)
                wait("tensor", "cDV", dv)
                wait_drain("tensor", ("pbV",))
                vtr = 0
                for kc in range(8):
                    vtr = inc("tensor", "cPE", 1,
                              lambda t, kc=kc: t.transpose(pb[:, 128 * kc:128 * (kc + 1)],
                                                           vtm[:, 128 * kc:128 * (kc + 1)], ident16[:]))
                add_drain(("vtm",), "cPE", vtr)
                wait("vector", "cPE", vtr)
                dvv = 0
                for kc in range(8):
                    dvv = inc("vector", "cDV", 1,
                              lambda v_, kc=kc, m=m: v_.tensor_copy(out=vcm[:, kc, 128 * m:128 * (m + 1)],
                                                                    in_=pb[:, 128 * kc:128 * (kc + 1)]))
                add_drain(("pbV",), "cDV", dvv)
            v_lastW1pe = cnt["cPE"]
            v_vtm_pe = cnt["cPE"]
            v_vcm_dv = cnt["cDV"]

            wait("tensor", "cDV", v_vcm_dv)
            wait("tensor", "dW", v_dW_all)
            barrier("tensor")
            wait("vector", "cPE", v_vtm_pe)
            for qt in range(8):
                p = qt % 2
                s_off = 1024 * p
                key = ("S", p)
                wait_drain("tensor", key)
                v = 0
                for c in range(NTB):
                    for s2_ in range(2):
                        v = inc("tensor", "cPE", 1,
                                lambda t, c=c, s2_=s2_, qt=qt, s_off=s_off:
                                t.matmul(pf[:, s_off + 512 * s2_:s_off + 512 * (s2_ + 1)],
                                         Q[:, c, 128 * qt:128 * (qt + 1)],
                                         Kt[:, c, 512 * s2_:512 * (s2_ + 1)],
                                         start=(c == 0), stop=(c == NTB - 1)))
                if qt == 7:
                    v_SlastPE[b] = v
                wait("vector", "cPE", v)
                vsync_op(lambda v_, p=p, s_off=s_off: v_.reduce_max(out=mxc[p][:], in_=pf[:, s_off:s_off + 1024], axis=AX.X))
                dvm = inc("vector", "cDV", 1,
                          lambda v_, p=p: v_.tensor_scalar(out=mbc[p][:], in0=mxc[p][:], scalar1=-SCALE,
                                                           scalar2=None, op0=ALU.mult))
                wait("scalar", "cDV", dvm)
                wait_drain("scalar", ("pth",))
                va = 0
                for hh in range(2):
                    va = inc("scalar", "cAC", 1,
                             lambda sc, p=p, s_off=s_off, hh=hh:
                             sc.activation(out=pth16[:, 512 * hh:512 * (hh + 1)],
                                           in_=pf[:, s_off + 512 * hh:s_off + 512 * (hh + 1)],
                                           func=AF.Exp, bias=mbc[p][:], scale=SCALE,
                                           accum_out=(su1[p][:] if hh == 0 else su2[p][:])))
                add_drain(key, "cAC", va)
                wait("vector", "cAC", va)
                wait("gpsimd", "cAC", va)
                vsuc = inc("gpsimd", "cPL", 1,
                           lambda g_, p=p: g_.tensor_tensor(out=suc[p][:], in0=su1[p][:], in1=su2[p][:], op=ALU.add))
                wait("vector", "cPL", vsuc)
                v_rcc = vsync_op(lambda v_, p=p: v_.reciprocal(out=rcc[p][:], in_=suc[p][:]))
                wait_drain("gpsimd", ("Wn",))
                wait("gpsimd", "cDV", v_rcc)
                dvn = 0
                for hh in range(2):
                    dvn = inc("gpsimd", "cPL", 1,
                              lambda g_, p=p, hh=hh:
                              g_.tensor_scalar(out=Wn[:, 512 * hh:512 * (hh + 1)],
                                               in0=pth16[:, 512 * hh:512 * (hh + 1)],
                                               scalar1=rcc[p][:], scalar2=None, op0=ALU.mult))
                add_drain(("pth",), "cPL", dvn)
                wait("tensor", "cPL", dvn)
                wait_drain("tensor", ("Wt",))
                vtr = 0
                for kc in range(8):
                    vtr = inc("tensor", "cPE", 1,
                              lambda t, kc=kc: t.transpose(pb[:, 128 * kc:128 * (kc + 1)],
                                                           Wn[:, 128 * kc:128 * (kc + 1)], ident16[:]))
                add_drain(("Wn",), "cPE", vtr)
                wait("vector", "cPE", vtr)
                dvt = inc("vector", "cDV", 1, lambda v_: v_.tensor_copy(out=Wt[:], in_=pb[:]))
                wait("tensor", "cDV", dvt)
                va_last = 0
                for w in range(4):
                    slot = 4 + (4 * qt + w) % 3
                    off = 512 * slot
                    key2 = ("a", slot)
                    wait_drain("tensor", key2)
                    v = 0
                    for kc in range(8):
                        v = inc("tensor", "cPE", 1,
                                lambda t, kc=kc, w=w, off=off:
                                t.matmul(pf[:, off:off + 512], Wt[:, 128 * kc:128 * (kc + 1)],
                                         vcm[:, kc, 512 * w:512 * (w + 1)],
                                         start=(kc == 0), stop=(kc == 7)))
                    va_last = v
                    wait("scalar", "cPE", v)
                    vaa = inc("scalar", "cAC", 1,
                              lambda sc, qt=qt, w=w, off=off:
                              sc.activation(out=acm[:, qt, 512 * w:512 * (w + 1)], in_=pf[:, off:off + 512],
                                            func=AF.Copy, scale=1.0))
                    add_drain(key2, "cAC", vaa)
                add_drain(("Wt",), "cPE", va_last)
            v_alast[b] = cnt["cPE"]
            v_acm_done = cnt["cAC"]

            wait("sync", "cPE", v_alast[b])
            wait("vector", "cPE", v_alast[b])
            wait("tensor", "cAC", v_acm_done)
            barrier("tensor")
            barrier("scalar")
            for _sm in ("dX1S0", "dX1S1", "dZ20", "dZ21"):
                wait("scalar", _sm, cnt[_sm])
                wait("vector", _sm, cnt[_sm])
            wait("tensor", "dTRp", v_pwT)
            for m in range(NTB):
                g = NTB * b + m
                xbuf = m % 2
                wait_drain("sync", ("xst", xbuf))
                vxl = inc("sync", f"dXF{xbuf}", 16,
                          lambda s, g=g, xbuf=xbuf: s.dma_start(out=xst[xbuf][:], in_=xs[128 * g:128 * (g + 1), :]))
                pe_fill = [0, 0]
                for s2_ in range(2):
                    slot = (2 * m + s2_) % 4
                    off = 512 * slot
                    key = ("pj", slot)
                    wait_drain("tensor", key)
                    v = 0
                    for qc in range(8):
                        v = inc("tensor", "cPE", 1,
                                lambda t, qc=qc, m=m, s2_=s2_, off=off:
                                t.matmul(pf[:, off:off + 512], acm[:, qc, 128 * m:128 * (m + 1)],
                                         pwT[:, qc, 512 * s2_:512 * (s2_ + 1)],
                                         start=(qc == 0), stop=False))
                    v = inc("tensor", "cPE", 1,
                            lambda t, s2_=s2_, off=off, m=m:
                            t.matmul(pf[:, off:off + 512], fl16[0:1, 128 * m:128 * (m + 1)],
                                     projb16[0:1, 512 * s2_:512 * (s2_ + 1)],
                                     start=False, stop=True))
                    pe_fill[s2_] = v
                v_projpe[b] = pe_fill[1]
                wait("vector", f"dXF{xbuf}", vxl)
                x1buf = m % 2
                wait_drain("vector", ("x1t", x1buf))
                dvxl = inc("vector", "cDV", 1,
                           lambda v_, xbuf=xbuf, g=g: v_.tensor_scalar(out=sqscr[:], in0=xst[xbuf][:],
                                                                      scalar1=ST["a1"][:, g:g + 1],
                                                                      scalar2=ST["a2"][:, g:g + 1],
                                                                      op0=ALU.mult, op1=ALU.add))
                add_drain(("xst", xbuf), "cDV", dvxl)
                wait("vector", "cDV", dvxl)
                dvx = 0
                for s2_ in range(2):
                    slot = (2 * m + s2_) % 4
                    wait("vector", "cPE", pe_fill[s2_])
                    dvx = inc("vector", "cDV", 1,
                              lambda v_, s2_=s2_, slot=slot, g=g, x1buf=x1buf:
                              v_.scalar_tensor_tensor(out=x1t[x1buf][:, 512 * s2_:512 * (s2_ + 1)],
                                                      in0=pf[:, 512 * slot:512 * slot + 512],
                                                      scalar=ST["s"][:, g:g + 1],
                                                      in1=sqscr[:, 512 * s2_:512 * (s2_ + 1)],
                                                      op0=ALU.mult, op1=ALU.add))
                    add_drain(("pj", slot), "cDV", dvx)
                wait("scalar", "cDV", dvx)
                v_x1st[b] = inc("scalar", f"dX1S{x1buf}", 16,
                                lambda sc, g=g, x1buf=x1buf: sc.dma_start(out=x1scr[128 * g:128 * (g + 1), :],
                                                                          in_=x1t[x1buf][:]))
                vid = inc("scalar", "cAC", 1,
                          lambda sc, g=g, x1buf=x1buf: sc.activation(out=sqscr[:], in_=x1t[x1buf][:],
                                                                     func=AF.Identity,
                                                                     accum_out=ST["tmpA"][:, g:g + 1]))
                wait("scalar", "cAC", vid)
                vsq = inc("scalar", "cAC", 1,
                          lambda sc, g=g, x1buf=x1buf: sc.activation(out=sqscr[:], in_=x1t[x1buf][:],
                                                                     func=AF.Square,
                                                                     accum_out=ST["tmpB"][:, g:g + 1]))
                wait("gpsimd", "cAC", vsq)
                inc("gpsimd", "cPL", 1,
                    lambda g_, g=g: g_.tensor_scalar(out=ST["m2n"][:, g:g + 1], in0=ST["tmpA"][:, g:g + 1],
                                                     scalar1=-1.0 / 1024.0, scalar2=None, op0=ALU.mult))
                vpl2 = inc("gpsimd", "cPL", 1,
                           lambda g_, g=g: g_.tensor_scalar(out=ST["tmpB"][:, g:g + 1], in0=ST["tmpB"][:, g:g + 1],
                                                            scalar1=1.0 / 1024.0, scalar2=None, op0=ALU.mult))
                wait("vector", "cPL", vpl2)
                dvv = inc("vector", "cDV", 1,
                          lambda v_, g=g: v_.scalar_tensor_tensor(out=ST["tmpA"][:, g:g + 1],
                                                                  in0=ST["m2n"][:, g:g + 1],
                                                                  scalar=ST["m2n"][:, g:g + 1],
                                                                  in1=ST["tmpB"][:, g:g + 1],
                                                                  op0=ALU.mult, op1=ALU.subtract))
                wait("scalar", "cDV", dvv)
                vsq2 = inc("scalar", "cAC", 1,
                           lambda sc, g=g: sc.activation(out=ST["stx"][:, g:g + 1], in_=ST["tmpA"][:, g:g + 1],
                                                         func=AF.Sqrt, bias=ceps[:], scale=-1.0))
                wait("vector", "cAC", vsq2)
                v_s2 = vsync_op(lambda v_, g=g: v_.reciprocal(out=ST["s2"][:, g:g + 1], in_=ST["stx"][:, g:g + 1]))
                zbuf = m % 2
                wait_drain("gpsimd", ("z2t", zbuf))
                wait("gpsimd", "cDV", v_s2)
                dvz = inc("gpsimd", "cPL", 1,
                          lambda g_, g=g, x1buf=x1buf, zbuf=zbuf:
                          g_.tensor_scalar(out=z2t[zbuf][:], in0=x1t[x1buf][:],
                                           scalar1=ST["m2n"][:, g:g + 1], scalar2=ST["s2"][:, g:g + 1],
                                           op0=ALU.add, op1=ALU.mult))
                drain[("x1t", x1buf)] = [("cPL", dvz), (f"dX1S{x1buf}", v_x1st[b]), ("cAC", vsq)]
                wait("scalar", "cPL", dvz)
                v_z2st[b] = inc("scalar", f"dZ2{zbuf}", 16,
                                lambda sc, g=g, zbuf=zbuf: sc.dma_start(out=z2scr[128 * g:128 * (g + 1), :],
                                                                        in_=z2t[zbuf][:]))
                add_drain(("z2t", zbuf), f"dZ2{zbuf}", v_z2st[b])
            vz2end[b] = (cnt["dZ20"], cnt["dZ21"])
            vx1end[b] = (cnt["dX1S0"], cnt["dX1S1"])

        wait("gpsimd", "cPE", v_SlastPE[1])
        v_w1b = 0
        for j in range(ND):
            v_w1b = inc("gpsimd", "dCW", 16,
                        lambda g, j=j: g.dma_start(out=w1b[:, j, :], in_=w1[128 * j:128 * (j + 1), :]))
        wait("gpsimd", "cPE", v_projpe[1])
        v_w2b = 0
        for jj in range(8):
            v_w2b = inc("gpsimd", "dCW2", 16,
                        lambda g, jj=jj: g.dma_start(
                            out=w2b[:, 4 * jj:4 * (jj + 1), :],
                            in_=w2[512 * jj:512 * (jj + 1), :].rearrange("(c p) n -> p c n", p=128)))

        wait("sync", "cPE", v_projpe[1])
        wait("tensor", "dCW", v_w1b)
        barrier("tensor")
        for _sm in ("dZ20", "dZ21", "dX1S0", "dX1S1", "dXF0", "dXF1"):
            wait("vector", _sm, cnt[_sm])
        first_w2_wait = True

        def load_z2cm(b_, tc4_):
            base_tok = b_ * TB + 512 * tc4_
            wait("sync", "dZ20", vz2end[b_][0])
            wait("sync", "dZ21", vz2end[b_][1])
            wait_drain("sync", ("z2cm",))
            vz = 0
            for j in range(ND):
                vz = inc("sync", "dTZ", 16,
                         lambda s, j=j, base_tok=base_tok:
                         s.dma_start(out=z2cm[:, j, :],
                                     in_=z2scr[base_tok:base_tok + 512, 128 * j:128 * (j + 1)],
                                     transpose=True))
            return vz

        chunks = [(b_, t_) for b_ in range(2) for t_ in range(4)]
        v_z2cm = load_z2cm(*chunks[0])
        for ci, (b, tc4) in enumerate(chunks):
            if True:
                base_tok = b * TB + 512 * tc4
                gbase = base_tok // 128
                wait("tensor", "dTZ", v_z2cm)
                if first_w2_wait:
                    wait("tensor", "dCW2", v_w2b)
                    first_w2_wait = False
                hv, t0v, rfv, gbv, t3v = {}, {}, {}, {}, {}

                def emit_h(ft):
                    slot = ft % 3
                    off = 512 * slot
                    wait_drain("tensor", ("h", slot))
                    v = 0
                    for j in range(ND):
                        v = inc("tensor", "cPE", 1,
                                lambda t, j=j, ft=ft, off=off:
                                t.matmul(pf[:, off:off + 512], w1b[:, j, 128 * ft:128 * (ft + 1)],
                                         z2cm[:, j, :],
                                         start=(j == 0), stop=(j == ND - 1)))
                    hv[ft] = v

                def emit_t0(ft):
                    slot = ft % 3
                    off = 512 * slot
                    S_ = HS[ft % 2]
                    Fv = S_["f"]
                    wait("scalar", "cPE", hv[ft])
                    wait_drain("scalar", ("chain", ft % 2))
                    vt0 = inc("scalar", "cAC", 1,
                              lambda sc, off=off, ft=ft, Fv=Fv: sc.activation(
                                  out=Fv[0][:], in_=pf[:, off:off + 512], func=AF.Identity,
                                  bias=b1adj[:, ft:ft + 1], scale=256.0))
                    add_drain(("h", slot), "cAC", vt0)
                    wait("scalar", "cAC", vt0)
                    t0v[ft] = inc("scalar", "cAC", 1,
                                  lambda sc, S_=S_: sc.activation(out=S_["i"][1][:], in_=S_["f"][0][:],
                                                                  func=AF.Identity))

                def emit_stage1(ft):
                    S_ = HS[ft % 2]
                    Fv, Iv = S_["f"], S_["i"]
                    wait("vector", "cAC", t0v[ft])
                    wait_drain("vector", ("chain", ft % 2))
                    inc("vector", "cDV", 1,
                        lambda v_, Iv=Iv: v_.tensor_scalar(out=Iv[2][:], in0=Iv[1][:], scalar1=mA[:],
                                                           scalar2=None, op0=ALU.bitwise_and))
                    dvr = inc("vector", "cDV", 1,
                              lambda v_, Iv=Iv: v_.tensor_scalar(out=Iv[5][:], in0=Iv[1][:], scalar1=mR[:],
                                                                 scalar2=None, op0=ALU.bitwise_and))
                    wait("vector", "cDV", dvr)
                    rfv[ft] = inc("vector", "cDV", 1,
                                  lambda v_, Iv=Iv, Fv=Fv: v_.tensor_scalar(out=Fv[3][:], in0=Iv[5][:],
                                                                            scalar1=4.0, scalar2=None,
                                                                            op0=ALU.mult))

                def emit_gagb(ft):
                    S_ = HS[ft % 2]
                    Fv, Iv = S_["f"], S_["i"]
                    wait("scalar", "cDV", rfv[ft])
                    inc("scalar", "cAC", 1,
                        lambda sc, Iv=Iv, Fv=Fv: sc.activation(out=Fv[4][:], in_=Iv[2][:], func=AF.Gelu,
                                                               scale=float(2.0 ** -8)))
                    gbv[ft] = inc("scalar", "cAC", 1,
                                  lambda sc, Iv=Iv, Fv=Fv: sc.activation(out=Fv[5][:], in_=Iv[2][:], func=AF.Gelu,
                                                                         bias=c025[:], scale=float(2.0 ** -8)))

                def emit_tail(ft):
                    S_ = HS[ft % 2]
                    Fv, Iv = S_["f"], S_["i"]
                    wait("vector", "cAC", gbv[ft])
                    dve1 = inc("vector", "cDV", 1,
                               lambda v_, Fv=Fv: v_.tensor_tensor(out=Fv[1][:], in0=Fv[5][:], in1=Fv[4][:], op=ALU.subtract))
                    wait("vector", "cDV", dve1)
                    dve2 = inc("vector", "cDV", 1,
                               lambda v_, Fv=Fv: v_.tensor_tensor(out=Fv[2][:], in0=Fv[3][:], in1=Fv[1][:], op=ALU.mult))
                    wait("vector", "cDV", dve2)
                    dve3 = inc("vector", "cDV", 1,
                               lambda v_, Fv=Fv: v_.scalar_tensor_tensor(out=Fv[0][:], in0=Fv[4][:], scalar=256.0,
                                                                         in1=Fv[2][:], op0=ALU.mult, op1=ALU.add))
                    t3v[ft] = dve3

                def emit_act2(ft):
                    S_ = HS[ft % 2]
                    Fv, Iv = S_["f"], S_["i"]
                    wait("scalar", "cDV", t3v[ft])
                    vy1 = inc("scalar", "cAC", 1,
                              lambda sc, Iv=Iv, Fv=Fv: sc.activation(out=Iv[1][:], in_=Fv[0][:],
                                                                     func=AF.Identity))
                    wait("scalar", "cAC", vy1)
                    wait_drain("scalar", ("ytr", ft))
                    q = inc("scalar", "cAC", 1,
                            lambda sc, ft=ft, Iv=Iv: sc.activation(out=yt[:, ft, :], in_=Iv[1][:],
                                                                   func=AF.Identity))
                    fill[("yt", ft)] = q
                    drain[("chain", ft % 2)] = [("cAC", q)]

                def w2_sl0_partial(ftw):
                    wait("tensor", "cAC", fill[("yt", ftw)])
                    for tt in range(4):
                        off2 = 512 * (3 + tt)
                        if ftw == 0:
                            wait_drain("tensor", ("o", tt))
                        inc("tensor", "cPE", 1,
                            lambda t, ftw=ftw, tt=tt, off2=off2:
                            t.matmul(pf[:, off2:off2 + 512], yt[:, ftw, 128 * tt:128 * (tt + 1)],
                                     w2b[:, ftw, 0:512],
                                     start=(ftw == 0), stop=False))

                for ft in range(NF + 1):
                    if ft < NF:
                        emit_h(ft)
                    if ft >= 1:
                        emit_gagb(ft - 1)
                    if ft < NF:
                        emit_t0(ft)
                    if ft >= 1:
                        emit_tail(ft - 1)
                    if ft >= 1:
                        emit_act2(ft - 1)
                    if ft < NF:
                        emit_stage1(ft)
                    if ft >= 5:
                        w2_sl0_partial(ft - 5)
                for ftw in range(NF - 4, NF):
                    w2_sl0_partial(ftw)
                v_h_last = hv[NF - 1]
                drain[("z2cm",)] = [("cPE", v_h_last)]
                if ci + 1 < len(chunks):
                    v_z2cm = load_z2cm(*chunks[ci + 1])
                for sl in range(2):
                    if sl == 1:
                        v_by_tt = [0, 0, 0, 0]
                        for ft in range(NF):
                            wait("tensor", "cAC", fill[("yt", ft)])
                            for tt in range(4):
                                off = 512 * (3 + tt)
                                if ft == 0:
                                    wait_drain("tensor", ("o", tt))
                                v = inc("tensor", "cPE", 1,
                                        lambda t, ft=ft, tt=tt, off=off:
                                        t.matmul(pf[:, off:off + 512], yt[:, ft, 128 * tt:128 * (tt + 1)],
                                                 w2b[:, ft, 512:1024],
                                                 start=(ft == 0), stop=False))
                                v_by_tt[tt] = v
                            drain[("ytr", ft)] = [("cPE", v_by_tt[3])]
                    for tt in range(4):
                        off = 512 * (3 + tt)
                        v = inc("tensor", "cPE", 1,
                                lambda t, tt=tt, sl=sl, off=off:
                                t.matmul(pf[:, off:off + 512], ones16r[0:1, :],
                                         b2row16[0:1, 512 * sl:512 * (sl + 1)],
                                         start=False, stop=True))
                        fill[("o", tt)] = v
                    for tt in range(4):
                        g2 = gbase + tt
                        off = 512 * (3 + tt)
                        wait_drain("sync", ("x1b",))
                        wait("sync", "dX1S0", vx1end[b][0])
                        wait("sync", "dX1S1", vx1end[b][1])
                        vx1 = inc("sync", "dX1L", 16,
                                  lambda s, g2=g2: s.dma_start(out=x1b[:],
                                                               in_=x1scr[128 * g2:128 * (g2 + 1), :]))
                        wait("vector", "cPE", fill[("o", tt)])
                        wait_drain("vector", ("fchain",))
                        dvu = inc("vector", "cDV", 1,
                                  lambda v_, off=off: v_.tensor_copy(out=fY1I[:], in_=pf[:, off:off + 512]))
                        add_drain(("o", tt), "cDV", dvu)
                        wait("vector", "cDV", dvu)
                        wait("vector", "dX1L", vx1)
                        dvo = inc("vector", "cDV", 1,
                                  lambda v_, sl=sl: v_.scalar_tensor_tensor(
                                      out=fX2[:], in0=fY1I[:], scalar=float(2.0 ** -8),
                                      in1=x1b[:, 512 * sl:512 * (sl + 1)], op0=ALU.mult, op1=ALU.add))
                        drain[("x1b",)] = [("cDV", dvo)]
                        drain[("fchain",)] = [("cDV", dvo)]
                        wait("scalar", "cDV", dvo)
                        vout = inc("scalar", "dOUT", 16,
                                   lambda sc, g2=g2, sl=sl: sc.dma_start(
                                       out=ys[128 * g2:128 * (g2 + 1), 512 * sl:512 * (sl + 1)],
                                       in_=fX2[:]))
                        app_drain(("fchain",), "dOUT", vout)
                drain[("z2cm",)] = [("cPE", v_h_last)]

        wait("scalar", "dOUT", cnt["dOUT"])
        wait("sync", "dOUT", cnt["dOUT"])

        blk = E(nc.Block())
        blk.sync(lambda eng: [f(eng) for f in ops["sync"]])
        blk.tensor(lambda eng: [f(eng) for f in ops["tensor"]])
        blk.vector(lambda eng: [f(eng) for f in ops["vector"]])
        blk.scalar(lambda eng: [f(eng) for f in ops["scalar"]])
        blk.gpsimd(lambda eng: [f(eng) for f in ops["gpsimd"]])

    return nc


_CACHE = {}


def kernel(L0_attn_x, L0_attn_norm_w, L0_attn_norm_b, L0_attn_qkv_w, L0_attn_qkv_b,
           L0_attn_proj_w, L0_attn_proj_b, L0_ffn_w1, L0_ffn_b1, L0_ffn_w2, L0_ffn_b2):
    x = np.ascontiguousarray(np.asarray(L0_attn_x, dtype=np.float32))
    com = dict(
        qkv_w=np.ascontiguousarray(np.asarray(L0_attn_qkv_w, dtype=np.float32)),
        qkv_b=np.ascontiguousarray(np.asarray(L0_attn_qkv_b, dtype=np.float32)),
        norm_w=np.ascontiguousarray(np.asarray(L0_attn_norm_w, dtype=np.float32)),
        norm_b=np.ascontiguousarray(np.asarray(L0_attn_norm_b, dtype=np.float32)),
        proj_w=np.ascontiguousarray(np.asarray(L0_attn_proj_w, dtype=np.float32)),
        proj_b=np.ascontiguousarray(np.asarray(L0_attn_proj_b, dtype=np.float32)),
        w1=np.ascontiguousarray(np.asarray(L0_ffn_w1, dtype=np.float32)),
        b1=np.ascontiguousarray(np.asarray(L0_ffn_b1, dtype=np.float32)),
        w2=np.ascontiguousarray(np.asarray(L0_ffn_w2, dtype=np.float32)),
        b2=np.ascontiguousarray(np.asarray(L0_ffn_b2, dtype=np.float32)),
        identf=np.eye(128, dtype=np.float32),
        ident16=np.eye(128, dtype=ml_dtypes.bfloat16),
    )
    if "nc" not in _CACHE:
        _CACHE["nc"] = build_program()
    nc = _CACHE["nc"]
    in_maps = []
    for c in range(8):
        m = dict(com)
        m["xs"] = np.ascontiguousarray(x[2 * c:2 * c + 2].reshape(T, D))
        in_maps.append(m)
    res = run_bass_kernel_spmd(nc, in_maps, core_ids=list(range(8)))
    out = np.empty((16, TB, D), dtype=np.float32)
    for c in range(8):
        out[2 * c:2 * c + 2] = np.asarray(res.results[c]["ys"]).reshape(2, TB, D)
    return out


# version 18
# speedup vs baseline: 1.2802x; 1.0002x over previous
import contextlib

import numpy as np
import ml_dtypes

import concourse.bass as bass
import concourse.mybir as mybir
from concourse.bass_utils import run_bass_kernel_spmd

F32 = mybir.dt.float32
BF16 = mybir.dt.bfloat16
I32 = mybir.dt.int32
AF = mybir.ActivationFunctionType
ALU = mybir.AluOpType
AX = mybir.AxisListType

T = 4096
TB = 2048
NTB = 16
NT = 32
D = 1024
ND = 8
FF = 4096
NF = 32
SCALE = float(np.float32(1.0) / np.float32(np.sqrt(np.float32(2048.0))))
EPS = 1e-5

BASE = 16448
CAP = 229344
OFF_IDENTF = BASE + 0
OFF_IDENT16 = BASE + 512
OFF_PTH0 = BASE + 768
OFF_CONS = BASE + 2816
OFF_STAT = BASE + 3456
OFF_NW8 = BASE + 5248
OFF_NB16 = BASE + 5280
OFF_NW16 = BASE + 5312
OFF_ONES16R = BASE + 5344
OFF_FL16 = BASE + 5632
OFF_FR16 = BASE + 9728
OFF_PROJB16 = BASE + 15872
PWL_SET0 = [BASE + 5632 + 2048 * i for i in range(6)]
OFF_Q = BASE + 17920
OFF_K = BASE + 50688
OFF_W1B = BASE + 17920
OFF_B16STG = BASE + 17920
OFF_ROWA = BASE + 26112
OFF_ROWB = BASE + 42496
OFF_W1Q = BASE + 83456
OFF_XCM = BASE + 116224
OFF_W2B = BASE + 83456
OFF_PWT = BASE + 148992
OFF_VCM = BASE + 165376
OFF_XST0 = OFF_VCM + 0
OFF_XST1 = OFF_VCM + 4096
OFF_X1T0 = OFF_VCM + 8192
OFF_X1T1 = OFF_VCM + 12288
OFF_SQSCR = OFF_VCM + 16384
OFF_Z2T0 = OFF_VCM + 20480
OFF_Z2T1 = OFF_VCM + 22528
OFF_BSCR0 = OFF_VCM + 24576
OFF_BSCR1 = OFF_VCM + 28672
OFF_WT = BASE + 198144
OFF_WN = BASE + 200192
OFF_Z2CM = BASE + 148992
OFF_YT = BASE + 165376
OFF_X1B = BASE + 198144
PWL_SET1 = [BASE + 157184 + 2048 * i for i in range(4)] + [BASE + 202240, BASE + 204288]
OFF_FT4 = BASE + 206336
OFF_FY1I = BASE + 208384
OFF_B2ROW16 = BASE + 210432
assert BASE + 212480 + 256 <= CAP

DTB = {F32: 4, BF16: 2, I32: 4}


def build_program():
    nc = bass.Bass("TRN2", target_bir_lowering=False)

    xs = nc.dram_tensor("xs", [T, D], F32, kind="ExternalInput").ap()
    qkv_w = nc.dram_tensor("qkv_w", [3 * D, D], F32, kind="ExternalInput").ap()
    qkv_b = nc.dram_tensor("qkv_b", [3 * D], F32, kind="ExternalInput").ap()
    norm_w = nc.dram_tensor("norm_w", [D], F32, kind="ExternalInput").ap()
    norm_b = nc.dram_tensor("norm_b", [D], F32, kind="ExternalInput").ap()
    proj_w = nc.dram_tensor("proj_w", [D, D], F32, kind="ExternalInput").ap()
    proj_b = nc.dram_tensor("proj_b", [D], F32, kind="ExternalInput").ap()
    w1 = nc.dram_tensor("w1", [D, FF], F32, kind="ExternalInput").ap()
    b1 = nc.dram_tensor("b1", [FF], F32, kind="ExternalInput").ap()
    w2 = nc.dram_tensor("w2", [FF, D], F32, kind="ExternalInput").ap()
    b2 = nc.dram_tensor("b2", [D], F32, kind="ExternalInput").ap()
    identf_in = nc.dram_tensor("identf", [128, 128], F32, kind="ExternalInput").ap()
    ident16_in = nc.dram_tensor("ident16", [128, 128], BF16, kind="ExternalInput").ap()
    ys = nc.dram_tensor("ys", [T, D], F32, kind="ExternalOutput").ap()

    x16scr = nc.dram_tensor("x16scr", [T, D], BF16, kind="Internal").ap()
    qw16scr = nc.dram_tensor("qw16scr", [3 * D, D], BF16, kind="Internal").ap()
    pw16scr = nc.dram_tensor("pw16scr", [D, D], BF16, kind="Internal").ap()
    w116scr = nc.dram_tensor("w116scr", [D, FF], BF16, kind="Internal").ap()
    w216scr = nc.dram_tensor("w216scr", [FF, D], BF16, kind="Internal").ap()
    x1scr = nc.dram_tensor("x1scr", [T, D], F32, kind="Internal").ap()
    z2scr = nc.dram_tensor("z2scr", [T, D], BF16, kind="Internal").ap()
    rowscr = nc.dram_tensor("rowscr", [2, T], F32, kind="Internal").ap()
    r16scr = nc.dram_tensor("r16scr", [4, T], BF16, kind="Internal").ap()

    with contextlib.ExitStack() as stack:
        E = stack.enter_context

        def sb(name, shape, dtype, off):
            need = 1
            for s_ in shape[1:]:
                need *= s_
            need *= DTB[dtype]
            assert off + need <= CAP, (name, off, need)
            return nc.alloc_sbuf_tensor_at(name, list(shape), dtype, offset=off)

        identf = sb("identf", [128, 128], F32, OFF_IDENTF)
        ident16 = sb("ident16", [128, 128], BF16, OFF_IDENT16)
        pth16 = sb("pth16", [128, 1024], BF16, OFF_PTH0)
        strm = sb("strm", [128, 64], F32, BASE + 212480)
        nwb = sb("nwb", [128, 16], BF16, OFF_PTH0 + 512)
        ST = {}
        for i, n in enumerate(["mu", "mun", "s1", "s", "rs", "a1", "a2", "m2n", "s2",
                               "tmpA", "tmpB", "ssq", "b1adj", "stx"]):
            ST[n] = sb("st_" + n, [128, 32], F32, OFF_STAT + 128 * i)
        b1adj = ST["b1adj"]
        nw8 = sb("nw8", [128, 8], F32, OFF_NW8)
        nb16c = sb("nb16c", [128, 8], BF16, OFF_NB16)
        nw16c = sb("nw16c", [128, 8], BF16, OFF_NW16)
        ones16r = sb("ones16r", [1, 128], BF16, OFF_ONES16R)
        fl16 = sb("fl16", [2, 2048], BF16, OFF_FL16)
        fr16 = sb("fr16", [2, 3072], BF16, OFF_FR16)
        projb16 = sb("projb16", [1, 1024], BF16, OFF_PROJB16)
        b2row16 = sb("b2row16", [1, 1024], BF16, OFF_B2ROW16)
        Q = sb("Q", [128, 16, 1024], BF16, OFF_Q)
        Kt = sb("K", [128, 16, 1024], BF16, OFF_K)
        w1b = sb("w1b", [128, 8, 4096], BF16, OFF_W1B)
        b16stg = sb("b16stg", [1, 4096], BF16, OFF_B16STG)
        rowA = sb("rowA", [1, 4096], F32, OFF_ROWA)
        rowB = sb("rowB", [1, 4096], F32, OFF_ROWB)
        W1q = sb("W1q", [128, 8, 2048], BF16, OFF_W1Q)
        W1v = sb("W1v", [128, 8, 1024], BF16, OFF_W1Q)
        xcm = sb("xcm", [128, 8, 2048], BF16, OFF_XCM)
        acm = sb("acm", [128, 8, 2048], BF16, OFF_XCM)
        w2b = sb("w2b", [128, 32, 1024], BF16, OFF_W2B)
        pwT = sb("pwT", [128, 8, 1024], BF16, OFF_PWT)
        vcm = sb("vcm", [128, 8, 2048], BF16, OFF_VCM)
        xst = [sb("xst0", [128, 1024], F32, OFF_XST0), sb("xst1", [128, 1024], F32, OFF_XST1)]
        x1t = [sb("x1t0", [128, 1024], F32, OFF_X1T0), sb("x1t1", [128, 1024], F32, OFF_X1T1)]
        sqscr = sb("sqscr", [128, 1024], F32, OFF_SQSCR)
        z2t = [sb("z2t0", [128, 1024], BF16, OFF_Z2T0), sb("z2t1", [128, 1024], BF16, OFF_Z2T1)]
        bscr = [sb("bscr0", [128, 2048], BF16, OFF_BSCR0), sb("bscr1", [128, 2048], BF16, OFF_BSCR1)]
        Wt = sb("Wt", [128, 1024], BF16, OFF_WT)
        Wn = sb("Wn", [128, 1024], BF16, OFF_WN)
        vtm = sb("vtm", [128, 1024], BF16, OFF_WN)
        z2cm = sb("z2cm", [128, 8, 512], BF16, OFF_Z2CM)
        yt = sb("yt", [128, 32, 512], BF16, OFF_YT)
        x1b = sb("x1b", [128, 1024], F32, OFF_X1B)

        def con(name, dtype, idx):
            return nc.alloc_sbuf_tensor_at(name, [128, 1], dtype, offset=OFF_CONS + 32 * idx)
        mA = con("mA", I32, 0)
        mR = con("mR", I32, 1)
        c16k = con("c16k", F32, 2)
        ceps = con("ceps", F32, 3)
        c025 = con("c025", F32, 4)
        onesf1 = con("onesf1", F32, 6)
        e0row = nc.alloc_sbuf_tensor_at("e0row", [1, 2], BF16, offset=OFF_CONS + 32 * 7)
        mxc = [con("mxc0", F32, 8), con("mxc1", F32, 9)]
        mbc = [con("mbc0", F32, 10), con("mbc1", F32, 11)]
        su1 = [con("su10", F32, 12), con("su11", F32, 13)]
        su2 = [con("su20", F32, 14), con("su21", F32, 15)]
        suc = [con("suc0", F32, 16), con("suc1", F32, 17)]
        rcc = [con("rcc0", F32, 18), con("rcc1", F32, 19)]

        SETO = [PWL_SET0, PWL_SET1]
        HS = []
        for s_i in range(2):
            fv = [nc.alloc_sbuf_tensor_at(f"hs{s_i}f{k}", [128, 512], F32, offset=SETO[s_i][k]) for k in range(6)]
            iv = {k: nc.alloc_sbuf_tensor_at(f"hs{s_i}i{k}", [128, 512], I32, offset=SETO[s_i][k]) for k in (1, 2, 5)}
            HS.append({"f": fv, "i": iv})
        fT4 = nc.alloc_sbuf_tensor_at("fT4", [128, 512], F32, offset=OFF_FT4)
        fY1I = nc.alloc_sbuf_tensor_at("fY1I", [128, 512], I32, offset=OFF_FY1I)
        fC2 = nc.alloc_sbuf_tensor_at("fC2", [128, 512], F32, offset=OFF_FY1I)
        fYF1 = nc.alloc_sbuf_tensor_at("fYF1", [128, 512], F32, offset=OFF_PTH0)
        fX2 = nc.alloc_sbuf_tensor_at("fX2", [128, 512], F32, offset=OFF_PTH0)

        pf = E(nc.psum_tensor("pf", [128, 3584], F32))
        pb = E(nc.psum_tensor("pb", [128, 1024], BF16))

        sem_names = ["dXF0", "dXF1", "dW", "dWq", "dTRa", "dTRb", "dTRp",
                     "dTB0", "dTB1", "dTZ", "dCX0", "dCX1", "dCQ", "dCP",
                     "dCW", "dCW2", "dCWS", "dZ20", "dZ21", "dX1S0", "dX1S1",
                     "dX1L", "dOUT", "cPE", "cDV", "cAC", "cPL"]
        sems = {n: E(nc.semaphore(n)) for n in sem_names}
        cnt = {n: 0 for n in sem_names}
        ops = {e: [] for e in ("sync", "tensor", "vector", "scalar", "gpsimd")}

        def emit(e, fn):
            ops[e].append(fn)

        def inc(e, sem, n, build):
            cnt[sem] += n
            emit(e, lambda eng, b_=build, s_=sems[sem], k_=n: b_(eng).then_inc(s_, k_))
            return cnt[sem]

        def wait(e, sem, v):
            if v > 0:
                emit(e, lambda eng, s_=sems[sem], v_=v: eng.wait_ge(s_, v_))

        fill = {}
        drain = {}

        def add_drain(key, sem, v):
            drain[key] = [(sem, v)]

        def app_drain(key, sem, v):
            drain.setdefault(key, []).append((sem, v))

        def wait_drain(e, key):
            for s_, v_ in drain.get(key, ()):
                wait(e, s_, v_)

        def barrier(e):
            for s_ in ("cDV", "cAC", "cPL"):
                wait(e, s_, cnt[s_])

        def vsync_op(build):
            v = inc("vector", "cDV", 1, build)
            wait("vector", "cDV", v)
            return v

        for f in (lambda v: v.memset(mA[:], -64),
                  lambda v: v.memset(mR[:], 63),
                  lambda v: v.memset(c16k[:], 16384.0),
                  lambda v: v.memset(ceps[:], float(EPS)),
                  lambda v: v.memset(c025[:], 0.25),
                  lambda v: v.memset(onesf1[:], 1.0)):
            emit("vector", f)
        v_memset = inc("vector", "cDV", 1, lambda v: v.memset(ones16r[:], 1.0))

        vx16 = [0, 0]
        for b in range(2):
            vx16[b] = inc("gpsimd", f"dCX{b}", 16,
                          lambda g, b=b: g.dma_start(out=x16scr[b * TB:(b + 1) * TB, :],
                                                     in_=xs[b * TB:(b + 1) * TB, :]))
        vqw16 = inc("gpsimd", "dCQ", 16, lambda g: g.dma_start(out=qw16scr, in_=qkv_w))
        vpw16 = inc("gpsimd", "dCP", 16, lambda g: g.dma_start(out=pw16scr, in_=proj_w))
        inc("gpsimd", "dCWS", 16, lambda g: g.dma_start(out=w116scr, in_=w1))
        v_wcast = inc("gpsimd", "dCWS", 16, lambda g: g.dma_start(out=w216scr, in_=w2))

        videt = inc("sync", "dW", 16, lambda s: s.dma_start(out=identf[:], in_=identf_in))
        videt16 = inc("sync", "dW", 16, lambda s: s.dma_start(out=ident16[:], in_=ident16_in))
        vnw = inc("sync", "dW", 16, lambda s: s.dma_start(out=rowA[:1, 0:1024], in_=norm_w[None, :]))
        vnb = inc("sync", "dW", 16, lambda s: s.dma_start(out=rowA[:1, 1024:2048], in_=norm_b[None, :]))
        vpbld = inc("sync", "dW", 16, lambda s: s.dma_start(out=rowA[:1, 2048:3072], in_=proj_b[None, :]))
        vb2ld = inc("sync", "dW", 16, lambda s: s.dma_start(out=rowA[:1, 3072:4096], in_=b2[None, :]))
        vb1ld = inc("sync", "dW", 16, lambda s: s.dma_start(out=rowB[:1, :], in_=b1[None, :]))
        v_dW_all = cnt["dW"]

        wait("sync", "dCQ", vqw16)

        def load_W1q():
            vv = 0
            for j in range(ND):
                vv = inc("sync", "dTRa", 16,
                         lambda s, j=j: s.dma_start(out=W1q[:, j, :],
                                                    in_=qw16scr[0:2048, 128 * j:128 * (j + 1)],
                                                    transpose=True))
            return vv

        def load_W1v():
            vv = 0
            for j in range(ND):
                vv = inc("sync", "dTRb", 16,
                         lambda s, j=j: s.dma_start(out=W1v[:, j, :],
                                                    in_=qw16scr[2048:3072, 128 * j:128 * (j + 1)],
                                                    transpose=True))
            return vv

        vW1 = load_W1q()
        wait("sync", "dCP", vpw16)
        for j in range(ND):
            inc("sync", "dTRp", 16,
                lambda s, j=j: s.dma_start(out=pwT[:, j, :], in_=pw16scr[:, 128 * j:128 * (j + 1)],
                                           transpose=True))
        v_pwT = cnt["dTRp"]

        wait("tensor", "dW", v_dW_all)
        wait("tensor", "cDV", v_memset)

        def columnize(src_row_fn, evac_fn, n):
            for c in range(n):
                slot = c % 4
                key = ("bank6",)
                wait_drain("tensor", key)
                v = inc("tensor", "cPE", 1,
                        lambda t, c=c, slot=slot: t.matmul(pf[:, 3072 + 128 * slot:3072 + 128 * slot + 1],
                                                           src_row_fn(c), onesf1[0:1, 0:1],
                                                           start=True, stop=True))
                wait("vector", "cPE", v)
                dv = evac_fn(c, slot)
                add_drain(key, "cDV", dv)

        columnize(lambda c: rowB[0:1, 128 * c:128 * (c + 1)],
                  lambda c, slot: inc("vector", "cDV", 1,
                                      lambda v_, c=c, slot=slot: v_.tensor_scalar(
                                          out=b1adj[:, c:c + 1],
                                          in0=pf[:, 3072 + 128 * slot:3072 + 128 * slot + 1],
                                          scalar1=256.0, scalar2=None, op0=ALU.mult)), 32)
        columnize(lambda c: rowA[0:1, 128 * c:128 * (c + 1)],
                  lambda c, slot: inc("vector", "cDV", 1,
                                      lambda v_, c=c, slot=slot: v_.tensor_copy(
                                          out=nw8[:, c:c + 1],
                                          in_=pf[:, 3072 + 128 * slot:3072 + 128 * slot + 1])), 8)
        columnize(lambda c: rowA[0:1, 1024 + 128 * c:1024 + 128 * (c + 1)],
                  lambda c, slot: inc("vector", "cDV", 1,
                                      lambda v_, c=c, slot=slot: v_.tensor_copy(
                                          out=nb16c[:, c:c + 1],
                                          in_=pf[:, 3072 + 128 * slot:3072 + 128 * slot + 1])), 8)
        vnw16 = inc("vector", "cDV", 1, lambda v_: v_.tensor_copy(out=nw16c[:], in_=nw8[:]))
        wait("vector", "dW", v_dW_all)
        inc("vector", "cDV", 1, lambda v_: v_.tensor_copy(out=projb16[:1, :], in_=rowA[:1, 2048:3072]))
        inc("vector", "cDV", 1, lambda v_: v_.tensor_scalar(out=b2row16[:1, :], in0=rowA[:1, 3072:4096],
                                                            scalar1=256.0, scalar2=None, op0=ALU.mult))

        wait("tensor", "dTRa", vW1)
        wait("tensor", "cDV", vnw16)
        barrier("tensor")
        wait("sync", "cPE", cnt["cPE"])
        vqbld = inc("sync", "dWq", 16, lambda s: s.dma_start(out=rowB[:1, 0:3072], in_=qkv_b[None, :]))
        v_e0 = inc("vector", "cDV", 1, lambda v_: v_.memset(e0row[:], 0.0))
        wait("vector", "cDV", v_e0)
        emit("vector", lambda v_: v_.tensor_scalar(out=e0row[0:1, 0:1], in0=onesf1[0:1, 0:1],
                                                   scalar1=1.0, scalar2=None, op0=ALU.mult))
        emit("vector", lambda v_: v_.tensor_copy(out=nwb[:, 0:16:2], in_=nb16c[:]))
        emit("vector", lambda v_: v_.tensor_copy(out=nwb[:, 1:16:2], in_=nw16c[:]))
        wait("vector", "dWq", vqbld)
        dvqb = inc("vector", "cDV", 1, lambda v_: v_.tensor_copy(out=b16stg[:1, 0:3072], in_=rowB[:1, 0:3072]))
        wait("tensor", "cDV", dvqb)

        def wsum_bias_slices(wt_tensor, col0, sls):
            for sl in sls:
                rel = 512 * sl - col0
                key = ("bank6",)
                wait_drain("tensor", key)
                v = 0
                for j in range(ND):
                    v = inc("tensor", "cPE", 1,
                            lambda t, j=j, rel=rel, wt=wt_tensor: t.matmul(
                                pf[0:2, 3072:3584], nwb[:, 2 * j:2 * j + 2],
                                wt[:, j, rel:rel + 512], start=(j == 0), stop=False))
                v = inc("tensor", "cPE", 1,
                        lambda t, sl=sl: t.matmul(pf[0:2, 3072:3584], e0row[0:1, :],
                                                  b16stg[0:1, 512 * sl:512 * (sl + 1)],
                                                  start=False, stop=True))
                wait("vector", "cPE", v)
                dv = inc("vector", "cDV", 1,
                         lambda v_, sl=sl: v_.tensor_copy(out=fr16[0:2, 512 * sl:512 * (sl + 1)],
                                                          in_=pf[0:2, 3072:3584]))
                add_drain(key, "cDV", dv)

        wsum_bias_slices(W1q, 0, [0, 1, 2, 3])
        v_wb1_pe = cnt["cPE"]
        wait("sync", "cPE", v_wb1_pe)
        vW1v = load_W1v()
        wait("tensor", "dTRb", vW1v)
        wsum_bias_slices(W1v, 2048, [4, 5])
        v_wb2_pe = cnt["cPE"]

        ev = lambda f: emit("vector", f)
        tile_sq = [0] * NT
        tile_rd = [0] * NT
        tile_mu = [0] * NT
        for g in range(NT):
            buf = g % 2
            if g >= 2:
                wait("sync", "cAC", tile_sq[g - 2])
                wait("sync", "cDV", tile_rd[g - 2])
            vld = inc("sync", f"dXF{buf}", 16,
                      lambda s, g=g, buf=buf: s.dma_start(out=xst[buf][:], in_=xs[128 * g:128 * (g + 1), :]))
            wait("vector", f"dXF{buf}", vld)
            tile_mu[g] = inc("vector", "cDV", 1,
                             lambda v_, g=g, buf=buf: v_.reduce_sum(out=ST["mu"][:, g:g + 1], in_=xst[buf][:], axis=AX.X))
            wait("scalar", f"dXF{buf}", vld)
            if g >= 1:
                wait("scalar", "cDV", tile_rd[g - 1])
            tile_sq[g] = inc("scalar", "cAC", 1,
                             lambda sc, buf=buf: sc.activation(out=sqscr[:], in_=xst[buf][:], func=AF.Square))
            wait("vector", "cAC", tile_sq[g])
            tile_rd[g] = inc("vector", "cDV", 1,
                             lambda v_, g=g: v_.reduce_sum(out=ST["ssq"][:, g:g + 1], in_=sqscr[:], axis=AX.X))

        wait("vector", "cDV", tile_mu[NT - 1])
        wait("vector", "cDV", tile_rd[NT - 1])
        vsync_op(lambda v_: v_.tensor_scalar(out=ST["tmpA"][:], in0=ST["mu"][:], scalar1=1.0 / 1024.0, scalar2=None, op0=ALU.mult))
        vsync_op(lambda v_: v_.tensor_scalar(out=ST["tmpB"][:], in0=ST["ssq"][:], scalar1=1.0 / 1024.0, scalar2=None, op0=ALU.mult))
        vsync_op(lambda v_: v_.tensor_tensor(out=ST["stx"][:], in0=ST["tmpA"][:], in1=ST["tmpA"][:], op=ALU.mult))
        vv1 = vsync_op(lambda v_: v_.tensor_tensor(out=ST["ssq"][:], in0=ST["tmpB"][:], in1=ST["stx"][:], op=ALU.subtract))
        wait("scalar", "cDV", vv1)
        va1 = inc("scalar", "cAC", 1,
                  lambda sc: sc.activation(out=ST["tmpB"][:], in_=ST["ssq"][:], func=AF.Sqrt, bias=ceps[:], scale=1.0))
        va1b = inc("scalar", "cAC", 1,
                   lambda sc: sc.activation(out=ST["stx"][:], in_=ST["ssq"][:], func=AF.Relu, bias=ceps[:], scale=1.0))
        wait("vector", "cAC", va1b)
        vsync_op(lambda v_: v_.reciprocal(out=ST["s1"][:], in_=ST["tmpB"][:]))
        vsync_op(lambda v_: v_.tensor_scalar(out=ST["a1"][:], in0=ST["s1"][:], scalar1=1.0, scalar2=None, op0=ALU.add))
        vsync_op(lambda v_: v_.scalar_tensor_tensor(out=ST["a2"][:], in0=ST["tmpA"][:], scalar=-1.0, in1=ST["s1"][:],
                                                    op0=ALU.mult, op1=ALU.mult))
        vsync_op(lambda v_: v_.tensor_scalar(out=ST["mun"][:], in0=ST["tmpA"][:], scalar1=-1.0, scalar2=None, op0=ALU.mult))
        vsync_op(lambda v_: v_.reciprocal(out=ST["tmpA"][:], in_=ST["stx"][:]))
        vv2 = vsync_op(lambda v_: v_.tensor_tensor(out=ST["stx"][:], in0=ST["ssq"][:], in1=ST["tmpA"][:], op=ALU.mult))
        wait("scalar", "cDV", vv2)
        va2 = inc("scalar", "cAC", 1,
                  lambda sc: sc.activation(out=ST["tmpA"][:], in_=ST["stx"][:], func=AF.Sqrt, bias=ceps[:], scale=1.0))
        wait("vector", "cAC", va2)
        vsync_op(lambda v_: v_.tensor_tensor(out=ST["rs"][:], in0=ST["tmpB"][:], in1=ST["tmpA"][:], op=ALU.mult))
        vsync_op(lambda v_: v_.reciprocal(out=ST["stx"][:], in_=ST["tmpA"][:]))
        vsync_op(lambda v_: v_.tensor_tensor(out=ST["s"][:], in0=ST["s1"][:], in1=ST["stx"][:], op=ALU.mult))
        vstat = inc("vector", "cDV", 1, lambda v_: v_.engine_nop())

        wait("tensor", "cDV", vstat)
        wait("tensor", "dW", v_dW_all)
        ev(lambda v_: v_.tensor_copy(out=strm[:, 0:64:2], in_=ST["rs"][:]))
        v_strm = inc("vector", "cDV", 1, lambda v_: v_.tensor_copy(out=strm[:, 1:64:2], in_=ST["mun"][:]))

        v_SlastPE = [0, 0]
        v_projpe = [0, 0]
        v_alast = [0, 0]
        v_z2st = [0, 0]
        v_x1st = [0, 0]
        vz2end = [None, None]
        vx1end = [None, None]
        qk_regions = [0, 1024, 2048]
        v_lastW1pe = v_wb2_pe
        for b in range(2):
            T0 = b * TB
            wait("tensor", "cDV", v_strm)
            if b == 1:
                barrier("tensor")
            for c in range(NTB):
                g = NTB * b + c
                key = ("bank6",)
                wait_drain("tensor", key)
                v = inc("tensor", "cPE", 1,
                        lambda t, g=g: t.matmul(pf[0:2, 3072:3200], strm[:, 2 * g:2 * g + 2],
                                                identf[:], start=True, stop=True))
                wait("vector", "cPE", v)
                if b == 1 and c == 0:
                    wait("vector", "cPE", v_projpe[0])
                dv = inc("vector", "cDV", 1,
                         lambda v_, c=c: v_.tensor_copy(out=fl16[0:2, 128 * c:128 * (c + 1)],
                                                        in_=pf[0:2, 3072:3200]))
                add_drain(key, "cDV", dv)
            v_fl_dv = cnt["cDV"]
            wait("sync", "cPE", v_lastW1pe)
            vW1 = load_W1q()
            wait("sync", f"dCX{b}", vx16[b])
            if b == 1:
                wait("sync", "cPE", v_alast[0])
            v_xcm = 0
            for j in range(ND):
                sbf = j % 2
                wait_drain("sync", ("bscr", sbf))
                vld = inc("sync", f"dTB{sbf}", 16,
                          lambda s, j=j, sbf=sbf, T0=T0: s.dma_start(out=bscr[sbf][:],
                                                                     in_=x16scr[T0:T0 + TB, 128 * j:128 * (j + 1)],
                                                                     transpose=True))
                wait("vector", f"dTB{sbf}", vld)
                v_xcm = inc("vector", "cDV", 1,
                            lambda v_, j=j, sbf=sbf: v_.tensor_scalar(out=xcm[:, j, :], in0=bscr[sbf][:],
                                                                      scalar1=nw8[:, j:j + 1], scalar2=None,
                                                                      op0=ALU.mult))
                add_drain(("bscr", sbf), "cDV", v_xcm)

            wait("tensor", "cDV", v_xcm)
            wait("tensor", "cDV", v_fl_dv)
            wait("tensor", "dTRa", vW1)
            barrier("tensor")
            for m in range(NTB):
                for half in range(2):
                    off = qk_regions[(2 * m + half) % 3]
                    key = ("qk", off)
                    wait_drain("tensor", key)
                    v = 0
                    for j in range(ND):
                        for s2_ in range(2):
                            v = inc("tensor", "cPE", 1,
                                    lambda t, j=j, s2_=s2_, m=m, half=half, off=off:
                                    t.matmul(pf[:, off + 512 * s2_:off + 512 * (s2_ + 1)],
                                             xcm[:, j, 128 * m:128 * (m + 1)],
                                             W1q[:, j, 1024 * half + 512 * s2_:1024 * half + 512 * (s2_ + 1)],
                                             start=(j == 0), stop=False))
                    for s2_ in range(2):
                        v = inc("tensor", "cPE", 1,
                                lambda t, s2_=s2_, half=half, off=off, m=m:
                                t.matmul(pf[:, off + 512 * s2_:off + 512 * (s2_ + 1)],
                                         fl16[0:2, 128 * m:128 * (m + 1)],
                                         fr16[0:2, 1024 * half + 512 * s2_:1024 * half + 512 * (s2_ + 1)],
                                         start=False, stop=True))
                    wait("vector", "cPE", v)
                    dst_t = Q if half == 0 else Kt
                    g = NTB * b + m
                    dv = inc("vector", "cDV", 1,
                             lambda v_, dst_t=dst_t, m=m, off=off, g=g:
                             v_.tensor_scalar(out=dst_t[:, m, :], in0=pf[:, off:off + 1024],
                                              scalar1=ST["s"][:, g:g + 1], scalar2=None, op0=ALU.mult))
                    add_drain(key, "cDV", dv)
            v_qk_pe = cnt["cPE"]

            for _k in (("x1t", 0), ("x1t", 1), ("z2t", 0), ("z2t", 1), ("xst", 0), ("xst", 1)):
                wait_drain("vector", _k)
            wait("sync", "cPE", v_qk_pe)
            vW1v = load_W1v()
            wait("tensor", "dTRb", vW1v)
            for m in range(NTB):
                off = qk_regions[m % 3]
                key = ("qk", off)
                wait_drain("tensor", key)
                v = 0
                for j in range(ND):
                    for s2_ in range(2):
                        v = inc("tensor", "cPE", 1,
                                lambda t, j=j, s2_=s2_, m=m, off=off:
                                t.matmul(pf[:, off + 512 * s2_:off + 512 * (s2_ + 1)],
                                         xcm[:, j, 128 * m:128 * (m + 1)],
                                         W1v[:, j, 512 * s2_:512 * (s2_ + 1)],
                                         start=(j == 0), stop=False))
                for s2_ in range(2):
                    v = inc("tensor", "cPE", 1,
                            lambda t, s2_=s2_, off=off, m=m:
                            t.matmul(pf[:, off + 512 * s2_:off + 512 * (s2_ + 1)],
                                     fl16[0:2, 128 * m:128 * (m + 1)],
                                     fr16[0:2, 2048 + 512 * s2_:2048 + 512 * (s2_ + 1)],
                                     start=False, stop=True))
                wait("vector", "cPE", v)
                wait_drain("vector", ("vtm",))
                dv = inc("vector", "cDV", 1,
                         lambda v_, off=off: v_.tensor_copy(out=vtm[:], in_=pf[:, off:off + 1024]))
                add_drain(key, "cDV", dv)
                wait("tensor", "cDV", dv)
                wait_drain("tensor", ("pbV",))
                vtr = 0
                for kc in range(8):
                    vtr = inc("tensor", "cPE", 1,
                              lambda t, kc=kc: t.transpose(pb[:, 128 * kc:128 * (kc + 1)],
                                                           vtm[:, 128 * kc:128 * (kc + 1)], ident16[:]))
                add_drain(("vtm",), "cPE", vtr)
                wait("vector", "cPE", vtr)
                dvv = 0
                for kc in range(8):
                    dvv = inc("vector", "cDV", 1,
                              lambda v_, kc=kc, m=m: v_.tensor_copy(out=vcm[:, kc, 128 * m:128 * (m + 1)],
                                                                    in_=pb[:, 128 * kc:128 * (kc + 1)]))
                add_drain(("pbV",), "cDV", dvv)
            v_lastW1pe = cnt["cPE"]
            v_vtm_pe = cnt["cPE"]
            v_vcm_dv = cnt["cDV"]

            wait("tensor", "cDV", v_vcm_dv)
            wait("tensor", "dW", v_dW_all)
            barrier("tensor")
            wait("vector", "cPE", v_vtm_pe)
            for qt in range(8):
                p = qt % 2
                s_off = 1024 * p
                key = ("S", p)
                wait_drain("tensor", key)
                v = 0
                for c in range(NTB):
                    for s2_ in range(2):
                        v = inc("tensor", "cPE", 1,
                                lambda t, c=c, s2_=s2_, qt=qt, s_off=s_off:
                                t.matmul(pf[:, s_off + 512 * s2_:s_off + 512 * (s2_ + 1)],
                                         Q[:, c, 128 * qt:128 * (qt + 1)],
                                         Kt[:, c, 512 * s2_:512 * (s2_ + 1)],
                                         start=(c == 0), stop=(c == NTB - 1)))
                if qt == 7:
                    v_SlastPE[b] = v
                wait("vector", "cPE", v)
                vsync_op(lambda v_, p=p, s_off=s_off: v_.reduce_max(out=mxc[p][:], in_=pf[:, s_off:s_off + 1024], axis=AX.X))
                dvm = inc("vector", "cDV", 1,
                          lambda v_, p=p: v_.tensor_scalar(out=mbc[p][:], in0=mxc[p][:], scalar1=-SCALE,
                                                           scalar2=None, op0=ALU.mult))
                wait("scalar", "cDV", dvm)
                wait_drain("scalar", ("pth",))
                va = 0
                for hh in range(2):
                    va = inc("scalar", "cAC", 1,
                             lambda sc, p=p, s_off=s_off, hh=hh:
                             sc.activation(out=pth16[:, 512 * hh:512 * (hh + 1)],
                                           in_=pf[:, s_off + 512 * hh:s_off + 512 * (hh + 1)],
                                           func=AF.Exp, bias=mbc[p][:], scale=SCALE,
                                           accum_out=(su1[p][:] if hh == 0 else su2[p][:])))
                add_drain(key, "cAC", va)
                wait("vector", "cAC", va)
                wait("gpsimd", "cAC", va)
                vsuc = inc("gpsimd", "cPL", 1,
                           lambda g_, p=p: g_.tensor_tensor(out=suc[p][:], in0=su1[p][:], in1=su2[p][:], op=ALU.add))
                wait("vector", "cPL", vsuc)
                v_rcc = vsync_op(lambda v_, p=p: v_.reciprocal(out=rcc[p][:], in_=suc[p][:]))
                wait_drain("gpsimd", ("Wn",))
                wait("gpsimd", "cDV", v_rcc)
                dvn = 0
                for hh in range(2):
                    dvn = inc("gpsimd", "cPL", 1,
                              lambda g_, p=p, hh=hh:
                              g_.tensor_scalar(out=Wn[:, 512 * hh:512 * (hh + 1)],
                                               in0=pth16[:, 512 * hh:512 * (hh + 1)],
                                               scalar1=rcc[p][:], scalar2=None, op0=ALU.mult))
                add_drain(("pth",), "cPL", dvn)
                wait("tensor", "cPL", dvn)
                wait_drain("tensor", ("Wt",))
                vtr = 0
                for kc in range(8):
                    vtr = inc("tensor", "cPE", 1,
                              lambda t, kc=kc: t.transpose(pb[:, 128 * kc:128 * (kc + 1)],
                                                           Wn[:, 128 * kc:128 * (kc + 1)], ident16[:]))
                add_drain(("Wn",), "cPE", vtr)
                wait("vector", "cPE", vtr)
                dvt = inc("vector", "cDV", 1, lambda v_: v_.tensor_copy(out=Wt[:], in_=pb[:]))
                wait("tensor", "cDV", dvt)
                va_last = 0
                for w in range(4):
                    slot = 4 + (4 * qt + w) % 3
                    off = 512 * slot
                    key2 = ("a", slot)
                    wait_drain("tensor", key2)
                    v = 0
                    for kc in range(8):
                        v = inc("tensor", "cPE", 1,
                                lambda t, kc=kc, w=w, off=off:
                                t.matmul(pf[:, off:off + 512], Wt[:, 128 * kc:128 * (kc + 1)],
                                         vcm[:, kc, 512 * w:512 * (w + 1)],
                                         start=(kc == 0), stop=(kc == 7)))
                    va_last = v
                    wait("scalar", "cPE", v)
                    vaa = inc("scalar", "cAC", 1,
                              lambda sc, qt=qt, w=w, off=off:
                              sc.activation(out=acm[:, qt, 512 * w:512 * (w + 1)], in_=pf[:, off:off + 512],
                                            func=AF.Copy, scale=1.0))
                    add_drain(key2, "cAC", vaa)
                add_drain(("Wt",), "cPE", va_last)
            v_alast[b] = cnt["cPE"]
            v_acm_done = cnt["cAC"]

            wait("sync", "cPE", v_alast[b])
            wait("vector", "cPE", v_alast[b])
            wait("tensor", "cAC", v_acm_done)
            barrier("tensor")
            barrier("scalar")
            for _sm in ("dX1S0", "dX1S1", "dZ20", "dZ21"):
                wait("scalar", _sm, cnt[_sm])
                wait("vector", _sm, cnt[_sm])
            wait("tensor", "dTRp", v_pwT)
            for m in range(NTB):
                g = NTB * b + m
                xbuf = m % 2
                wait_drain("sync", ("xst", xbuf))
                vxl = inc("sync", f"dXF{xbuf}", 16,
                          lambda s, g=g, xbuf=xbuf: s.dma_start(out=xst[xbuf][:], in_=xs[128 * g:128 * (g + 1), :]))
                pe_fill = [0, 0]
                for s2_ in range(2):
                    slot = (2 * m + s2_) % 4
                    off = 512 * slot
                    key = ("pj", slot)
                    wait_drain("tensor", key)
                    v = 0
                    for qc in range(8):
                        v = inc("tensor", "cPE", 1,
                                lambda t, qc=qc, m=m, s2_=s2_, off=off:
                                t.matmul(pf[:, off:off + 512], acm[:, qc, 128 * m:128 * (m + 1)],
                                         pwT[:, qc, 512 * s2_:512 * (s2_ + 1)],
                                         start=(qc == 0), stop=False))
                    v = inc("tensor", "cPE", 1,
                            lambda t, s2_=s2_, off=off, m=m:
                            t.matmul(pf[:, off:off + 512], fl16[0:1, 128 * m:128 * (m + 1)],
                                     projb16[0:1, 512 * s2_:512 * (s2_ + 1)],
                                     start=False, stop=True))
                    pe_fill[s2_] = v
                v_projpe[b] = pe_fill[1]
                wait("vector", f"dXF{xbuf}", vxl)
                x1buf = m % 2
                wait_drain("vector", ("x1t", x1buf))
                dvxl = inc("vector", "cDV", 1,
                           lambda v_, xbuf=xbuf, g=g: v_.tensor_scalar(out=sqscr[:], in0=xst[xbuf][:],
                                                                      scalar1=ST["a1"][:, g:g + 1],
                                                                      scalar2=ST["a2"][:, g:g + 1],
                                                                      op0=ALU.mult, op1=ALU.add))
                add_drain(("xst", xbuf), "cDV", dvxl)
                wait("vector", "cDV", dvxl)
                dvx = 0
                for s2_ in range(2):
                    slot = (2 * m + s2_) % 4
                    wait("vector", "cPE", pe_fill[s2_])
                    dvx = inc("vector", "cDV", 1,
                              lambda v_, s2_=s2_, slot=slot, g=g, x1buf=x1buf:
                              v_.scalar_tensor_tensor(out=x1t[x1buf][:, 512 * s2_:512 * (s2_ + 1)],
                                                      in0=pf[:, 512 * slot:512 * slot + 512],
                                                      scalar=ST["s"][:, g:g + 1],
                                                      in1=sqscr[:, 512 * s2_:512 * (s2_ + 1)],
                                                      op0=ALU.mult, op1=ALU.add))
                    add_drain(("pj", slot), "cDV", dvx)
                wait("scalar", "cDV", dvx)
                v_x1st[b] = inc("scalar", f"dX1S{x1buf}", 16,
                                lambda sc, g=g, x1buf=x1buf: sc.dma_start(out=x1scr[128 * g:128 * (g + 1), :],
                                                                          in_=x1t[x1buf][:]))
                vid = inc("scalar", "cAC", 1,
                          lambda sc, g=g, x1buf=x1buf: sc.activation(out=sqscr[:], in_=x1t[x1buf][:],
                                                                     func=AF.Identity,
                                                                     accum_out=ST["tmpA"][:, g:g + 1]))
                wait("scalar", "cAC", vid)
                vsq = inc("scalar", "cAC", 1,
                          lambda sc, g=g, x1buf=x1buf: sc.activation(out=sqscr[:], in_=x1t[x1buf][:],
                                                                     func=AF.Square,
                                                                     accum_out=ST["tmpB"][:, g:g + 1]))
                wait("gpsimd", "cAC", vsq)
                inc("gpsimd", "cPL", 1,
                    lambda g_, g=g: g_.tensor_scalar(out=ST["m2n"][:, g:g + 1], in0=ST["tmpA"][:, g:g + 1],
                                                     scalar1=-1.0 / 1024.0, scalar2=None, op0=ALU.mult))
                vpl2 = inc("gpsimd", "cPL", 1,
                           lambda g_, g=g: g_.tensor_scalar(out=ST["tmpB"][:, g:g + 1], in0=ST["tmpB"][:, g:g + 1],
                                                            scalar1=1.0 / 1024.0, scalar2=None, op0=ALU.mult))
                wait("vector", "cPL", vpl2)
                dvv = inc("vector", "cDV", 1,
                          lambda v_, g=g: v_.scalar_tensor_tensor(out=ST["tmpA"][:, g:g + 1],
                                                                  in0=ST["m2n"][:, g:g + 1],
                                                                  scalar=ST["m2n"][:, g:g + 1],
                                                                  in1=ST["tmpB"][:, g:g + 1],
                                                                  op0=ALU.mult, op1=ALU.subtract))
                wait("scalar", "cDV", dvv)
                vsq2 = inc("scalar", "cAC", 1,
                           lambda sc, g=g: sc.activation(out=ST["stx"][:, g:g + 1], in_=ST["tmpA"][:, g:g + 1],
                                                         func=AF.Sqrt, bias=ceps[:], scale=-1.0))
                wait("vector", "cAC", vsq2)
                v_s2 = vsync_op(lambda v_, g=g: v_.reciprocal(out=ST["s2"][:, g:g + 1], in_=ST["stx"][:, g:g + 1]))
                zbuf = m % 2
                wait_drain("gpsimd", ("z2t", zbuf))
                wait("gpsimd", "cDV", v_s2)
                dvz = inc("gpsimd", "cPL", 1,
                          lambda g_, g=g, x1buf=x1buf, zbuf=zbuf:
                          g_.tensor_scalar(out=z2t[zbuf][:], in0=x1t[x1buf][:],
                                           scalar1=ST["m2n"][:, g:g + 1], scalar2=ST["s2"][:, g:g + 1],
                                           op0=ALU.add, op1=ALU.mult))
                drain[("x1t", x1buf)] = [("cPL", dvz), (f"dX1S{x1buf}", v_x1st[b]), ("cAC", vsq)]
                wait("scalar", "cPL", dvz)
                v_z2st[b] = inc("scalar", f"dZ2{zbuf}", 16,
                                lambda sc, g=g, zbuf=zbuf: sc.dma_start(out=z2scr[128 * g:128 * (g + 1), :],
                                                                        in_=z2t[zbuf][:]))
                add_drain(("z2t", zbuf), f"dZ2{zbuf}", v_z2st[b])
            vz2end[b] = (cnt["dZ20"], cnt["dZ21"])
            vx1end[b] = (cnt["dX1S0"], cnt["dX1S1"])

        wait("gpsimd", "cPE", v_SlastPE[1])
        wait("gpsimd", "dCWS", v_wcast)
        v_w1b = 0
        for j in range(ND):
            v_w1b = inc("gpsimd", "dCW", 16,
                        lambda g, j=j: g.dma_start(out=w1b[:, j, :], in_=w116scr[128 * j:128 * (j + 1), :]))
        wait("gpsimd", "cPE", v_projpe[1])
        v_w2b = 0
        for jj in range(8):
            v_w2b = inc("gpsimd", "dCW2", 16,
                        lambda g, jj=jj: g.dma_start(
                            out=w2b[:, 4 * jj:4 * (jj + 1), :],
                            in_=w216scr[512 * jj:512 * (jj + 1), :].rearrange("(c p) n -> p c n", p=128)))

        wait("sync", "cPE", v_projpe[1])
        wait("tensor", "dCW", v_w1b)
        barrier("tensor")
        for _sm in ("dZ20", "dZ21", "dX1S0", "dX1S1", "dXF0", "dXF1"):
            wait("vector", _sm, cnt[_sm])
        first_w2_wait = True

        def load_z2cm(b_, tc4_):
            base_tok = b_ * TB + 512 * tc4_
            wait("sync", "dZ20", vz2end[b_][0])
            wait("sync", "dZ21", vz2end[b_][1])
            wait_drain("sync", ("z2cm",))
            vz = 0
            for j in range(ND):
                vz = inc("sync", "dTZ", 16,
                         lambda s, j=j, base_tok=base_tok:
                         s.dma_start(out=z2cm[:, j, :],
                                     in_=z2scr[base_tok:base_tok + 512, 128 * j:128 * (j + 1)],
                                     transpose=True))
            return vz

        chunks = [(b_, t_) for b_ in range(2) for t_ in range(4)]
        v_z2cm = load_z2cm(*chunks[0])
        for ci, (b, tc4) in enumerate(chunks):
            if True:
                base_tok = b * TB + 512 * tc4
                gbase = base_tok // 128
                wait("tensor", "dTZ", v_z2cm)
                if first_w2_wait:
                    wait("tensor", "dCW2", v_w2b)
                    first_w2_wait = False
                hv, t0v, rfv, gbv, t3v = {}, {}, {}, {}, {}

                def emit_h(ft):
                    slot = ft % 3
                    off = 512 * slot
                    wait_drain("tensor", ("h", slot))
                    v = 0
                    for j in range(ND):
                        v = inc("tensor", "cPE", 1,
                                lambda t, j=j, ft=ft, off=off:
                                t.matmul(pf[:, off:off + 512], w1b[:, j, 128 * ft:128 * (ft + 1)],
                                         z2cm[:, j, :],
                                         start=(j == 0), stop=(j == ND - 1)))
                    hv[ft] = v

                def emit_t0(ft):
                    slot = ft % 3
                    off = 512 * slot
                    S_ = HS[ft % 2]
                    Fv = S_["f"]
                    wait("scalar", "cPE", hv[ft])
                    wait_drain("scalar", ("chain", ft % 2))
                    vt0 = inc("scalar", "cAC", 1,
                              lambda sc, off=off, ft=ft, Fv=Fv: sc.activation(
                                  out=Fv[0][:], in_=pf[:, off:off + 512], func=AF.Identity,
                                  bias=b1adj[:, ft:ft + 1], scale=256.0))
                    add_drain(("h", slot), "cAC", vt0)
                    wait("scalar", "cAC", vt0)
                    t0v[ft] = inc("scalar", "cAC", 1,
                                  lambda sc, S_=S_: sc.activation(out=S_["i"][1][:], in_=S_["f"][0][:],
                                                                  func=AF.Identity))

                def emit_stage1(ft):
                    S_ = HS[ft % 2]
                    Fv, Iv = S_["f"], S_["i"]
                    wait("vector", "cAC", t0v[ft])
                    wait_drain("vector", ("chain", ft % 2))
                    inc("vector", "cDV", 1,
                        lambda v_, Iv=Iv: v_.tensor_scalar(out=Iv[2][:], in0=Iv[1][:], scalar1=mA[:],
                                                           scalar2=None, op0=ALU.bitwise_and))
                    dvr = inc("vector", "cDV", 1,
                              lambda v_, Iv=Iv: v_.tensor_scalar(out=Iv[5][:], in0=Iv[1][:], scalar1=mR[:],
                                                                 scalar2=None, op0=ALU.bitwise_and))
                    wait("vector", "cDV", dvr)
                    rfv[ft] = inc("vector", "cDV", 1,
                                  lambda v_, Iv=Iv, Fv=Fv: v_.tensor_scalar(out=Fv[3][:], in0=Iv[5][:],
                                                                            scalar1=4.0, scalar2=None,
                                                                            op0=ALU.mult))

                def emit_gagb(ft):
                    S_ = HS[ft % 2]
                    Fv, Iv = S_["f"], S_["i"]
                    wait("scalar", "cDV", rfv[ft])
                    inc("scalar", "cAC", 1,
                        lambda sc, Iv=Iv, Fv=Fv: sc.activation(out=Fv[4][:], in_=Iv[2][:], func=AF.Gelu,
                                                               scale=float(2.0 ** -8)))
                    gbv[ft] = inc("scalar", "cAC", 1,
                                  lambda sc, Iv=Iv, Fv=Fv: sc.activation(out=Fv[5][:], in_=Iv[2][:], func=AF.Gelu,
                                                                         bias=c025[:], scale=float(2.0 ** -8)))

                def emit_tail(ft):
                    S_ = HS[ft % 2]
                    Fv, Iv = S_["f"], S_["i"]
                    wait("vector", "cAC", gbv[ft])
                    dve1 = inc("vector", "cDV", 1,
                               lambda v_, Fv=Fv: v_.tensor_tensor(out=Fv[1][:], in0=Fv[5][:], in1=Fv[4][:], op=ALU.subtract))
                    wait("vector", "cDV", dve1)
                    dve2 = inc("vector", "cDV", 1,
                               lambda v_, Fv=Fv: v_.tensor_tensor(out=Fv[2][:], in0=Fv[3][:], in1=Fv[1][:], op=ALU.mult))
                    wait("vector", "cDV", dve2)
                    dve3 = inc("vector", "cDV", 1,
                               lambda v_, Fv=Fv: v_.scalar_tensor_tensor(out=Fv[0][:], in0=Fv[4][:], scalar=256.0,
                                                                         in1=Fv[2][:], op0=ALU.mult, op1=ALU.add))
                    t3v[ft] = dve3

                def emit_act2(ft):
                    S_ = HS[ft % 2]
                    Fv, Iv = S_["f"], S_["i"]
                    wait("scalar", "cDV", t3v[ft])
                    vy1 = inc("scalar", "cAC", 1,
                              lambda sc, Iv=Iv, Fv=Fv: sc.activation(out=Iv[1][:], in_=Fv[0][:],
                                                                     func=AF.Identity))
                    wait("scalar", "cAC", vy1)
                    wait_drain("scalar", ("ytr", ft))
                    q = inc("scalar", "cAC", 1,
                            lambda sc, ft=ft, Iv=Iv: sc.activation(out=yt[:, ft, :], in_=Iv[1][:],
                                                                   func=AF.Identity))
                    fill[("yt", ft)] = q
                    drain[("chain", ft % 2)] = [("cAC", q)]

                def w2_sl0_partial(ftw):
                    wait("tensor", "cAC", fill[("yt", ftw)])
                    for tt in range(4):
                        off2 = 512 * (3 + tt)
                        if ftw == 0:
                            wait_drain("tensor", ("o", tt))
                        inc("tensor", "cPE", 1,
                            lambda t, ftw=ftw, tt=tt, off2=off2:
                            t.matmul(pf[:, off2:off2 + 512], yt[:, ftw, 128 * tt:128 * (tt + 1)],
                                     w2b[:, ftw, 0:512],
                                     start=(ftw == 0), stop=False))

                for ft in range(NF + 1):
                    if ft < NF:
                        emit_h(ft)
                    if ft >= 1:
                        emit_gagb(ft - 1)
                    if ft < NF:
                        emit_t0(ft)
                    if ft >= 1:
                        emit_tail(ft - 1)
                    if ft >= 1:
                        emit_act2(ft - 1)
                    if ft < NF:
                        emit_stage1(ft)
                    if ft >= 5:
                        w2_sl0_partial(ft - 5)
                for ftw in range(NF - 4, NF):
                    w2_sl0_partial(ftw)
                v_h_last = hv[NF - 1]
                drain[("z2cm",)] = [("cPE", v_h_last)]
                if ci + 1 < len(chunks):
                    v_z2cm = load_z2cm(*chunks[ci + 1])
                for sl in range(2):
                    if sl == 1:
                        v_by_tt = [0, 0, 0, 0]
                        for ft in range(NF):
                            wait("tensor", "cAC", fill[("yt", ft)])
                            for tt in range(4):
                                off = 512 * (3 + tt)
                                if ft == 0:
                                    wait_drain("tensor", ("o", tt))
                                v = inc("tensor", "cPE", 1,
                                        lambda t, ft=ft, tt=tt, off=off:
                                        t.matmul(pf[:, off:off + 512], yt[:, ft, 128 * tt:128 * (tt + 1)],
                                                 w2b[:, ft, 512:1024],
                                                 start=(ft == 0), stop=False))
                                v_by_tt[tt] = v
                            drain[("ytr", ft)] = [("cPE", v_by_tt[3])]
                    for tt in range(4):
                        off = 512 * (3 + tt)
                        v = inc("tensor", "cPE", 1,
                                lambda t, tt=tt, sl=sl, off=off:
                                t.matmul(pf[:, off:off + 512], ones16r[0:1, :],
                                         b2row16[0:1, 512 * sl:512 * (sl + 1)],
                                         start=False, stop=True))
                        fill[("o", tt)] = v
                    for tt in range(4):
                        g2 = gbase + tt
                        off = 512 * (3 + tt)
                        wait_drain("sync", ("x1b",))
                        wait("sync", "dX1S0", vx1end[b][0])
                        wait("sync", "dX1S1", vx1end[b][1])
                        vx1 = inc("sync", "dX1L", 16,
                                  lambda s, g2=g2: s.dma_start(out=x1b[:],
                                                               in_=x1scr[128 * g2:128 * (g2 + 1), :]))
                        wait("vector", "cPE", fill[("o", tt)])
                        wait_drain("vector", ("fchain",))
                        dvu = inc("vector", "cDV", 1,
                                  lambda v_, off=off: v_.tensor_copy(out=fY1I[:], in_=pf[:, off:off + 512]))
                        add_drain(("o", tt), "cDV", dvu)
                        wait("vector", "cDV", dvu)
                        wait("vector", "dX1L", vx1)
                        dvo = inc("vector", "cDV", 1,
                                  lambda v_, sl=sl: v_.scalar_tensor_tensor(
                                      out=fX2[:], in0=fY1I[:], scalar=float(2.0 ** -8),
                                      in1=x1b[:, 512 * sl:512 * (sl + 1)], op0=ALU.mult, op1=ALU.add))
                        drain[("x1b",)] = [("cDV", dvo)]
                        drain[("fchain",)] = [("cDV", dvo)]
                        wait("scalar", "cDV", dvo)
                        vout = inc("scalar", "dOUT", 16,
                                   lambda sc, g2=g2, sl=sl: sc.dma_start(
                                       out=ys[128 * g2:128 * (g2 + 1), 512 * sl:512 * (sl + 1)],
                                       in_=fX2[:]))
                        app_drain(("fchain",), "dOUT", vout)
                drain[("z2cm",)] = [("cPE", v_h_last)]

        wait("scalar", "dOUT", cnt["dOUT"])
        wait("sync", "dOUT", cnt["dOUT"])

        blk = E(nc.Block())
        blk.sync(lambda eng: [f(eng) for f in ops["sync"]])
        blk.tensor(lambda eng: [f(eng) for f in ops["tensor"]])
        blk.vector(lambda eng: [f(eng) for f in ops["vector"]])
        blk.scalar(lambda eng: [f(eng) for f in ops["scalar"]])
        blk.gpsimd(lambda eng: [f(eng) for f in ops["gpsimd"]])

    return nc


_CACHE = {}


def kernel(L0_attn_x, L0_attn_norm_w, L0_attn_norm_b, L0_attn_qkv_w, L0_attn_qkv_b,
           L0_attn_proj_w, L0_attn_proj_b, L0_ffn_w1, L0_ffn_b1, L0_ffn_w2, L0_ffn_b2):
    x = np.ascontiguousarray(np.asarray(L0_attn_x, dtype=np.float32))
    com = dict(
        qkv_w=np.ascontiguousarray(np.asarray(L0_attn_qkv_w, dtype=np.float32)),
        qkv_b=np.ascontiguousarray(np.asarray(L0_attn_qkv_b, dtype=np.float32)),
        norm_w=np.ascontiguousarray(np.asarray(L0_attn_norm_w, dtype=np.float32)),
        norm_b=np.ascontiguousarray(np.asarray(L0_attn_norm_b, dtype=np.float32)),
        proj_w=np.ascontiguousarray(np.asarray(L0_attn_proj_w, dtype=np.float32)),
        proj_b=np.ascontiguousarray(np.asarray(L0_attn_proj_b, dtype=np.float32)),
        w1=np.ascontiguousarray(np.asarray(L0_ffn_w1, dtype=np.float32)),
        b1=np.ascontiguousarray(np.asarray(L0_ffn_b1, dtype=np.float32)),
        w2=np.ascontiguousarray(np.asarray(L0_ffn_w2, dtype=np.float32)),
        b2=np.ascontiguousarray(np.asarray(L0_ffn_b2, dtype=np.float32)),
        identf=np.eye(128, dtype=np.float32),
        ident16=np.eye(128, dtype=ml_dtypes.bfloat16),
    )
    if "nc" not in _CACHE:
        _CACHE["nc"] = build_program()
    nc = _CACHE["nc"]
    in_maps = []
    for c in range(8):
        m = dict(com)
        m["xs"] = np.ascontiguousarray(x[2 * c:2 * c + 2].reshape(T, D))
        in_maps.append(m)
    res = run_bass_kernel_spmd(nc, in_maps, core_ids=list(range(8)))
    out = np.empty((16, TB, D), dtype=np.float32)
    for c in range(8):
        out[2 * c:2 * c + 2] = np.asarray(res.results[c]["ys"]).reshape(2, TB, D)
    return out


# version 19 (final)
# speedup vs baseline: 1.3810x; 1.0788x over previous
import contextlib

import numpy as np
import ml_dtypes

import concourse.bass as bass
import concourse.mybir as mybir
from concourse.bass_utils import run_bass_kernel_spmd

F32 = mybir.dt.float32
BF16 = mybir.dt.bfloat16
I32 = mybir.dt.int32
AF = mybir.ActivationFunctionType
ALU = mybir.AluOpType
AX = mybir.AxisListType

T = 4096
TB = 2048
NTB = 16
NT = 32
D = 1024
ND = 8
FF = 4096
NF = 32
SCALE = float(np.float32(1.0) / np.float32(np.sqrt(np.float32(2048.0))))
EPS = 1e-5

BASE = 16448
CAP = 229344
OFF_IDENTF = BASE + 0
OFF_IDENT16 = BASE + 512
OFF_PTH0 = BASE + 768
OFF_CONS = BASE + 2816
OFF_STAT = BASE + 3456
OFF_NW8 = BASE + 5248
OFF_NB16 = BASE + 5280
OFF_NW16 = BASE + 5312
OFF_ONES16R = BASE + 5344
OFF_FL16 = BASE + 5632
OFF_FR16 = BASE + 9728
OFF_PROJB16 = BASE + 15872
PWL_SET0 = [BASE + 5632 + 2048 * i for i in range(6)]
OFF_Q = BASE + 17920
OFF_K = BASE + 50688
OFF_W1B = BASE + 17920
OFF_B16STG = BASE + 17920
OFF_ROWA = BASE + 26112
OFF_ROWB = BASE + 42496
OFF_W1Q = BASE + 83456
OFF_XCM = BASE + 116224
OFF_W2B = BASE + 83456
OFF_PWT = BASE + 148992
OFF_VCM = BASE + 165376
OFF_XST0 = OFF_VCM + 0
OFF_XST1 = OFF_VCM + 4096
OFF_X1T0 = OFF_VCM + 8192
OFF_X1T1 = OFF_VCM + 12288
OFF_SQSCR = OFF_VCM + 16384
OFF_Z2T0 = OFF_VCM + 20480
OFF_Z2T1 = OFF_VCM + 22528
OFF_BSCR0 = OFF_VCM + 24576
OFF_BSCR1 = OFF_VCM + 28672
OFF_WT = BASE + 198144
OFF_WN = BASE + 200192
OFF_Z2CM = BASE + 148992
OFF_YT = BASE + 165376
OFF_X1B = BASE + 198144
PWL_SET1 = [BASE + 157184 + 2048 * i for i in range(4)] + [BASE + 202240, BASE + 204288]
OFF_FT4 = BASE + 206336
OFF_FY1I = BASE + 208384
OFF_B2ROW16 = BASE + 210432
assert BASE + 212480 + 256 <= CAP

DTB = {F32: 4, BF16: 2, I32: 4}


def build_program():
    nc = bass.Bass("TRN2", target_bir_lowering=False)

    xs = nc.dram_tensor("xs", [T, D], F32, kind="ExternalInput").ap()
    qkv_w = nc.dram_tensor("qkv_w", [3 * D, D], F32, kind="ExternalInput").ap()
    qkv_b = nc.dram_tensor("qkv_b", [3 * D], F32, kind="ExternalInput").ap()
    norm_w = nc.dram_tensor("norm_w", [D], F32, kind="ExternalInput").ap()
    norm_b = nc.dram_tensor("norm_b", [D], F32, kind="ExternalInput").ap()
    proj_w = nc.dram_tensor("proj_w", [D, D], F32, kind="ExternalInput").ap()
    proj_b = nc.dram_tensor("proj_b", [D], F32, kind="ExternalInput").ap()
    w1 = nc.dram_tensor("w1", [D, FF], F32, kind="ExternalInput").ap()
    b1 = nc.dram_tensor("b1", [FF], F32, kind="ExternalInput").ap()
    w2 = nc.dram_tensor("w2", [FF, D], F32, kind="ExternalInput").ap()
    b2 = nc.dram_tensor("b2", [D], F32, kind="ExternalInput").ap()
    identf_in = nc.dram_tensor("identf", [128, 128], F32, kind="ExternalInput").ap()
    ident16_in = nc.dram_tensor("ident16", [128, 128], BF16, kind="ExternalInput").ap()
    ys = nc.dram_tensor("ys", [T, D], F32, kind="ExternalOutput").ap()

    x16scr = nc.dram_tensor("x16scr", [T, D], BF16, kind="Internal").ap()
    qw16scr = nc.dram_tensor("qw16scr", [3 * D, D], BF16, kind="Internal").ap()
    pw16scr = nc.dram_tensor("pw16scr", [D, D], BF16, kind="Internal").ap()
    x1scr = nc.dram_tensor("x1scr", [T, D], F32, kind="Internal").ap()
    z2scr = nc.dram_tensor("z2scr", [T, D], BF16, kind="Internal").ap()
    rowscr = nc.dram_tensor("rowscr", [2, T], F32, kind="Internal").ap()
    r16scr = nc.dram_tensor("r16scr", [4, T], BF16, kind="Internal").ap()

    with contextlib.ExitStack() as stack:
        E = stack.enter_context

        def sb(name, shape, dtype, off):
            need = 1
            for s_ in shape[1:]:
                need *= s_
            need *= DTB[dtype]
            assert off + need <= CAP, (name, off, need)
            return nc.alloc_sbuf_tensor_at(name, list(shape), dtype, offset=off)

        identf = sb("identf", [128, 128], F32, OFF_IDENTF)
        ident16 = sb("ident16", [128, 128], BF16, OFF_IDENT16)
        pth16 = sb("pth16", [128, 1024], BF16, OFF_PTH0)
        strm = sb("strm", [128, 64], F32, BASE + 212480)
        nwb = sb("nwb", [128, 16], BF16, OFF_PTH0 + 512)
        ST = {}
        for i, n in enumerate(["mu", "mun", "s1", "s", "rs", "a1", "a2", "m2n", "s2",
                               "tmpA", "tmpB", "ssq", "b1adj", "stx"]):
            ST[n] = sb("st_" + n, [128, 32], F32, OFF_STAT + 128 * i)
        b1adj = ST["b1adj"]
        nw8 = sb("nw8", [128, 8], F32, OFF_NW8)
        nb16c = sb("nb16c", [128, 8], BF16, OFF_NB16)
        nw16c = sb("nw16c", [128, 8], BF16, OFF_NW16)
        ones16r = sb("ones16r", [1, 128], BF16, OFF_ONES16R)
        fl16 = sb("fl16", [2, 2048], BF16, OFF_FL16)
        fr16 = sb("fr16", [2, 3072], BF16, OFF_FR16)
        projb16 = sb("projb16", [1, 1024], BF16, OFF_PROJB16)
        b2row16 = sb("b2row16", [1, 1024], BF16, OFF_B2ROW16)
        Q = sb("Q", [128, 16, 1024], BF16, OFF_Q)
        Kt = sb("K", [128, 16, 1024], BF16, OFF_K)
        w1b = sb("w1b", [128, 8, 4096], BF16, OFF_W1B)
        b16stg = sb("b16stg", [1, 4096], BF16, OFF_B16STG)
        rowA = sb("rowA", [1, 4096], F32, OFF_ROWA)
        rowB = sb("rowB", [1, 4096], F32, OFF_ROWB)
        W1q = sb("W1q", [128, 8, 2048], BF16, OFF_W1Q)
        W1v = sb("W1v", [128, 8, 1024], BF16, OFF_W1Q)
        xcm = sb("xcm", [128, 8, 2048], BF16, OFF_XCM)
        acm = sb("acm", [128, 8, 2048], BF16, OFF_XCM)
        w2b = sb("w2b", [128, 32, 1024], BF16, OFF_W2B)
        pwT = sb("pwT", [128, 8, 1024], BF16, OFF_PWT)
        vcm = sb("vcm", [128, 8, 2048], BF16, OFF_VCM)
        xst = [sb("xst0", [128, 1024], F32, OFF_XST0), sb("xst1", [128, 1024], F32, OFF_XST1)]
        x1t = [sb("x1t0", [128, 1024], F32, OFF_X1T0), sb("x1t1", [128, 1024], F32, OFF_X1T1)]
        sqscr = sb("sqscr", [128, 1024], F32, OFF_SQSCR)
        z2t = [sb("z2t0", [128, 1024], BF16, OFF_Z2T0), sb("z2t1", [128, 1024], BF16, OFF_Z2T1)]
        bscr = [sb("bscr0", [128, 2048], BF16, OFF_BSCR0), sb("bscr1", [128, 2048], BF16, OFF_BSCR1)]
        Wt = sb("Wt", [128, 1024], BF16, OFF_WT)
        Wn = sb("Wn", [128, 1024], BF16, OFF_WN)
        vtm = sb("vtm", [128, 1024], BF16, OFF_WN)
        z2cm = sb("z2cm", [128, 8, 512], BF16, OFF_Z2CM)
        yt = sb("yt", [128, 32, 512], BF16, OFF_YT)
        x1b = sb("x1b", [128, 1024], F32, OFF_X1B)

        def con(name, dtype, idx):
            return nc.alloc_sbuf_tensor_at(name, [128, 1], dtype, offset=OFF_CONS + 32 * idx)
        mA = con("mA", I32, 0)
        mR = con("mR", I32, 1)
        c16k = con("c16k", F32, 2)
        ceps = con("ceps", F32, 3)
        c025 = con("c025", F32, 4)
        onesf1 = con("onesf1", F32, 6)
        e0row = nc.alloc_sbuf_tensor_at("e0row", [1, 2], BF16, offset=OFF_CONS + 32 * 7)
        mxc = [con("mxc0", F32, 8), con("mxc1", F32, 9)]
        mbc = [con("mbc0", F32, 10), con("mbc1", F32, 11)]
        su1 = [con("su10", F32, 12), con("su11", F32, 13)]
        su2 = [con("su20", F32, 14), con("su21", F32, 15)]
        suc = [con("suc0", F32, 16), con("suc1", F32, 17)]
        rcc = [con("rcc0", F32, 18), con("rcc1", F32, 19)]

        SETO = [PWL_SET0, PWL_SET1]
        HS = []
        for s_i in range(2):
            fv = [nc.alloc_sbuf_tensor_at(f"hs{s_i}f{k}", [128, 512], F32, offset=SETO[s_i][k]) for k in range(6)]
            iv = {k: nc.alloc_sbuf_tensor_at(f"hs{s_i}i{k}", [128, 512], I32, offset=SETO[s_i][k]) for k in (1, 2, 5)}
            HS.append({"f": fv, "i": iv})
        fT4 = nc.alloc_sbuf_tensor_at("fT4", [128, 512], F32, offset=OFF_FT4)
        fY1I = nc.alloc_sbuf_tensor_at("fY1I", [128, 512], I32, offset=OFF_FY1I)
        fC2 = nc.alloc_sbuf_tensor_at("fC2", [128, 512], F32, offset=OFF_FY1I)
        fYF1 = nc.alloc_sbuf_tensor_at("fYF1", [128, 512], F32, offset=OFF_PTH0)
        fX2 = nc.alloc_sbuf_tensor_at("fX2", [128, 512], F32, offset=OFF_PTH0)

        pf = E(nc.psum_tensor("pf", [128, 3584], F32))
        pb = E(nc.psum_tensor("pb", [128, 1024], BF16))

        sem_names = ["dXF0", "dXF1", "dW", "dWq", "dTRa", "dTRb", "dTRp",
                     "dTB0", "dTB1", "dTZ", "dCX0", "dCX1", "dCQ", "dCP",
                     "dCW", "dCW2", "dZ20", "dZ21", "dX1S0", "dX1S1",
                     "dX1L", "dOUT", "cPE", "cDV", "cAC", "cPL"]
        sems = {n: E(nc.semaphore(n)) for n in sem_names}
        cnt = {n: 0 for n in sem_names}
        ops = {e: [] for e in ("sync", "tensor", "vector", "scalar", "gpsimd")}

        def emit(e, fn):
            ops[e].append(fn)

        def inc(e, sem, n, build):
            cnt[sem] += n
            emit(e, lambda eng, b_=build, s_=sems[sem], k_=n: b_(eng).then_inc(s_, k_))
            return cnt[sem]

        def wait(e, sem, v):
            if v > 0:
                emit(e, lambda eng, s_=sems[sem], v_=v: eng.wait_ge(s_, v_))

        fill = {}
        drain = {}

        def add_drain(key, sem, v):
            drain[key] = [(sem, v)]

        def app_drain(key, sem, v):
            drain.setdefault(key, []).append((sem, v))

        def wait_drain(e, key):
            for s_, v_ in drain.get(key, ()):
                wait(e, s_, v_)

        def barrier(e):
            for s_ in ("cDV", "cAC", "cPL"):
                wait(e, s_, cnt[s_])

        def vsync_op(build):
            v = inc("vector", "cDV", 1, build)
            wait("vector", "cDV", v)
            return v

        for f in (lambda v: v.memset(mA[:], -64),
                  lambda v: v.memset(mR[:], 63),
                  lambda v: v.memset(c16k[:], 16384.0),
                  lambda v: v.memset(ceps[:], float(EPS)),
                  lambda v: v.memset(c025[:], 0.25),
                  lambda v: v.memset(onesf1[:], 1.0)):
            emit("vector", f)
        v_memset = inc("vector", "cDV", 1, lambda v: v.memset(ones16r[:], 1.0))

        vx16 = [0, 0]
        for b in range(2):
            vx16[b] = inc("gpsimd", f"dCX{b}", 16,
                          lambda g, b=b: g.dma_start(out=x16scr[b * TB:(b + 1) * TB, :],
                                                     in_=xs[b * TB:(b + 1) * TB, :]))
        vqw16 = inc("gpsimd", "dCQ", 16, lambda g: g.dma_start(out=qw16scr, in_=qkv_w))
        vpw16 = inc("gpsimd", "dCP", 16, lambda g: g.dma_start(out=pw16scr, in_=proj_w))

        videt = inc("sync", "dW", 16, lambda s: s.dma_start(out=identf[:], in_=identf_in))
        videt16 = inc("sync", "dW", 16, lambda s: s.dma_start(out=ident16[:], in_=ident16_in))
        vnw = inc("sync", "dW", 16, lambda s: s.dma_start(out=rowA[:1, 0:1024], in_=norm_w[None, :]))
        vnb = inc("sync", "dW", 16, lambda s: s.dma_start(out=rowA[:1, 1024:2048], in_=norm_b[None, :]))
        vpbld = inc("sync", "dW", 16, lambda s: s.dma_start(out=rowA[:1, 2048:3072], in_=proj_b[None, :]))
        vb2ld = inc("sync", "dW", 16, lambda s: s.dma_start(out=rowA[:1, 3072:4096], in_=b2[None, :]))
        vb1ld = inc("sync", "dW", 16, lambda s: s.dma_start(out=rowB[:1, :], in_=b1[None, :]))
        v_dW_all = cnt["dW"]

        wait("sync", "dCQ", vqw16)

        def load_W1q():
            vv = 0
            for j in range(ND):
                vv = inc("sync", "dTRa", 16,
                         lambda s, j=j: s.dma_start(out=W1q[:, j, :],
                                                    in_=qw16scr[0:2048, 128 * j:128 * (j + 1)],
                                                    transpose=True))
            return vv

        def load_W1v():
            vv = 0
            for j in range(ND):
                vv = inc("sync", "dTRb", 16,
                         lambda s, j=j: s.dma_start(out=W1v[:, j, :],
                                                    in_=qw16scr[2048:3072, 128 * j:128 * (j + 1)],
                                                    transpose=True))
            return vv

        vW1 = load_W1q()
        wait("sync", "dCP", vpw16)
        for j in range(ND):
            inc("sync", "dTRp", 16,
                lambda s, j=j: s.dma_start(out=pwT[:, j, :], in_=pw16scr[:, 128 * j:128 * (j + 1)],
                                           transpose=True))
        v_pwT = cnt["dTRp"]

        wait("tensor", "dW", v_dW_all)
        wait("tensor", "cDV", v_memset)

        def columnize(src_row_fn, evac_fn, n):
            for c in range(n):
                slot = c % 4
                key = ("bank6",)
                wait_drain("tensor", key)
                v = inc("tensor", "cPE", 1,
                        lambda t, c=c, slot=slot: t.matmul(pf[:, 3072 + 128 * slot:3072 + 128 * slot + 1],
                                                           src_row_fn(c), onesf1[0:1, 0:1],
                                                           start=True, stop=True))
                wait("vector", "cPE", v)
                dv = evac_fn(c, slot)
                add_drain(key, "cDV", dv)

        columnize(lambda c: rowB[0:1, 128 * c:128 * (c + 1)],
                  lambda c, slot: inc("vector", "cDV", 1,
                                      lambda v_, c=c, slot=slot: v_.tensor_scalar(
                                          out=b1adj[:, c:c + 1],
                                          in0=pf[:, 3072 + 128 * slot:3072 + 128 * slot + 1],
                                          scalar1=256.0, scalar2=None, op0=ALU.mult)), 32)
        columnize(lambda c: rowA[0:1, 128 * c:128 * (c + 1)],
                  lambda c, slot: inc("vector", "cDV", 1,
                                      lambda v_, c=c, slot=slot: v_.tensor_copy(
                                          out=nw8[:, c:c + 1],
                                          in_=pf[:, 3072 + 128 * slot:3072 + 128 * slot + 1])), 8)
        columnize(lambda c: rowA[0:1, 1024 + 128 * c:1024 + 128 * (c + 1)],
                  lambda c, slot: inc("vector", "cDV", 1,
                                      lambda v_, c=c, slot=slot: v_.tensor_copy(
                                          out=nb16c[:, c:c + 1],
                                          in_=pf[:, 3072 + 128 * slot:3072 + 128 * slot + 1])), 8)
        vnw16 = inc("vector", "cDV", 1, lambda v_: v_.tensor_copy(out=nw16c[:], in_=nw8[:]))
        wait("vector", "dW", v_dW_all)
        inc("vector", "cDV", 1, lambda v_: v_.tensor_copy(out=projb16[:1, :], in_=rowA[:1, 2048:3072]))
        inc("vector", "cDV", 1, lambda v_: v_.tensor_scalar(out=b2row16[:1, :], in0=rowA[:1, 3072:4096],
                                                            scalar1=256.0, scalar2=None, op0=ALU.mult))

        wait("tensor", "dTRa", vW1)
        wait("tensor", "cDV", vnw16)
        barrier("tensor")
        wait("sync", "cPE", cnt["cPE"])
        vqbld = inc("sync", "dWq", 16, lambda s: s.dma_start(out=rowB[:1, 0:3072], in_=qkv_b[None, :]))
        v_e0 = inc("vector", "cDV", 1, lambda v_: v_.memset(e0row[:], 0.0))
        wait("vector", "cDV", v_e0)
        emit("vector", lambda v_: v_.tensor_scalar(out=e0row[0:1, 0:1], in0=onesf1[0:1, 0:1],
                                                   scalar1=1.0, scalar2=None, op0=ALU.mult))
        emit("vector", lambda v_: v_.tensor_copy(out=nwb[:, 0:16:2], in_=nb16c[:]))
        emit("vector", lambda v_: v_.tensor_copy(out=nwb[:, 1:16:2], in_=nw16c[:]))
        wait("vector", "dWq", vqbld)
        dvqb = inc("vector", "cDV", 1, lambda v_: v_.tensor_copy(out=b16stg[:1, 0:3072], in_=rowB[:1, 0:3072]))
        wait("tensor", "cDV", dvqb)

        def wsum_bias_slices(wt_tensor, col0, sls):
            for sl in sls:
                rel = 512 * sl - col0
                key = ("bank6",)
                wait_drain("tensor", key)
                v = 0
                for j in range(ND):
                    v = inc("tensor", "cPE", 1,
                            lambda t, j=j, rel=rel, wt=wt_tensor: t.matmul(
                                pf[0:2, 3072:3584], nwb[:, 2 * j:2 * j + 2],
                                wt[:, j, rel:rel + 512], start=(j == 0), stop=False))
                v = inc("tensor", "cPE", 1,
                        lambda t, sl=sl: t.matmul(pf[0:2, 3072:3584], e0row[0:1, :],
                                                  b16stg[0:1, 512 * sl:512 * (sl + 1)],
                                                  start=False, stop=True))
                wait("vector", "cPE", v)
                dv = inc("vector", "cDV", 1,
                         lambda v_, sl=sl: v_.tensor_copy(out=fr16[0:2, 512 * sl:512 * (sl + 1)],
                                                          in_=pf[0:2, 3072:3584]))
                add_drain(key, "cDV", dv)

        wsum_bias_slices(W1q, 0, [0, 1, 2, 3])
        v_wb1_pe = cnt["cPE"]
        wait("sync", "cPE", v_wb1_pe)
        vW1v = load_W1v()
        wait("tensor", "dTRb", vW1v)
        wsum_bias_slices(W1v, 2048, [4, 5])
        v_wb2_pe = cnt["cPE"]

        ev = lambda f: emit("vector", f)
        tile_sq = [0] * NT
        tile_rd = [0] * NT
        tile_mu = [0] * NT
        for g in range(NT):
            buf = g % 2
            if g >= 2:
                wait("sync", "cAC", tile_sq[g - 2])
                wait("sync", "cDV", tile_rd[g - 2])
            vld = inc("sync", f"dXF{buf}", 16,
                      lambda s, g=g, buf=buf: s.dma_start(out=xst[buf][:], in_=xs[128 * g:128 * (g + 1), :]))
            wait("vector", f"dXF{buf}", vld)
            tile_mu[g] = inc("vector", "cDV", 1,
                             lambda v_, g=g, buf=buf: v_.reduce_sum(out=ST["mu"][:, g:g + 1], in_=xst[buf][:], axis=AX.X))
            wait("scalar", f"dXF{buf}", vld)
            if g >= 1:
                wait("scalar", "cDV", tile_rd[g - 1])
            tile_sq[g] = inc("scalar", "cAC", 1,
                             lambda sc, buf=buf: sc.activation(out=sqscr[:], in_=xst[buf][:], func=AF.Square))
            wait("vector", "cAC", tile_sq[g])
            tile_rd[g] = inc("vector", "cDV", 1,
                             lambda v_, g=g: v_.reduce_sum(out=ST["ssq"][:, g:g + 1], in_=sqscr[:], axis=AX.X))

        wait("vector", "cDV", tile_mu[NT - 1])
        wait("vector", "cDV", tile_rd[NT - 1])
        vsync_op(lambda v_: v_.tensor_scalar(out=ST["tmpA"][:], in0=ST["mu"][:], scalar1=1.0 / 1024.0, scalar2=None, op0=ALU.mult))
        vsync_op(lambda v_: v_.tensor_scalar(out=ST["tmpB"][:], in0=ST["ssq"][:], scalar1=1.0 / 1024.0, scalar2=None, op0=ALU.mult))
        vsync_op(lambda v_: v_.tensor_tensor(out=ST["stx"][:], in0=ST["tmpA"][:], in1=ST["tmpA"][:], op=ALU.mult))
        vv1 = vsync_op(lambda v_: v_.tensor_tensor(out=ST["ssq"][:], in0=ST["tmpB"][:], in1=ST["stx"][:], op=ALU.subtract))
        wait("scalar", "cDV", vv1)
        va1 = inc("scalar", "cAC", 1,
                  lambda sc: sc.activation(out=ST["tmpB"][:], in_=ST["ssq"][:], func=AF.Sqrt, bias=ceps[:], scale=1.0))
        va1b = inc("scalar", "cAC", 1,
                   lambda sc: sc.activation(out=ST["stx"][:], in_=ST["ssq"][:], func=AF.Relu, bias=ceps[:], scale=1.0))
        wait("vector", "cAC", va1b)
        vsync_op(lambda v_: v_.reciprocal(out=ST["s1"][:], in_=ST["tmpB"][:]))
        vsync_op(lambda v_: v_.tensor_scalar(out=ST["a1"][:], in0=ST["s1"][:], scalar1=1.0, scalar2=None, op0=ALU.add))
        vsync_op(lambda v_: v_.scalar_tensor_tensor(out=ST["a2"][:], in0=ST["tmpA"][:], scalar=-1.0, in1=ST["s1"][:],
                                                    op0=ALU.mult, op1=ALU.mult))
        vsync_op(lambda v_: v_.tensor_scalar(out=ST["mun"][:], in0=ST["tmpA"][:], scalar1=-1.0, scalar2=None, op0=ALU.mult))
        vsync_op(lambda v_: v_.reciprocal(out=ST["tmpA"][:], in_=ST["stx"][:]))
        vv2 = vsync_op(lambda v_: v_.tensor_tensor(out=ST["stx"][:], in0=ST["ssq"][:], in1=ST["tmpA"][:], op=ALU.mult))
        wait("scalar", "cDV", vv2)
        va2 = inc("scalar", "cAC", 1,
                  lambda sc: sc.activation(out=ST["tmpA"][:], in_=ST["stx"][:], func=AF.Sqrt, bias=ceps[:], scale=1.0))
        wait("vector", "cAC", va2)
        vsync_op(lambda v_: v_.tensor_tensor(out=ST["rs"][:], in0=ST["tmpB"][:], in1=ST["tmpA"][:], op=ALU.mult))
        vsync_op(lambda v_: v_.reciprocal(out=ST["stx"][:], in_=ST["tmpA"][:]))
        vsync_op(lambda v_: v_.tensor_tensor(out=ST["s"][:], in0=ST["s1"][:], in1=ST["stx"][:], op=ALU.mult))
        vstat = inc("vector", "cDV", 1, lambda v_: v_.engine_nop())

        wait("tensor", "cDV", vstat)
        wait("tensor", "dW", v_dW_all)
        ev(lambda v_: v_.tensor_copy(out=strm[:, 0:64:2], in_=ST["rs"][:]))
        v_strm = inc("vector", "cDV", 1, lambda v_: v_.tensor_copy(out=strm[:, 1:64:2], in_=ST["mun"][:]))

        v_SlastPE = [0, 0]
        v_projpe = [0, 0]
        v_alast = [0, 0]
        v_z2st = [0, 0]
        v_x1st = [0, 0]
        vz2end = [None, None]
        vx1end = [None, None]
        qk_regions = [0, 1024, 2048]
        v_lastW1pe = v_wb2_pe
        for b in range(2):
            T0 = b * TB
            wait("tensor", "cDV", v_strm)
            if b == 1:
                barrier("tensor")
            for c in range(NTB):
                g = NTB * b + c
                key = ("bank6",)
                wait_drain("tensor", key)
                v = inc("tensor", "cPE", 1,
                        lambda t, g=g: t.matmul(pf[0:2, 3072:3200], strm[:, 2 * g:2 * g + 2],
                                                identf[:], start=True, stop=True))
                wait("vector", "cPE", v)
                if b == 1 and c == 0:
                    wait("vector", "cPE", v_projpe[0])
                dv = inc("vector", "cDV", 1,
                         lambda v_, c=c: v_.tensor_copy(out=fl16[0:2, 128 * c:128 * (c + 1)],
                                                        in_=pf[0:2, 3072:3200]))
                add_drain(key, "cDV", dv)
            v_fl_dv = cnt["cDV"]
            wait("sync", "cPE", v_lastW1pe)
            vW1 = load_W1q()
            wait("sync", f"dCX{b}", vx16[b])
            if b == 1:
                wait("sync", "cPE", v_alast[0])
            v_xcm = 0
            for j in range(ND):
                sbf = j % 2
                wait_drain("sync", ("bscr", sbf))
                vld = inc("sync", f"dTB{sbf}", 16,
                          lambda s, j=j, sbf=sbf, T0=T0: s.dma_start(out=bscr[sbf][:],
                                                                     in_=x16scr[T0:T0 + TB, 128 * j:128 * (j + 1)],
                                                                     transpose=True))
                wait("vector", f"dTB{sbf}", vld)
                v_xcm = inc("vector", "cDV", 1,
                            lambda v_, j=j, sbf=sbf: v_.tensor_scalar(out=xcm[:, j, :], in0=bscr[sbf][:],
                                                                      scalar1=nw8[:, j:j + 1], scalar2=None,
                                                                      op0=ALU.mult))
                add_drain(("bscr", sbf), "cDV", v_xcm)

            wait("tensor", "cDV", v_xcm)
            wait("tensor", "cDV", v_fl_dv)
            wait("tensor", "dTRa", vW1)
            barrier("tensor")
            for m in range(NTB):
                for half in range(2):
                    off = qk_regions[(2 * m + half) % 3]
                    key = ("qk", off)
                    wait_drain("tensor", key)
                    v = 0
                    for j in range(ND):
                        for s2_ in range(2):
                            v = inc("tensor", "cPE", 1,
                                    lambda t, j=j, s2_=s2_, m=m, half=half, off=off:
                                    t.matmul(pf[:, off + 512 * s2_:off + 512 * (s2_ + 1)],
                                             xcm[:, j, 128 * m:128 * (m + 1)],
                                             W1q[:, j, 1024 * half + 512 * s2_:1024 * half + 512 * (s2_ + 1)],
                                             start=(j == 0), stop=False))
                    for s2_ in range(2):
                        v = inc("tensor", "cPE", 1,
                                lambda t, s2_=s2_, half=half, off=off, m=m:
                                t.matmul(pf[:, off + 512 * s2_:off + 512 * (s2_ + 1)],
                                         fl16[0:2, 128 * m:128 * (m + 1)],
                                         fr16[0:2, 1024 * half + 512 * s2_:1024 * half + 512 * (s2_ + 1)],
                                         start=False, stop=True))
                    wait("vector", "cPE", v)
                    dst_t = Q if half == 0 else Kt
                    g = NTB * b + m
                    dv = inc("vector", "cDV", 1,
                             lambda v_, dst_t=dst_t, m=m, off=off, g=g:
                             v_.tensor_scalar(out=dst_t[:, m, :], in0=pf[:, off:off + 1024],
                                              scalar1=ST["s"][:, g:g + 1], scalar2=None, op0=ALU.mult))
                    add_drain(key, "cDV", dv)
            v_qk_pe = cnt["cPE"]

            for _k in (("x1t", 0), ("x1t", 1), ("z2t", 0), ("z2t", 1), ("xst", 0), ("xst", 1)):
                wait_drain("vector", _k)
            wait("sync", "cPE", v_qk_pe)
            vW1v = load_W1v()
            wait("tensor", "dTRb", vW1v)
            for m in range(NTB):
                off = qk_regions[m % 3]
                key = ("qk", off)
                wait_drain("tensor", key)
                v = 0
                for j in range(ND):
                    for s2_ in range(2):
                        v = inc("tensor", "cPE", 1,
                                lambda t, j=j, s2_=s2_, m=m, off=off:
                                t.matmul(pf[:, off + 512 * s2_:off + 512 * (s2_ + 1)],
                                         xcm[:, j, 128 * m:128 * (m + 1)],
                                         W1v[:, j, 512 * s2_:512 * (s2_ + 1)],
                                         start=(j == 0), stop=False))
                for s2_ in range(2):
                    v = inc("tensor", "cPE", 1,
                            lambda t, s2_=s2_, off=off, m=m:
                            t.matmul(pf[:, off + 512 * s2_:off + 512 * (s2_ + 1)],
                                     fl16[0:2, 128 * m:128 * (m + 1)],
                                     fr16[0:2, 2048 + 512 * s2_:2048 + 512 * (s2_ + 1)],
                                     start=False, stop=True))
                wait("vector", "cPE", v)
                wait_drain("vector", ("vtm",))
                dv = inc("vector", "cDV", 1,
                         lambda v_, off=off: v_.tensor_copy(out=vtm[:], in_=pf[:, off:off + 1024]))
                add_drain(key, "cDV", dv)
                wait("tensor", "cDV", dv)
                wait_drain("tensor", ("pbV",))
                vtr = 0
                for kc in range(8):
                    vtr = inc("tensor", "cPE", 1,
                              lambda t, kc=kc: t.transpose(pb[:, 128 * kc:128 * (kc + 1)],
                                                           vtm[:, 128 * kc:128 * (kc + 1)], ident16[:]))
                add_drain(("vtm",), "cPE", vtr)
                wait("vector", "cPE", vtr)
                dvv = 0
                for kc in range(8):
                    dvv = inc("vector", "cDV", 1,
                              lambda v_, kc=kc, m=m: v_.tensor_copy(out=vcm[:, kc, 128 * m:128 * (m + 1)],
                                                                    in_=pb[:, 128 * kc:128 * (kc + 1)]))
                add_drain(("pbV",), "cDV", dvv)
            v_lastW1pe = cnt["cPE"]
            v_vtm_pe = cnt["cPE"]
            v_vcm_dv = cnt["cDV"]

            wait("tensor", "cDV", v_vcm_dv)
            wait("tensor", "dW", v_dW_all)
            barrier("tensor")
            wait("vector", "cPE", v_vtm_pe)
            for qt in range(8):
                p = qt % 2
                s_off = 1024 * p
                key = ("S", p)
                wait_drain("tensor", key)
                v = 0
                for c in range(NTB):
                    for s2_ in range(2):
                        v = inc("tensor", "cPE", 1,
                                lambda t, c=c, s2_=s2_, qt=qt, s_off=s_off:
                                t.matmul(pf[:, s_off + 512 * s2_:s_off + 512 * (s2_ + 1)],
                                         Q[:, c, 128 * qt:128 * (qt + 1)],
                                         Kt[:, c, 512 * s2_:512 * (s2_ + 1)],
                                         start=(c == 0), stop=(c == NTB - 1)))
                if qt == 7:
                    v_SlastPE[b] = v
                wait("vector", "cPE", v)
                vsync_op(lambda v_, p=p, s_off=s_off: v_.reduce_max(out=mxc[p][:], in_=pf[:, s_off:s_off + 1024], axis=AX.X))
                dvm = inc("vector", "cDV", 1,
                          lambda v_, p=p: v_.tensor_scalar(out=mbc[p][:], in0=mxc[p][:], scalar1=-SCALE,
                                                           scalar2=None, op0=ALU.mult))
                wait("scalar", "cDV", dvm)
                wait_drain("scalar", ("pth",))
                va = 0
                for hh in range(2):
                    va = inc("scalar", "cAC", 1,
                             lambda sc, p=p, s_off=s_off, hh=hh:
                             sc.activation(out=pth16[:, 512 * hh:512 * (hh + 1)],
                                           in_=pf[:, s_off + 512 * hh:s_off + 512 * (hh + 1)],
                                           func=AF.Exp, bias=mbc[p][:], scale=SCALE,
                                           accum_out=(su1[p][:] if hh == 0 else su2[p][:])))
                add_drain(key, "cAC", va)
                wait("vector", "cAC", va)
                wait("gpsimd", "cAC", va)
                vsuc = inc("gpsimd", "cPL", 1,
                           lambda g_, p=p: g_.tensor_tensor(out=suc[p][:], in0=su1[p][:], in1=su2[p][:], op=ALU.add))
                wait("vector", "cPL", vsuc)
                v_rcc = vsync_op(lambda v_, p=p: v_.reciprocal(out=rcc[p][:], in_=suc[p][:]))
                wait_drain("vector", ("Wn",))
                dvn = 0
                for hh in range(2):
                    dvn = inc("vector", "cDV", 1,
                              lambda v_, p=p, hh=hh:
                              v_.tensor_scalar(out=Wn[:, 512 * hh:512 * (hh + 1)],
                                               in0=pth16[:, 512 * hh:512 * (hh + 1)],
                                               scalar1=rcc[p][:], scalar2=None, op0=ALU.mult))
                add_drain(("pth",), "cDV", dvn)
                wait("tensor", "cDV", dvn)
                wait_drain("tensor", ("Wt",))
                vtr = 0
                for kc in range(8):
                    vtr = inc("tensor", "cPE", 1,
                              lambda t, kc=kc: t.transpose(pb[:, 128 * kc:128 * (kc + 1)],
                                                           Wn[:, 128 * kc:128 * (kc + 1)], ident16[:]))
                add_drain(("Wn",), "cPE", vtr)
                wait("vector", "cPE", vtr)
                dvt = inc("vector", "cDV", 1, lambda v_: v_.tensor_copy(out=Wt[:], in_=pb[:]))
                wait("tensor", "cDV", dvt)
                va_last = 0
                for w in range(4):
                    slot = 4 + (4 * qt + w) % 3
                    off = 512 * slot
                    key2 = ("a", slot)
                    wait_drain("tensor", key2)
                    v = 0
                    for kc in range(8):
                        v = inc("tensor", "cPE", 1,
                                lambda t, kc=kc, w=w, off=off:
                                t.matmul(pf[:, off:off + 512], Wt[:, 128 * kc:128 * (kc + 1)],
                                         vcm[:, kc, 512 * w:512 * (w + 1)],
                                         start=(kc == 0), stop=(kc == 7)))
                    va_last = v
                    wait("scalar", "cPE", v)
                    vaa = inc("scalar", "cAC", 1,
                              lambda sc, qt=qt, w=w, off=off:
                              sc.activation(out=acm[:, qt, 512 * w:512 * (w + 1)], in_=pf[:, off:off + 512],
                                            func=AF.Copy, scale=1.0))
                    add_drain(key2, "cAC", vaa)
                add_drain(("Wt",), "cPE", va_last)
            v_alast[b] = cnt["cPE"]
            v_acm_done = cnt["cAC"]

            wait("sync", "cPE", v_alast[b])
            wait("vector", "cPE", v_alast[b])
            wait("tensor", "cAC", v_acm_done)
            barrier("tensor")
            barrier("scalar")
            for _sm in ("dX1S0", "dX1S1", "dZ20", "dZ21"):
                wait("scalar", _sm, cnt[_sm])
                wait("vector", _sm, cnt[_sm])
            wait("tensor", "dTRp", v_pwT)
            for m in range(NTB):
                g = NTB * b + m
                xbuf = m % 2
                wait_drain("sync", ("xst", xbuf))
                vxl = inc("sync", f"dXF{xbuf}", 16,
                          lambda s, g=g, xbuf=xbuf: s.dma_start(out=xst[xbuf][:], in_=xs[128 * g:128 * (g + 1), :]))
                pe_fill = [0, 0]
                for s2_ in range(2):
                    slot = (2 * m + s2_) % 4
                    off = 512 * slot
                    key = ("pj", slot)
                    wait_drain("tensor", key)
                    v = 0
                    for qc in range(8):
                        v = inc("tensor", "cPE", 1,
                                lambda t, qc=qc, m=m, s2_=s2_, off=off:
                                t.matmul(pf[:, off:off + 512], acm[:, qc, 128 * m:128 * (m + 1)],
                                         pwT[:, qc, 512 * s2_:512 * (s2_ + 1)],
                                         start=(qc == 0), stop=False))
                    v = inc("tensor", "cPE", 1,
                            lambda t, s2_=s2_, off=off, m=m:
                            t.matmul(pf[:, off:off + 512], fl16[0:1, 128 * m:128 * (m + 1)],
                                     projb16[0:1, 512 * s2_:512 * (s2_ + 1)],
                                     start=False, stop=True))
                    pe_fill[s2_] = v
                v_projpe[b] = pe_fill[1]
                wait("vector", f"dXF{xbuf}", vxl)
                x1buf = m % 2
                wait_drain("vector", ("x1t", x1buf))
                dvxl = inc("vector", "cDV", 1,
                           lambda v_, xbuf=xbuf, g=g: v_.tensor_scalar(out=sqscr[:], in0=xst[xbuf][:],
                                                                      scalar1=ST["a1"][:, g:g + 1],
                                                                      scalar2=ST["a2"][:, g:g + 1],
                                                                      op0=ALU.mult, op1=ALU.add))
                add_drain(("xst", xbuf), "cDV", dvxl)
                wait("vector", "cDV", dvxl)
                dvx = 0
                for s2_ in range(2):
                    slot = (2 * m + s2_) % 4
                    wait("vector", "cPE", pe_fill[s2_])
                    dvx = inc("vector", "cDV", 1,
                              lambda v_, s2_=s2_, slot=slot, g=g, x1buf=x1buf:
                              v_.scalar_tensor_tensor(out=x1t[x1buf][:, 512 * s2_:512 * (s2_ + 1)],
                                                      in0=pf[:, 512 * slot:512 * slot + 512],
                                                      scalar=ST["s"][:, g:g + 1],
                                                      in1=sqscr[:, 512 * s2_:512 * (s2_ + 1)],
                                                      op0=ALU.mult, op1=ALU.add))
                    add_drain(("pj", slot), "cDV", dvx)
                wait("scalar", "cDV", dvx)
                v_x1st[b] = inc("scalar", f"dX1S{x1buf}", 16,
                                lambda sc, g=g, x1buf=x1buf: sc.dma_start(out=x1scr[128 * g:128 * (g + 1), :],
                                                                          in_=x1t[x1buf][:]))
                vid = inc("scalar", "cAC", 1,
                          lambda sc, g=g, x1buf=x1buf: sc.activation(out=sqscr[:], in_=x1t[x1buf][:],
                                                                     func=AF.Identity,
                                                                     accum_out=ST["tmpA"][:, g:g + 1]))
                wait("scalar", "cAC", vid)
                vsq = inc("scalar", "cAC", 1,
                          lambda sc, g=g, x1buf=x1buf: sc.activation(out=sqscr[:], in_=x1t[x1buf][:],
                                                                     func=AF.Square,
                                                                     accum_out=ST["tmpB"][:, g:g + 1]))
                wait("gpsimd", "cAC", vsq)
                inc("gpsimd", "cPL", 1,
                    lambda g_, g=g: g_.tensor_scalar(out=ST["m2n"][:, g:g + 1], in0=ST["tmpA"][:, g:g + 1],
                                                     scalar1=-1.0 / 1024.0, scalar2=None, op0=ALU.mult))
                vpl2 = inc("gpsimd", "cPL", 1,
                           lambda g_, g=g: g_.tensor_scalar(out=ST["tmpB"][:, g:g + 1], in0=ST["tmpB"][:, g:g + 1],
                                                            scalar1=1.0 / 1024.0, scalar2=None, op0=ALU.mult))
                wait("vector", "cPL", vpl2)
                dvv = inc("vector", "cDV", 1,
                          lambda v_, g=g: v_.scalar_tensor_tensor(out=ST["tmpA"][:, g:g + 1],
                                                                  in0=ST["m2n"][:, g:g + 1],
                                                                  scalar=ST["m2n"][:, g:g + 1],
                                                                  in1=ST["tmpB"][:, g:g + 1],
                                                                  op0=ALU.mult, op1=ALU.subtract))
                wait("scalar", "cDV", dvv)
                vsq2 = inc("scalar", "cAC", 1,
                           lambda sc, g=g: sc.activation(out=ST["stx"][:, g:g + 1], in_=ST["tmpA"][:, g:g + 1],
                                                         func=AF.Sqrt, bias=ceps[:], scale=-1.0))
                wait("vector", "cAC", vsq2)
                v_s2 = vsync_op(lambda v_, g=g: v_.reciprocal(out=ST["s2"][:, g:g + 1], in_=ST["stx"][:, g:g + 1]))
                zbuf = m % 2
                wait_drain("vector", ("z2t", zbuf))
                dvz = inc("vector", "cDV", 1,
                          lambda v_, g=g, x1buf=x1buf, zbuf=zbuf:
                          v_.tensor_scalar(out=z2t[zbuf][:], in0=x1t[x1buf][:],
                                           scalar1=ST["m2n"][:, g:g + 1], scalar2=ST["s2"][:, g:g + 1],
                                           op0=ALU.add, op1=ALU.mult))
                drain[("x1t", x1buf)] = [("cDV", dvz), (f"dX1S{x1buf}", v_x1st[b]), ("cAC", vsq)]
                wait("scalar", "cDV", dvz)
                v_z2st[b] = inc("scalar", f"dZ2{zbuf}", 16,
                                lambda sc, g=g, zbuf=zbuf: sc.dma_start(out=z2scr[128 * g:128 * (g + 1), :],
                                                                        in_=z2t[zbuf][:]))
                add_drain(("z2t", zbuf), f"dZ2{zbuf}", v_z2st[b])
            vz2end[b] = (cnt["dZ20"], cnt["dZ21"])
            vx1end[b] = (cnt["dX1S0"], cnt["dX1S1"])

        wait("gpsimd", "cPE", v_SlastPE[1])
        v_w1b = 0
        for j in range(ND):
            v_w1b = inc("gpsimd", "dCW", 16,
                        lambda g, j=j: g.dma_start(out=w1b[:, j, :], in_=w1[128 * j:128 * (j + 1), :]))
        wait("gpsimd", "cPE", v_projpe[1])
        v_w2b = 0
        for jj in range(8):
            v_w2b = inc("gpsimd", "dCW2", 16,
                        lambda g, jj=jj: g.dma_start(
                            out=w2b[:, 4 * jj:4 * (jj + 1), :],
                            in_=w2[512 * jj:512 * (jj + 1), :].rearrange("(c p) n -> p c n", p=128)))

        wait("sync", "cPE", v_projpe[1])
        wait("tensor", "dCW", v_w1b)
        barrier("tensor")
        for _sm in ("dZ20", "dZ21", "dX1S0", "dX1S1", "dXF0", "dXF1"):
            wait("vector", _sm, cnt[_sm])
        first_w2_wait = True

        def load_z2cm(b_, tc4_):
            base_tok = b_ * TB + 512 * tc4_
            wait("sync", "dZ20", vz2end[b_][0])
            wait("sync", "dZ21", vz2end[b_][1])
            wait_drain("sync", ("z2cm",))
            vz = 0
            for j in range(ND):
                vz = inc("sync", "dTZ", 16,
                         lambda s, j=j, base_tok=base_tok:
                         s.dma_start(out=z2cm[:, j, :],
                                     in_=z2scr[base_tok:base_tok + 512, 128 * j:128 * (j + 1)],
                                     transpose=True))
            return vz

        chunks = [(b_, t_) for b_ in range(2) for t_ in range(4)]
        v_z2cm = load_z2cm(*chunks[0])
        for ci, (b, tc4) in enumerate(chunks):
            if True:
                base_tok = b * TB + 512 * tc4
                gbase = base_tok // 128
                wait("tensor", "dTZ", v_z2cm)
                if first_w2_wait:
                    wait("tensor", "dCW2", v_w2b)
                    first_w2_wait = False
                hv, t0v, rfv, gbv, t3v = {}, {}, {}, {}, {}

                def emit_h(ft):
                    slot = ft % 3
                    off = 512 * slot
                    wait_drain("tensor", ("h", slot))
                    v = 0
                    for j in range(ND):
                        v = inc("tensor", "cPE", 1,
                                lambda t, j=j, ft=ft, off=off:
                                t.matmul(pf[:, off:off + 512], w1b[:, j, 128 * ft:128 * (ft + 1)],
                                         z2cm[:, j, :],
                                         start=(j == 0), stop=(j == ND - 1)))
                    hv[ft] = v

                def emit_t0(ft):
                    slot = ft % 3
                    off = 512 * slot
                    S_ = HS[ft % 2]
                    Fv = S_["f"]
                    wait("scalar", "cPE", hv[ft])
                    wait_drain("scalar", ("chain", ft % 2))
                    vt0 = inc("scalar", "cAC", 1,
                              lambda sc, off=off, ft=ft, Fv=Fv: sc.activation(
                                  out=Fv[0][:], in_=pf[:, off:off + 512], func=AF.Identity,
                                  bias=b1adj[:, ft:ft + 1], scale=256.0))
                    add_drain(("h", slot), "cAC", vt0)
                    wait("scalar", "cAC", vt0)
                    t0v[ft] = inc("scalar", "cAC", 1,
                                  lambda sc, S_=S_: sc.activation(out=S_["i"][1][:], in_=S_["f"][0][:],
                                                                  func=AF.Identity))

                def emit_stage1(ft):
                    S_ = HS[ft % 2]
                    Fv, Iv = S_["f"], S_["i"]
                    wait("vector", "cAC", t0v[ft])
                    wait_drain("vector", ("chain", ft % 2))
                    inc("vector", "cDV", 1,
                        lambda v_, Iv=Iv: v_.tensor_scalar(out=Iv[2][:], in0=Iv[1][:], scalar1=mA[:],
                                                           scalar2=None, op0=ALU.bitwise_and))
                    dvr = inc("vector", "cDV", 1,
                              lambda v_, Iv=Iv: v_.tensor_scalar(out=Iv[5][:], in0=Iv[1][:], scalar1=mR[:],
                                                                 scalar2=None, op0=ALU.bitwise_and))
                    wait("vector", "cDV", dvr)
                    rfv[ft] = inc("vector", "cDV", 1,
                                  lambda v_, Iv=Iv, Fv=Fv: v_.tensor_scalar(out=Fv[3][:], in0=Iv[5][:],
                                                                            scalar1=4.0, scalar2=None,
                                                                            op0=ALU.mult))

                def emit_gagb(ft):
                    S_ = HS[ft % 2]
                    Fv, Iv = S_["f"], S_["i"]
                    wait("scalar", "cDV", rfv[ft])
                    inc("scalar", "cAC", 1,
                        lambda sc, Iv=Iv, Fv=Fv: sc.activation(out=Fv[4][:], in_=Iv[2][:], func=AF.Gelu,
                                                               scale=float(2.0 ** -8)))
                    gbv[ft] = inc("scalar", "cAC", 1,
                                  lambda sc, Iv=Iv, Fv=Fv: sc.activation(out=Fv[5][:], in_=Iv[2][:], func=AF.Gelu,
                                                                         bias=c025[:], scale=float(2.0 ** -8)))

                def emit_tail(ft):
                    S_ = HS[ft % 2]
                    Fv, Iv = S_["f"], S_["i"]
                    wait("vector", "cAC", gbv[ft])
                    dve1 = inc("vector", "cDV", 1,
                               lambda v_, Fv=Fv: v_.tensor_tensor(out=Fv[1][:], in0=Fv[5][:], in1=Fv[4][:], op=ALU.subtract))
                    wait("vector", "cDV", dve1)
                    dve2 = inc("vector", "cDV", 1,
                               lambda v_, Fv=Fv: v_.tensor_tensor(out=Fv[2][:], in0=Fv[3][:], in1=Fv[1][:], op=ALU.mult))
                    wait("vector", "cDV", dve2)
                    dve3 = inc("vector", "cDV", 1,
                               lambda v_, Fv=Fv: v_.scalar_tensor_tensor(out=Fv[0][:], in0=Fv[4][:], scalar=256.0,
                                                                         in1=Fv[2][:], op0=ALU.mult, op1=ALU.add))
                    t3v[ft] = dve3

                def emit_act2(ft):
                    S_ = HS[ft % 2]
                    Fv, Iv = S_["f"], S_["i"]
                    wait("scalar", "cDV", t3v[ft])
                    vy1 = inc("scalar", "cAC", 1,
                              lambda sc, Iv=Iv, Fv=Fv: sc.activation(out=Iv[1][:], in_=Fv[0][:],
                                                                     func=AF.Identity))
                    wait("scalar", "cAC", vy1)
                    wait_drain("scalar", ("ytr", ft))
                    q = inc("scalar", "cAC", 1,
                            lambda sc, ft=ft, Iv=Iv: sc.activation(out=yt[:, ft, :], in_=Iv[1][:],
                                                                   func=AF.Identity))
                    fill[("yt", ft)] = q
                    drain[("chain", ft % 2)] = [("cAC", q)]

                def w2_sl0_partial(ftw):
                    wait("tensor", "cAC", fill[("yt", ftw)])
                    for tt in range(4):
                        off2 = 512 * (3 + tt)
                        if ftw == 0:
                            wait_drain("tensor", ("o", tt))
                        inc("tensor", "cPE", 1,
                            lambda t, ftw=ftw, tt=tt, off2=off2:
                            t.matmul(pf[:, off2:off2 + 512], yt[:, ftw, 128 * tt:128 * (tt + 1)],
                                     w2b[:, ftw, 0:512],
                                     start=(ftw == 0), stop=False))

                for ft in range(NF + 1):
                    if ft < NF:
                        emit_h(ft)
                    if ft >= 1:
                        emit_gagb(ft - 1)
                    if ft < NF:
                        emit_t0(ft)
                    if ft >= 1:
                        emit_tail(ft - 1)
                    if ft >= 1:
                        emit_act2(ft - 1)
                    if ft < NF:
                        emit_stage1(ft)
                    if ft >= 5:
                        w2_sl0_partial(ft - 5)
                for ftw in range(NF - 4, NF):
                    w2_sl0_partial(ftw)
                v_h_last = hv[NF - 1]
                drain[("z2cm",)] = [("cPE", v_h_last)]
                if ci + 1 < len(chunks):
                    v_z2cm = load_z2cm(*chunks[ci + 1])
                for sl in range(2):
                    if sl == 1:
                        v_by_tt = [0, 0, 0, 0]
                        for ft in range(NF):
                            wait("tensor", "cAC", fill[("yt", ft)])
                            for tt in range(4):
                                off = 512 * (3 + tt)
                                if ft == 0:
                                    wait_drain("tensor", ("o", tt))
                                v = inc("tensor", "cPE", 1,
                                        lambda t, ft=ft, tt=tt, off=off:
                                        t.matmul(pf[:, off:off + 512], yt[:, ft, 128 * tt:128 * (tt + 1)],
                                                 w2b[:, ft, 512:1024],
                                                 start=(ft == 0), stop=False))
                                v_by_tt[tt] = v
                            drain[("ytr", ft)] = [("cPE", v_by_tt[3])]
                    for tt in range(4):
                        off = 512 * (3 + tt)
                        v = inc("tensor", "cPE", 1,
                                lambda t, tt=tt, sl=sl, off=off:
                                t.matmul(pf[:, off:off + 512], ones16r[0:1, :],
                                         b2row16[0:1, 512 * sl:512 * (sl + 1)],
                                         start=False, stop=True))
                        fill[("o", tt)] = v
                    for tt in range(4):
                        g2 = gbase + tt
                        off = 512 * (3 + tt)
                        wait_drain("sync", ("x1b",))
                        wait("sync", "dX1S0", vx1end[b][0])
                        wait("sync", "dX1S1", vx1end[b][1])
                        vx1 = inc("sync", "dX1L", 16,
                                  lambda s, g2=g2: s.dma_start(out=x1b[:],
                                                               in_=x1scr[128 * g2:128 * (g2 + 1), :]))
                        wait("vector", "cPE", fill[("o", tt)])
                        wait_drain("vector", ("fchain",))
                        dvu = inc("vector", "cDV", 1,
                                  lambda v_, off=off: v_.tensor_copy(out=fY1I[:], in_=pf[:, off:off + 512]))
                        add_drain(("o", tt), "cDV", dvu)
                        wait("vector", "cDV", dvu)
                        wait("vector", "dX1L", vx1)
                        dvo = inc("vector", "cDV", 1,
                                  lambda v_, sl=sl: v_.scalar_tensor_tensor(
                                      out=fX2[:], in0=fY1I[:], scalar=float(2.0 ** -8),
                                      in1=x1b[:, 512 * sl:512 * (sl + 1)], op0=ALU.mult, op1=ALU.add))
                        drain[("x1b",)] = [("cDV", dvo)]
                        drain[("fchain",)] = [("cDV", dvo)]
                        wait("scalar", "cDV", dvo)
                        vout = inc("scalar", "dOUT", 16,
                                   lambda sc, g2=g2, sl=sl: sc.dma_start(
                                       out=ys[128 * g2:128 * (g2 + 1), 512 * sl:512 * (sl + 1)],
                                       in_=fX2[:]))
                        app_drain(("fchain",), "dOUT", vout)
                drain[("z2cm",)] = [("cPE", v_h_last)]

        wait("scalar", "dOUT", cnt["dOUT"])
        wait("sync", "dOUT", cnt["dOUT"])

        blk = E(nc.Block())
        blk.sync(lambda eng: [f(eng) for f in ops["sync"]])
        blk.tensor(lambda eng: [f(eng) for f in ops["tensor"]])
        blk.vector(lambda eng: [f(eng) for f in ops["vector"]])
        blk.scalar(lambda eng: [f(eng) for f in ops["scalar"]])
        blk.gpsimd(lambda eng: [f(eng) for f in ops["gpsimd"]])

    return nc


_CACHE = {}


def kernel(L0_attn_x, L0_attn_norm_w, L0_attn_norm_b, L0_attn_qkv_w, L0_attn_qkv_b,
           L0_attn_proj_w, L0_attn_proj_b, L0_ffn_w1, L0_ffn_b1, L0_ffn_w2, L0_ffn_b2):
    x = np.ascontiguousarray(np.asarray(L0_attn_x, dtype=np.float32))
    com = dict(
        qkv_w=np.ascontiguousarray(np.asarray(L0_attn_qkv_w, dtype=np.float32)),
        qkv_b=np.ascontiguousarray(np.asarray(L0_attn_qkv_b, dtype=np.float32)),
        norm_w=np.ascontiguousarray(np.asarray(L0_attn_norm_w, dtype=np.float32)),
        norm_b=np.ascontiguousarray(np.asarray(L0_attn_norm_b, dtype=np.float32)),
        proj_w=np.ascontiguousarray(np.asarray(L0_attn_proj_w, dtype=np.float32)),
        proj_b=np.ascontiguousarray(np.asarray(L0_attn_proj_b, dtype=np.float32)),
        w1=np.ascontiguousarray(np.asarray(L0_ffn_w1, dtype=np.float32)),
        b1=np.ascontiguousarray(np.asarray(L0_ffn_b1, dtype=np.float32)),
        w2=np.ascontiguousarray(np.asarray(L0_ffn_w2, dtype=np.float32)),
        b2=np.ascontiguousarray(np.asarray(L0_ffn_b2, dtype=np.float32)),
        identf=np.eye(128, dtype=np.float32),
        ident16=np.eye(128, dtype=ml_dtypes.bfloat16),
    )
    if "nc" not in _CACHE:
        _CACHE["nc"] = build_program()
    nc = _CACHE["nc"]
    in_maps = []
    for c in range(8):
        m = dict(com)
        m["xs"] = np.ascontiguousarray(x[2 * c:2 * c + 2].reshape(T, D))
        in_maps.append(m)
    res = run_bass_kernel_spmd(nc, in_maps, core_ids=list(range(8)))
    out = np.empty((16, TB, D), dtype=np.float32)
    for c in range(8):
        out[2 * c:2 * c + 2] = np.asarray(res.results[c]["ys"]).reshape(2, TB, D)
    return out
